# version 03
# baseline (speedup 1.0000x reference)
import numpy as np
import ml_dtypes
import concourse.bass as bass
import concourse.mybir as mybir
from concourse.bass_utils import run_bass_kernel_spmd

F32 = mybir.dt.float32
BF16 = mybir.dt.bfloat16
I32 = mybir.dt.int32
AF = mybir.ActivationFunctionType
ALU = mybir.AluOpType
AX = mybir.AxisListType.X

B, L, E, H, D = 4, 8192, 1024, 16, 64
T = 4096
NS = 8
NG = T // 512
LN_EPS = 1e-5


def build_nc(dbg="full", prefix=True):
    SKIP = set()
    CHUNK_LVL = 9
    NG_OVR = 0
    nc = bass.Bass("TRN2", target_bir_lowering=False, detect_race_conditions=False)

    x_own = nc.dram_tensor("x_own", [T, E], F32, kind="ExternalInput").ap()
    x_pre = nc.dram_tensor("x_pre", [T, E], F32, kind="ExternalInput").ap()
    w_qkv = nc.dram_tensor("w_qkv", [3 * E, E], F32, kind="ExternalInput").ap()
    w_out = nc.dram_tensor("w_out", [E, E], F32, kind="ExternalInput").ap()
    w_ffn = nc.dram_tensor("w_ffn", [E, E], F32, kind="ExternalInput").ap()
    b_ffn = nc.dram_tensor("b_ffn", [1, E], F32, kind="ExternalInput").ap()
    a_lora = nc.dram_tensor("a_lora", [E, 16], F32, kind="ExternalInput").ap()
    b_lora = nc.dram_tensor("b_lora", [16, E], F32, kind="ExternalInput").ap()
    mask_in = nc.dram_tensor("mask_in", [128, 512], F32, kind="ExternalInput").ap()
    ident_in = nc.dram_tensor("ident_in", [128, 128], BF16, kind="ExternalInput").ap()
    augc_in = nc.dram_tensor("augc_in", [128, 1], F32, kind="ExternalInput").ap()
    out = nc.dram_tensor("out", [T, E], F32, kind="ExternalOutput").ap()

    dbg_t = {}
    if dbg == "xn":
        dbg_t["xn_dump"] = nc.dram_tensor("xn_dump", [T, E], F32, kind="ExternalOutput").ap()
    if dbg == "qkv":
        dbg_t["q_dump"] = nc.dram_tensor("q_dump", [E, 512], F32, kind="ExternalOutput").ap()
        dbg_t["k_dump"] = nc.dram_tensor("k_dump", [E, 512], F32, kind="ExternalOutput").ap()
        dbg_t["v_dump"] = nc.dram_tensor("v_dump", [512, 65 * H], F32, kind="ExternalOutput").ap()
        dbg_t["kt_dump"] = nc.dram_tensor("kt_dump", [512, E], F32, kind="ExternalOutput").ap()
        dbg_t["g_dump"] = nc.dram_tensor("g_dump", [512, E], F32, kind="ExternalOutput").ap()
    if dbg == "attn":
        dbg_t["praw_dump"] = nc.dram_tensor("praw_dump", [T, E], F32, kind="ExternalOutput").ap()
        dbg_t["den_dump"] = nc.dram_tensor("den_dump", [T, 32], F32, kind="ExternalOutput").ap()
        dbg_t["S_dump"] = nc.dram_tensor("S_dump", [T, 520], F32, kind="ExternalOutput").ap()

    from contextlib import ExitStack
    ctx = ExitStack()
    sbt = lambda n, s, d: ctx.enter_context(nc.sbuf_tensor(n, s, d))

    wqkvT_all = sbt("wqkvT_all", [128, 3 * E * NS], BF16)
    wqkvT = [wqkvT_all[:, 3 * E * s:3 * E * (s + 1)] for s in range(NS)]
    woutT_all = sbt("woutT_all", [128, E * NS], BF16)
    woutT = [woutT_all[:, E * s:E * (s + 1)] for s in range(NS)]
    wq_ffn = [sbt(f"wqffn{s}", [128, E], BF16) for s in range(NS)]
    a_sb = sbt("a_sb", [128, 16 * NS], BF16)
    b_sb = sbt("b_sb", [16, E], BF16)
    bias_sb = sbt("bias_sb", [1, E], BF16)
    mask_sb = sbt("mask_sb", [128, 512], F32)
    ident_sb = sbt("ident_sb", [128, 128], BF16)
    augc_sb = sbt("augc_sb", [128, 1], F32)
    ones_col = sbt("ones_col", [128, 1], BF16)
    ones_row = sbt("ones_row", [1, 128], BF16)

    wstage = [sbt(f"wstage{i}", [128, E], BF16) for i in range(2)]
    qscr = sbt("qscr", [128, E], F32)
    qscr_i = sbt("qscr_i", [128, E], I32)
    amax_c = sbt("amax_c", [128, NS + 2], F32)
    amax_r = sbt("amax_r", [1, 128], F32)
    scale_c = sbt("scale_c", [128, 2], F32)

    x_sb = [sbt(f"x_sb{i}", [128, E], F32) for i in range(4)]
    xn_sb = [sbt(f"xn_sb{i}", [128, E], BF16) for i in range(4)]
    xnT_all = sbt("xnT_all", [128, 512 * NS], BF16)
    xnT_sb = [xnT_all[:, 512 * s:512 * (s + 1)] for s in range(NS)]
    qT_sb = [sbt(f"qT{s}", [128, 512], BF16) for s in range(NS)]
    kT_sb = [sbt(f"kT{s}", [128, 512], BF16) for s in range(NS)]
    ktok_sb = [sbt(f"ktok{i}", [128, E], BF16) for i in range(4)]
    vaug_sb = [sbt(f"vaug{i}", [128, 65 * H], BF16) for i in range(4)]
    e8_sb = sbt("e8_sb", [128, 1024], F32)
    em_sb = sbt("em_sb", [128, 512], F32)
    stat_sum = sbt("stat_sum", [128, 4], F32)
    stat_sq = sbt("stat_sq", [128, 4], F32)
    stat_m = sbt("stat_m", [128, 4], F32)
    stat_v = sbt("stat_v", [128, 4], F32)
    stat_y = sbt("stat_y", [128, 4], F32)
    stat_t = sbt("stat_t", [128, 4], F32)
    ln_bias = sbt("ln_bias", [128, 4], F32)
    ln_scale = sbt("ln_scale", [128, 4], F32)
    lnscr = qscr

    attnT_sb = [sbt(f"attnT{w}", [128, 512], BF16) for w in range(4)]
    S_sb = sbt("S_sb", [128, 66 * 8], BF16)
    den_scr = sbt("den_scr", [128, 16], F32)
    rden_sb = sbt("rden_sb", [128, 16], F32)
    praw_tok = sbt("praw_tok", [128, E], BF16)
    praw_fm_all = sbt("praw_fm_all", [128, 1024], BF16)
    u_sb = sbt("u_sb", [16, 512], BF16)
    g_sb = [sbt(f"g_sb{i}", [128, E], BF16) for i in range(4)]
    sg_sb = [sbt(f"sg_sb{i}", [128, E], F32) for i in range(2)]
    out_sb = [sbt(f"out_sb{i}", [128, E], F32) for i in range(2)]
    kf32 = [x_sb[0], x_sb[1], x_sb[2], x_sb[3], sg_sb[0], sg_sb[1], out_sb[0], out_sb[1]]

    pb = [ctx.enter_context(nc.psum_tensor(f"pb{i}", [128, 512], F32)) for i in range(8)]
    pbv = [p.ap().bitcast(BF16) for p in pb]

    sems = {}
    cnt = {}
    for name in ["ldc", "ldk", "lab", "ldw0", "ldw1",
                 "ldx0", "ldx1", "ldx2", "ldx3",
                 "pe", "dve", "act", "gp", "sto", "sto0", "sto1"]:
        sems[name] = ctx.enter_context(nc.semaphore(name))
        cnt[name] = 0
    S = sems

    prog = {k: [] for k in ["sync", "gpsimd", "vector", "scalar", "tensor"]}
    SERIAL = ""

    _eng_sem = {"vector": "dve", "scalar": "act", "tensor": "pe"}

    def emit(eng, fn):
        if False and SERIAL and eng in ("vector", "scalar", "tensor"):
            waits = [(nm, cnt[nm]) for nm in ("pe", "dve", "act")
                     if nm != _eng_sem[eng] and cnt[nm] > 0]
            def wrapped(e, fn=fn, waits=waits):
                for nm, vv in waits:
                    e.wait_ge(S[nm], vv)
                fn(e)
            prog[eng].append(wrapped)
        else:
            prog[eng].append(fn)

    def bump(sem, n=1):
        cnt[sem] += n
        return cnt[sem]

    def ap_of(t):
        return t[:]

    def man_ap(tile, off, dims):
        return bass.AP(tile[:].tensor, off, dims)

    def f(sync):
        sync.dma_start(out=mask_sb[:], in_=mask_in[:]).then_inc(S["ldc"], 16)
        sync.dma_start(out=ident_sb[:], in_=ident_in[:]).then_inc(S["ldc"], 16)
        sync.dma_start(out=augc_sb[:], in_=augc_in[:]).then_inc(S["ldc"], 16)
        for s in range(NS):
            sync.dma_start(out=kf32[s][:], in_=w_ffn[128 * s:128 * (s + 1), :]).then_inc(S["ldk"], 16)
    emit("sync", f)
    const_done = bump("ldc", 48)
    kf32_done = bump("ldk", 16 * NS)

    def f(g):
        for s in range(NS):
            g.dma_start(out=a_sb[:, 16 * s:16 * (s + 1)],
                        in_=a_lora[128 * s:128 * (s + 1), :]).then_inc(S["lab"], 16)
        g.dma_start(out=b_sb[:], in_=b_lora[:]).then_inc(S["lab"], 16)
        g.dma_start(out=bias_sb[:], in_=b_ffn[:]).then_inc(S["lab"], 16)
    emit("gpsimd", f)
    ab_done = bump("lab", 16 * (NS + 2))

    raw_tiles = [("qkv", i) for i in range(24)] + [("out", i) for i in range(8)]
    wT_pe_base = cnt["pe"]
    wT_act_base = cnt["act"]

    if "wt" not in SKIP:
        def f(g, wT_pe_base=wT_pe_base):
            for idx, (kind, i) in enumerate(raw_tiles):
                stg = wstage[idx % 2]
                src = w_qkv[128 * i:128 * (i + 1), :] if kind == "qkv" else w_out[128 * i:128 * (i + 1), :]
                if idx >= 2:
                    g.wait_ge(S["pe"], wT_pe_base + (idx - 1) * 8)
                g.dma_start(out=stg[:], in_=src).then_inc(S["ldw" + str(idx % 2)], 16)
        emit("gpsimd", f)
        bump("ldw0", 16 * ((len(raw_tiles) + 1) // 2))
        bump("ldw1", 16 * (len(raw_tiles) // 2))

    if "wt" not in SKIP:
        def f(t, wT_act_base=wT_act_base, const_done=const_done):
            t.wait_ge(S["ldc"], const_done)
            for idx, (kind, i) in enumerate(raw_tiles):
                stg = wstage[idx % 2]
                t.wait_ge(S["ldw" + str(idx % 2)], 16 * (idx // 2 + 1))
                bv = pbv[idx % 2]
                if idx >= 2:
                    t.wait_ge(S["act"], wT_act_base + (idx - 1) * 8)
                for s in range(NS):
                    t.transpose(bv[:, 128 * s:128 * (s + 1)], stg[:, 128 * s:128 * (s + 1)],
                                ident_sb[:]).then_inc(S["pe"], 1)
        emit("tensor", f)
        bump("pe", 8 * len(raw_tiles))

        def f(a, wT_pe_base=wT_pe_base):
            for idx, (kind, i) in enumerate(raw_tiles):
                bv = pbv[idx % 2]
                for s in range(NS):
                    a.wait_ge(S["pe"], wT_pe_base + (idx + 1) * 8)
                    dst = (wqkvT[s] if kind == "qkv" else woutT[s])[:, 128 * i:128 * (i + 1)]
                    a.copy(dst, bv[:, 128 * s:128 * (s + 1)]).then_inc(S["act"], 1)
        emit("scalar", f)
    wT_act_done = bump("act", 8 * len(raw_tiles)) if "wt" not in SKIP else cnt["act"]

    if "quant" in SKIP:
        def f(v):
            v.memset(wq_ffn[0][:], 0.0)
            v.memset(ones_col[:], 1.0)
            v.memset(ones_row[:], 1.0).then_inc(S["dve"], 1)
        emit("vector", f)
        quant_done = bump("dve")
    if "quant" not in SKIP:
      def f(v, kf32_done=kf32_done):
        v.wait_ge(S["ldk"], kf32_done)
        for s in range(NS):
            v.reduce_max(amax_c[:, s:s + 1], kf32[s][:], axis=AX, apply_absolute_value=True)
        v.drain()
        v.reduce_max(amax_c[:, NS:NS + 1], amax_c[:, 0:NS], axis=AX).then_inc(S["dve"], 1)
      emit("vector", f)
      q1 = bump("dve")

    if "quant" not in SKIP:
      def f(g, q1=q1):
        g.wait_ge(S["dve"], q1)
        g.dma_start(out=amax_r[0:1, :], in_=amax_c[:, NS:NS + 1]).then_inc(S["gp"], 16)
      emit("gpsimd", f)
      q2 = bump("gp", 16)

      def f(v, q2=q2):
        v.wait_ge(S["gp"], q2)
        v.reduce_max(amax_r[0:1, 0:1], amax_r[0:1, :], axis=AX).then_inc(S["dve"], 1)
      emit("vector", f)
      q3 = bump("dve")

      def f(g, q3=q3):
        g.wait_ge(S["dve"], q3)
        g.dma_start(out=amax_c[:, NS + 1:NS + 2],
                    in_=man_ap(amax_r, 0, [[128, 1], [0, 128]])).then_inc(S["gp"], 16)
      emit("gpsimd", f)
      q4 = bump("gp", 16)

      def f(v, q4=q4):
        v.wait_ge(S["gp"], q4)
        am = amax_c[:, NS + 1:NS + 2]
        v.tensor_scalar(scale_c[:, 1:2], am, 1.0 / 7.0, None, ALU.mult)
        v.drain()
        v.reciprocal(scale_c[:, 0:1], scale_c[:, 1:2])
        v.drain()
        for s in range(NS):
            v.tensor_scalar(qscr[:], kf32[s][:], scale_c[:, 0:1], None, ALU.mult)
            v.tensor_copy(qscr_i[:], qscr[:])
            v.tensor_copy(qscr[:], qscr_i[:])
            v.tensor_scalar(wq_ffn[s][:], qscr[:], scale_c[:, 1:2], None, ALU.mult)
        v.memset(ones_col[:], 1.0)
        v.memset(ones_row[:], 1.0).then_inc(S["dve"], 1)
      emit("vector", f)
      quant_done = bump("dve")

    last = dict(
        x_free=[quant_done] * 4,
        gate_dve=[0, 0],
        out_dma=[0, 0],
        Ssb_dve=0,
        first_state=True,
        recip_dve=0,
        praw_dve=0,
        prawT_act=0,
        pb23_free_dve=0,
        pb01_k_dve=0,
        silu_act_last=0,
    )

    def set_aug(seg):
        if seg == "own":
            pe_now = cnt["pe"]
            def f(v, pe_now=pe_now):
                v.wait_ge(S["pe"], pe_now)
                for i in range(4):
                    v.memset(man_ap(vaug_sb[i], 64, [[65 * H, 128], [65, H]]), 1.0)
                v.engine_nop().then_inc(S["dve"], 1)
            emit("vector", f)
        else:
            def f(v, const_done=const_done):
                v.wait_ge(S["ldc"], const_done)
                for i in range(4):
                    v.tensor_copy(man_ap(vaug_sb[i], 64, [[65 * H, 128], [65, H]]),
                                  augc_sb[:, 0:1].broadcast_to([128, H]))
                v.engine_nop().then_inc(S["dve"], 1)
            emit("vector", f)
        return bump("dve")

    def f(v):
        v.memset(S_sb[:], 0.0).then_inc(S["dve"], 1)
    emit("vector", f)
    last["Ssb_dve"] = bump("dve")

    def emit_group(seg, g_i, x_src, aug_done):
        own = seg == "own"
        t0g = 512 * g_i

        xf = list(last["x_free"])
        xfg = list(last.get("x_free_gp", [0, 0, 0, 0]))
        def f(sync, xf=xf, xfg=xfg, x_src=x_src, t0g=t0g):
            for i in range(4):
                sync.wait_ge(S["dve"], xf[i])
                sync.wait_ge(S["gp"], xfg[i])
                sync.dma_start(out=x_sb[i][:], in_=x_src[t0g + 128 * i:t0g + 128 * (i + 1), :]
                               ).then_inc(S["ldx" + str(i)], 16)
        emit("sync", f)
        xv = [bump("ldx" + str(i), 16) for i in range(4)]

        sq_base = cnt["act"]
        def f(a, xv=xv):
            for i in range(4):
                a.wait_ge(S["ldx" + str(i)], xv[i])
                a.activation(lnscr[:], x_sb[i][:], AF.Square,
                             accum_out=stat_sq[:, i:i + 1]).then_inc(S["act"], 1)
        emit("scalar", f)
        bump("act", 4)

        def f(v, xv=xv, sq_base=sq_base):
            for i in range(4):
                v.wait_ge(S["ldx" + str(i)], xv[i])
                v.reduce_sum(stat_sum[:, i:i + 1], x_sb[i][:], axis=AX)
            v.wait_ge(S["act"], sq_base + 4)
            v.drain()
            v.tensor_scalar(stat_m[:], stat_sum[:], 1.0 / E, None, ALU.mult)
            v.tensor_scalar(stat_v[:], stat_sq[:], 1.0 / E, None, ALU.mult)
            v.drain()
            v.tensor_tensor(stat_t[:], stat_m[:], stat_m[:], ALU.mult)
            v.drain()
            v.tensor_tensor(stat_v[:], stat_v[:], stat_t[:], ALU.subtract)
            v.drain()
            v.tensor_scalar(stat_v[:], stat_v[:], LN_EPS, None, ALU.add)
            v.drain()
            v.engine_nop().then_inc(S["dve"], 1)
        emit("vector", f)
        var_done = bump("dve")

        def f(a, var_done=var_done):
            a.wait_ge(S["dve"], var_done)
            a.activation(stat_t[:], stat_v[:], AF.Sqrt).then_inc(S["act"], 1)
        emit("scalar", f)
        sqrt_done = bump("act")

        def f(v, sqrt_done=sqrt_done):
            v.wait_ge(S["act"], sqrt_done)
            v.reciprocal(stat_y[:], stat_t[:])
            v.drain()
            v.tensor_tensor(stat_t[:], stat_y[:], stat_y[:], ALU.mult)
            v.drain()
            v.tensor_tensor(stat_t[:], stat_t[:], stat_v[:], ALU.mult)
            v.drain()
            v.tensor_scalar(stat_t[:], stat_t[:], -0.5, 1.5, ALU.mult, ALU.add)
            v.drain()
            v.tensor_tensor(ln_scale[:], stat_y[:], stat_t[:], ALU.mult)
            v.drain()
            v.tensor_tensor(ln_bias[:], stat_m[:], ln_scale[:], ALU.mult)
            v.drain()
            v.tensor_scalar(ln_bias[:], ln_bias[:], -1.0, None, ALU.mult)
            v.drain()
            v.engine_nop().then_inc(S["dve"], 1)
        emit("vector", f)
        inv_done = bump("dve")

        xn_base = cnt["act"]
        def f(a, inv_done=inv_done):
            a.wait_ge(S["dve"], inv_done)
            for i in range(4):
                a.activation(xn_sb[i][:], x_sb[i][:], AF.Identity,
                             bias=ln_bias[:, i:i + 1], scale=ln_scale[:, i:i + 1]).then_inc(S["act"], 1)
        emit("scalar", f)
        bump("act", 4)

        if dbg == "xn":
            def f(g, xn_base=xn_base, t0g=t0g):
                for i in range(4):
                    g.wait_ge(S["act"], xn_base + i + 1)
                    g.dma_start(out=dbg_t["xn_dump"][t0g + 128 * i:t0g + 128 * (i + 1), :],
                                in_=xn_sb[i][:]).then_inc(S["sto"], 16)
            emit("gpsimd", f)
            dump_done = bump("sto", 64)
            def f(v, xn_base=xn_base, dump_done=dump_done):
                v.wait_ge(S["act"], xn_base + 4)
                v.wait_ge(S["sto"], dump_done)
                v.engine_nop().then_inc(S["dve"], 1)
            emit("vector", f)
            mk = bump("dve")
            for i in range(4):
                last["x_free"][i] = mk
            return

        xnT_pe_base = cnt["pe"]
        gmax = max(last["gate_dve"])
        k01 = last["pb01_k_dve"]
        def f(t, xn_base=xn_base, xnT_pe_base=xnT_pe_base, gmax=gmax, k01=k01,
              wT_act_done=wT_act_done):
            t.wait_ge(S["act"], wT_act_done)
            t.wait_ge(S["dve"], max(gmax, k01))
            for i in range(4):
                t.wait_ge(S["act"], xn_base + i + 1)
                bv = pbv[i % 2]
                if i >= 2:
                    t.wait_ge(S["act"], xnT_pe_base_act[0] + 8 * (i - 1))
                for s in range(NS):
                    t.transpose(bv[:, 128 * s:128 * (s + 1)], xn_sb[i][:, 128 * s:128 * (s + 1)],
                                ident_sb[:]).then_inc(S["pe"], 1)
        emit("tensor", f)
        bump("pe", 32)

        xnT_pe_base_act = [cnt["act"]]
        def f(a, xnT_pe_base=xnT_pe_base):
            for i in range(4):
                bv = pbv[i % 2]
                a.wait_ge(S["pe"], xnT_pe_base + 8 * (i + 1))
                dst = man_ap(xnT_all, 128 * i, [[512 * NS, 128], [512, 8], [1, 128]])
                a.copy(dst, bv[:, 0:1024]).then_inc(S["act"], 8)
        emit("scalar", f)
        xnT_done = bump("act", 32)

        qkT_done = 0
        u_done = 0
        if own:
            p1_pe_base = cnt["pe"]
            p1_stt_base_box = [0]
            def f(t, xnT_done=xnT_done, p1_stt_base_box=p1_stt_base_box):
                t.wait_ge(S["act"], xnT_done)
                for ft in range(16):
                    bank = pb[ft % 2]
                    if ft >= 2:
                        t.wait_ge(S["dve"], p1_stt_base_box[0] + (ft - 1))
                    for s in range(NS):
                        t.matmul(bank.ap(), wqkvT[s][:, 128 * ft:128 * (ft + 1)],
                                 xnT_sb[s][:], start=(s == 0), stop=(s == NS - 1)).then_inc(S["pe"], 1)
            emit("tensor", f)
            bump("pe", 16 * NS)

            p1_exp_base = cnt["act"]
            def f(a, p1_pe_base=p1_pe_base, p1_stt_base_box=p1_stt_base_box):
                for ft in range(16):
                    bank = pb[ft % 2]
                    a.wait_ge(S["pe"], p1_pe_base + NS * (ft + 1))
                    if ft >= 2:
                        a.wait_ge(S["dve"], p1_stt_base_box[0] + (ft - 1))
                    a.activation(e8_sb[:, 512 * (ft % 2):512 * (ft % 2 + 1)], bank.ap(),
                                 AF.Exp).then_inc(S["act"], 1)
            emit("scalar", f)
            bump("act", 16)

            p1_stt_base = cnt["dve"]
            p1_stt_base_box[0] = p1_stt_base
            def f(v, p1_exp_base=p1_exp_base):
                for ft in range(16):
                    bank = pb[ft % 2]
                    dstT = (qT_sb if ft < 8 else kT_sb)[ft % 8]
                    v.wait_ge(S["act"], p1_exp_base + ft + 1)
                    eh = e8_sb[:, 512 * (ft % 2):512 * (ft % 2 + 1)]
                    v.tensor_scalar_min(em_sb[:], eh, 1.0)
                    v.scalar_tensor_tensor(dstT[:], bank.ap(), 0.0, em_sb[:],
                                           ALU.max, ALU.add).then_inc(S["dve"], 1)
            emit("vector", f)
            qkT_done = bump("dve", 16)

            u_pe_base = cnt["pe"]
            pf23 = max(last["pb23_free_dve"], last.get("mask_dve_last", 0))
            pf23a = last["prawT_act"]
            def f(t, ab_done=ab_done, pf23=pf23, pf23a=pf23a, qkT_done=qkT_done):
                t.wait_ge(S["lab"], ab_done)
                t.wait_ge(S["dve"], pf23)
                t.wait_ge(S["act"], pf23a)
                for s in range(NS):
                    t.matmul(pb[2].ap()[0:16, :], a_sb[:, 16 * s:16 * (s + 1)], xnT_sb[s][:],
                             start=(s == 0), stop=(s == NS - 1)).then_inc(S["pe"], 1)
            emit("tensor", f)
            bump("pe", NS)

            def f(v, u_pe_base=u_pe_base):
                v.wait_ge(S["pe"], u_pe_base + NS)
                v.tensor_copy(u_sb[:], pb[2].ap()[0:16, :]).then_inc(S["dve"], 1)
            emit("vector", f)
            u_done = bump("dve")

        p2 = []
        for i in range(4):
            prev_kv = p2[i - 1]["kv_dve"] if i else 0
            prev_silu = p2[i - 1]["silu_act"] if (i and own) else 0
            gmax2 = max(last["gate_dve"])
            k01b = last["pb01_k_dve"]
            pf23 = last["pb23_free_dve"]
            pf23a = last["prawT_act"]
            md_last = last.get("mask_dve_last", 0)
            sil_last = last["silu_act_last"]
            prw_last = last["praw_dve"]
            def f(t, i=i, own=own, qkT_done=qkT_done, u_done=u_done, prev_kv=prev_kv,
                  prev_silu=prev_silu, gmax2=gmax2, k01b=k01b, pf23=pf23, pf23a=pf23a,
                  xnT_done=xnT_done, quant_done=quant_done, sil_last=sil_last,
                  prw_last=prw_last, md_last=md_last):
                t.wait_ge(S["act"], xnT_done)
                t.wait_ge(S["dve"], quant_done)
                if own:
                    t.wait_ge(S["dve"], qkT_done)
                    t.wait_ge(S["dve"], u_done)
                else:
                    t.wait_ge(S["dve"], max(gmax2, k01b))
                t.wait_ge(S["dve"], pf23)
                t.wait_ge(S["act"], pf23a)
                t.wait_ge(S["dve"], md_last)
                if i > 0:
                    t.wait_ge(S["dve"], prev_kv)
                    if own:
                        t.wait_ge(S["act"], prev_silu)
                if own and i == 0:
                    t.wait_ge(S["act"], sil_last)
                    t.wait_ge(S["dve"], prw_last)
                for s in range(NS):
                    st = (s == 0)
                    sp = (s == NS - 1)
                    xt = xnT_sb[s][:, 128 * i:128 * (i + 1)]
                    t.matmul(pb[0].ap(), xt, wqkvT[s][:, E:E + 512], start=st, stop=sp)
                    t.matmul(pb[1].ap(), xt, wqkvT[s][:, E + 512:2 * E], start=st, stop=sp)
                    t.matmul(pb[2].ap(), xt, wqkvT[s][:, 2 * E:2 * E + 512], start=st, stop=sp)
                    mmi = t.matmul(pb[3].ap(), xt, wqkvT[s][:, 2 * E + 512:3 * E], start=st, stop=sp)
                    if own:
                        t.matmul(pb[4].ap(), xt, wq_ffn[s][:, 0:512], start=st, stop=False)
                        t.matmul(pb[5].ap(), xt, wq_ffn[s][:, 512:1024], start=st, stop=False)
                if own:
                    t.matmul(pb[4].ap(), u_sb[:, 128 * i:128 * (i + 1)], b_sb[:, 0:512],
                             start=False, stop=False)
                    t.matmul(pb[5].ap(), u_sb[:, 128 * i:128 * (i + 1)], b_sb[:, 512:1024],
                             start=False, stop=False)
                    t.matmul(pb[4].ap(), ones_row[:], bias_sb[:, 0:512], start=False, stop=True)
                    mmi = t.matmul(pb[5].ap(), ones_row[:], bias_sb[:, 512:1024],
                                   start=False, stop=True)
                mmi.then_inc(S["pe"], 1)
            emit("tensor", f)
            p2_pe_done = bump("pe")

            kexp_base = cnt["act"]
            def f(a, p2_pe_done=p2_pe_done, qkT_done=qkT_done):
                a.wait_ge(S["pe"], p2_pe_done)
                a.wait_ge(S["dve"], qkT_done)
                a.activation(e8_sb[:, 0:512], pb[0].ap(), AF.Exp).then_inc(S["act"], 1)
                a.activation(e8_sb[:, 512:1024], pb[1].ap(), AF.Exp).then_inc(S["act"], 1)
            emit("scalar", f)
            bump("act", 2)

            def f(v, i=i, kexp_base=kexp_base, p2_pe_done=p2_pe_done, aug_done=aug_done):
                v.wait_ge(S["act"], kexp_base + 1)
                v.tensor_scalar_min(em_sb[:], e8_sb[:, 0:512], 1.0)
                v.scalar_tensor_tensor(ktok_sb[i][:, 0:512], pb[0].ap(), 0.0, em_sb[:],
                                       ALU.max, ALU.add)
                v.wait_ge(S["act"], kexp_base + 2)
                v.tensor_scalar_min(em_sb[:], e8_sb[:, 512:1024], 1.0)
                v.scalar_tensor_tensor(ktok_sb[i][:, 512:1024], pb[1].ap(), 0.0, em_sb[:],
                                       ALU.max, ALU.add)
                v.wait_ge(S["dve"], aug_done)
                v.tensor_copy(man_ap(vaug_sb[i], 0, [[65 * H, 128], [65, 8], [1, 64]]),
                              pb[2].ap().rearrange("p (h e) -> p h e", h=8))
                v.tensor_copy(man_ap(vaug_sb[i], 65 * 8, [[65 * H, 128], [65, 8], [1, 64]]),
                              pb[3].ap().rearrange("p (h e) -> p h e", h=8)).then_inc(S["dve"], 1)
            emit("vector", f)
            kv_dve = bump("dve")

            silu_act = 0
            if own:
                def f(a, i=i, p2_pe_done=p2_pe_done):
                    a.wait_ge(S["pe"], p2_pe_done)
                    a.activation(g_sb[i][:, 0:512], pb[4].ap(), AF.Silu).then_inc(S["act"], 1)
                    a.activation(g_sb[i][:, 512:1024], pb[5].ap(), AF.Silu).then_inc(S["act"], 1)
                emit("scalar", f)
                silu_act = bump("act", 2)
            p2.append(dict(kv_dve=kv_dve, silu_act=silu_act))

        last["pb01_k_dve"] = p2[3]["kv_dve"]
        last["pb23_free_dve"] = p2[3]["kv_dve"]
        if own:
            last["silu_act_last"] = p2[3]["silu_act"]

        if dbg == "qkv":
            def f(g, p2=p2, qkT_done=qkT_done):
                for i in range(4):
                    g.wait_ge(S["dve"], p2[i]["kv_dve"])
                    g.dma_start(out=dbg_t["kt_dump"][128 * i:128 * (i + 1), :],
                                in_=ktok_sb[i][:]).then_inc(S["sto"], 16)
                    g.dma_start(out=dbg_t["v_dump"][128 * i:128 * (i + 1), :],
                                in_=vaug_sb[i][:]).then_inc(S["sto"], 16)
                    g.wait_ge(S["act"], p2[i]["silu_act"])
                    g.dma_start(out=dbg_t["g_dump"][128 * i:128 * (i + 1), :],
                                in_=g_sb[i][:]).then_inc(S["sto"], 16)
                g.wait_ge(S["dve"], qkT_done)
                for s in range(NS):
                    g.dma_start(out=dbg_t["q_dump"][128 * s:128 * (s + 1), :],
                                in_=qT_sb[s][:]).then_inc(S["sto"], 16)
                    g.dma_start(out=dbg_t["k_dump"][128 * s:128 * (s + 1), :],
                                in_=kT_sb[s][:]).then_inc(S["sto"], 16)
            emit("gpsimd", f)
            bump("sto", 16 * 28)
            for i in range(4):
                last["x_free"][i] = cnt["dve"]
            return

        if own:
            ROT = True
            pend = None
            for i in range(4):
                if ROT:
                    pend = emit_chunk_own(g_i, i, t0g, p2, qkT_done, pend)
                else:
                    p_ = emit_chunk_own(g_i, i, t0g, p2, qkT_done, None)
                    emit_pend_state(p_)
                    emit_pend_prawT(p_)
                    emit_pend_op(p_)
            if ROT:
                emit_pend_state(pend)
                emit_pend_prawT(pend)
                emit_pend_op(pend)
        else:
            for i in range(4):
                emit_chunk_pre(i, p2)

        if not own:
            for i in range(4):
                last["x_free"][i] = cnt["dve"]
            if g_i == NG - 1:
                sp = cnt["pe"]
                def f(v, sp=sp):
                    v.wait_ge(S["pe"], sp)
                    v.tensor_scalar(S_sb[:, 0:264], pb[6].ap()[:, 0:264], 0.125, None, ALU.mult)
                    v.tensor_scalar(S_sb[:, 264:528], pb[7].ap()[:, 0:264], 0.125, None,
                                    ALU.mult).then_inc(S["dve"], 1)
                emit("vector", f)
                last["Ssb_dve"] = bump("dve")

    def emit_pend_state(pend):
        pi = pend["i"]
        first = last["first_state"]
        kva = pend["kv"]
        ssb_prev = last["Ssb_dve"]
        def f(t, pi=pi, first=first, kva=kva, ssb_prev=ssb_prev):
            t.wait_ge(S["dve"], kva)
            t.wait_ge(S["dve"], ssb_prev)
            for p in range(8):
                for par in range(2):
                    h = 2 * p + par
                    bank = pb[6] if p < 4 else pb[7]
                    r0 = 64 * par
                    c0 = 66 * (p % 4)
                    stf = first and (p in (0, 4))
                    t.matmul(bank.ap()[r0:r0 + 64, c0:c0 + 65],
                             ktok_sb[pi][:, 64 * h:64 * h + 64],
                             vaug_sb[pi][:, 65 * h:65 * h + 65],
                             start=stf, stop=False).then_inc(S["pe"], 1)
        emit("tensor", f)
        state_done = bump("pe", 16)
        last["first_state"] = False

        dmp = last.get("dump_sto", 0) if "praw_dump" in dbg_t else 0
        def f(v, state_done=state_done, dmp=dmp):
            v.wait_ge(S["pe"], state_done)
            v.wait_ge(S["sto"], dmp)
            v.tensor_scalar(S_sb[:, 0:264], pb[6].ap()[:, 0:264], 0.125, None, ALU.mult)
            v.tensor_scalar(S_sb[:, 264:528], pb[7].ap()[:, 0:264], 0.125, None,
                            ALU.mult).then_inc(S["dve"], 1)
        emit("vector", f)
        last["Ssb_dve"] = bump("dve")

    def emit_pend_prawT(pend):
        prawT_base = cnt["pe"]
        pd = pend["praw_dve"]
        gm = max(last["gate_dve"])
        def f(t, prawT_base=prawT_base, pd=pd, gm=gm):
            t.wait_ge(S["dve"], pd)
            t.wait_ge(S["dve"], gm)
            for sl in range(NS):
                bv = pbv[sl % 2]
                t.transpose(bv[:, 512 + 128 * (sl // 2):512 + 128 * (sl // 2) + 128],
                            praw_tok[:, 128 * sl:128 * (sl + 1)],
                            ident_sb[:]).then_inc(S["pe"], 1)
        emit("tensor", f)
        prawT_pe = bump("pe", 8)
        last["prawT_pe"] = prawT_pe

        def f(a, prawT_base=prawT_base):
            a.wait_ge(S["pe"], prawT_base + 8)
            a.copy(man_ap(praw_fm_all, 0, [[1024, 128], [256, 4], [1, 128]]),
                   pbv[0][:, 512:1024]).then_inc(S["act"], 4)
            a.copy(man_ap(praw_fm_all, 128, [[1024, 128], [256, 4], [1, 128]]),
                   pbv[1][:, 512:1024]).then_inc(S["act"], 4)
        emit("scalar", f)
        last["prawT_act"] = bump("act", 8)

    def emit_pend_op(pend):
        pi = pend["i"]
        t0g = pend["t0g"]
        prawT_act = last["prawT_act"]
        rec = last["recip_dve"]
        gm = max(last["gate_dve"])
        def f(t, prawT_act=prawT_act, rec=rec, gm=gm):
            t.wait_ge(S["act"], prawT_act)
            t.wait_ge(S["dve"], rec)
            t.wait_ge(S["dve"], gm)
            for sl in range(NS):
                pf = praw_fm_all[:, 128 * sl:128 * (sl + 1)]
                t.matmul(pb[0].ap(), pf, woutT[sl][:, 0:512], start=(sl == 0), stop=(sl == NS - 1))
                mm2 = t.matmul(pb[1].ap(), pf, woutT[sl][:, 512:1024], start=(sl == 0),
                               stop=(sl == NS - 1))
            mm2.then_inc(S["pe"], 1)
        emit("tensor", f)
        op_done = bump("pe")

        ob = pi % 2
        ga = pend["silu_act"]
        od = last["out_dma"][ob]
        pta3 = last["prawT_act"]
        def f(v, pi=pi, ob=ob, op_done=op_done, ga=ga, od=od, pta3=pta3):
            v.wait_ge(S["pe"], op_done)
            v.wait_ge(S["act"], ga)
            v.wait_ge(S["sto" + str(ob)], od)
            v.wait_ge(S["act"], pta3)
            v.scalar_tensor_tensor(sg_sb[ob][:, 0:512], g_sb[pi][:, 0:512], 1.0,
                                   pb[0].ap(), ALU.mult, ALU.mult)
            v.scalar_tensor_tensor(sg_sb[ob][:, 512:1024], g_sb[pi][:, 512:1024], 1.0,
                                   pb[1].ap(), ALU.mult, ALU.mult).then_inc(S["dve"], 1)
        emit("vector", f)
        sg_done = bump("dve")
        last["gate_dve"][ob] = sg_done
        last["x_free"][pi] = sg_done

        if False:
            def f(g, pi=pi, ob=ob, sg_done=sg_done):
                g.wait_ge(S["dve"], sg_done)
                g.tensor_tensor(out_sb[ob][:], sg_sb[ob][:], x_sb[pi][:], ALU.add).then_inc(S["gp"], 1)
            emit("gpsimd", f)
            gate_done = bump("gp")
            last["x_free_gp"] = last.get("x_free_gp", [0, 0, 0, 0])
            last["x_free_gp"][pi] = gate_done
            def f(sync, pi=pi, ob=ob, t0g=t0g, gate_done=gate_done):
                sync.wait_ge(S["gp"], gate_done)
                sync.dma_start(out=out[t0g + 128 * pi:t0g + 128 * (pi + 1), :],
                               in_=out_sb[ob][:]).then_inc(S["sto" + str(ob)], 16)
            emit("sync", f)
            last["out_dma"][ob] = bump("sto" + str(ob), 16)
        else:
            def f(v2, pi=pi, ob=ob, sg_done=sg_done):
                v2.tensor_tensor(out_sb[ob][:], sg_sb[ob][:], x_sb[pi][:], ALU.add).then_inc(S["dve"], 1)
            emit("vector", f)
            gate_done = bump("dve")
            last["x_free"][pi] = gate_done
            def f(sync, pi=pi, ob=ob, t0g=t0g, gate_done=gate_done):
                sync.wait_ge(S["dve"], gate_done)
                sync.dma_start(out=out[t0g + 128 * pi:t0g + 128 * (pi + 1), :],
                               in_=out_sb[ob][:]).then_inc(S["sto" + str(ob)], 16)
            emit("sync", f)
            last["out_dma"][ob] = bump("sto" + str(ob), 16)

    CSER = False

    def emit_chunk_own(g_i, i, t0g, p2, qkT_done, pend):
        mask_dve = [0, 0]
        if CSER:
            dv_now, ac_now, pe_now2 = cnt["dve"], cnt["act"], cnt["pe"]
            def fser(t, a=dv_now, b=ac_now):
                t.wait_ge(S["dve"], a); t.wait_ge(S["act"], b)
            emit("tensor", fser)
            def fser2(v, a=pe_now2):
                v.wait_ge(S["pe"], a)
            emit("vector", fser2)
            def fser3(a_, a=pe_now2):
                a_.wait_ge(S["pe"], a)
            emit("scalar", fser3)
        sc_pe = [0, 0]
        nd_pe = [0, 0]

        pf23 = last["pb23_free_dve"]
        md_prev = last.get("mask_dve_last", 0)
        base = cnt["pe"]
        def f(t, i=i, qkT_done=qkT_done, pf23=pf23, md_prev=md_prev):
            t.wait_ge(S["dve"], qkT_done)
            t.wait_ge(S["dve"], pf23)
            t.wait_ge(S["dve"], md_prev)
            for p in range(4):
                kTe = kT_sb[p][0:64, 128 * i:128 * (i + 1)]
                kTo = kT_sb[p][64:128, 128 * i:128 * (i + 1)]
                qTe = qT_sb[p][0:64, 128 * i:128 * (i + 1)]
                qTo = qT_sb[p][64:128, 128 * i:128 * (i + 1)]
                c0 = 128 * p
                t.matmul(pb[2].ap()[:, c0:c0 + 128], kTe, qTe, start=True, stop=True)
                t.matmul(pb[3].ap()[:, c0:c0 + 128], kTo, qTo, start=True,
                         stop=True).then_inc(S["pe"], 1)
        emit("tensor", f)
        sc_pe[0] = bump("pe", 4)

        prev_nd = last.get("nd_pe_prev", 0)
        def f(v, sc=sc_pe[0], prev_nd=prev_nd):
            v.wait_ge(S["pe"], sc)
            v.wait_ge(S["pe"], prev_nd)
            v.tensor_tensor(attnT_sb[0][:], pb[2].ap(), mask_sb[:], ALU.mult)
            v.tensor_tensor(attnT_sb[1][:], pb[3].ap(), mask_sb[:], ALU.mult).then_inc(S["dve"], 1)
        emit("vector", f)
        mask_dve[0] = bump("dve")

        if pend is not None:
            emit_pend_state(pend)
            emit_pend_prawT(pend)

        def f(t, i=i, md0=mask_dve[0]):
            t.wait_ge(S["dve"], md0)
            for p in range(4, 8):
                kTe = kT_sb[p][0:64, 128 * i:128 * (i + 1)]
                kTo = kT_sb[p][64:128, 128 * i:128 * (i + 1)]
                qTe = qT_sb[p][0:64, 128 * i:128 * (i + 1)]
                qTo = qT_sb[p][64:128, 128 * i:128 * (i + 1)]
                c0 = 128 * (p % 4)
                t.matmul(pb[2].ap()[:, c0:c0 + 128], kTe, qTe, start=True, stop=True)
                t.matmul(pb[3].ap()[:, c0:c0 + 128], kTo, qTo, start=True,
                         stop=True).then_inc(S["pe"], 1)
        emit("tensor", f)
        sc_pe[1] = bump("pe", 4)

        def f(v, sc=sc_pe[1], prev_nd=prev_nd):
            v.wait_ge(S["pe"], sc)
            v.wait_ge(S["pe"], prev_nd)
            v.tensor_tensor(attnT_sb[2][:], pb[2].ap(), mask_sb[:], ALU.mult)
            v.tensor_tensor(attnT_sb[3][:], pb[3].ap(), mask_sb[:], ALU.mult).then_inc(S["dve"], 1)
        emit("vector", f)
        mask_dve[1] = bump("dve")
        last["mask_dve_last"] = mask_dve[1]

        for wv in range(2):
            Ssb = last["Ssb_dve"]
            kva = p2[3]["kv_dve"]
            sil = p2[3]["silu_act"]
            rec = last["recip_dve"]
            gm = max(last["gate_dve"])
            prw = last["praw_dve"]
            pta = last["prawT_act"]
            def f(t, i=i, wv=wv, md=mask_dve[wv], Ssb=Ssb, kva=kva, sil=sil, rec=rec, gm=gm,
                  prw=prw, pta=pta):
                t.wait_ge(S["dve"], md)
                t.wait_ge(S["dve"], Ssb)
                t.wait_ge(S["dve"], kva)
                if wv == 0:
                    t.wait_ge(S["act"], sil)
                    t.wait_ge(S["act"], pta)
                    t.wait_ge(S["dve"], prw)
                    t.wait_ge(S["dve"], rec)
                    t.wait_ge(S["dve"], gm)
                for p in range(4 * wv, 4 * wv + 4):
                    he, ho = 2 * p, 2 * p + 1
                    pc = p % 4
                    fw = (pc == 0 and wv == 0)
                    aTe = attnT_sb[2 * wv][:, 128 * pc:128 * pc + 128]
                    aTo = attnT_sb[2 * wv + 1][:, 128 * pc:128 * pc + 128]
                    qTe = qT_sb[p][0:64, 128 * i:128 * (i + 1)]
                    qTo = qT_sb[p][64:128, 128 * i:128 * (i + 1)]
                    t.matmul(pb[4].ap()[:, 64 * p:64 * p + 64], aTe,
                             vaug_sb[i][:, 65 * he:65 * he + 64], start=fw, stop=False)
                    t.matmul(pb[0].ap()[:, 2 * p:2 * p + 1], aTe, ones_col[:], start=fw, stop=False)
                    t.matmul(pb[5].ap()[:, 64 * p:64 * p + 64], aTo,
                             vaug_sb[i][:, 65 * ho:65 * ho + 64], start=fw, stop=False)
                    t.matmul(pb[1].ap()[:, 2 * p:2 * p + 1], aTo, ones_col[:], start=fw, stop=False)
                    t.matmul(pb[4].ap()[:, 64 * p:64 * p + 64], qTe,
                             S_sb[0:64, 66 * p:66 * p + 64], start=False, stop=True)
                    t.matmul(pb[0].ap()[:, 2 * p:2 * p + 1], qTe,
                             S_sb[0:64, 66 * p + 64:66 * p + 65], start=False, stop=True)
                    t.matmul(pb[5].ap()[:, 64 * p:64 * p + 64], qTo,
                             S_sb[64:128, 66 * p:66 * p + 64], start=False, stop=True)
                    t.matmul(pb[1].ap()[:, 2 * p:2 * p + 1], qTo,
                             S_sb[64:128, 66 * p + 64:66 * p + 65], start=False,
                             stop=True).then_inc(S["pe"], 1)
            emit("tensor", f)
            nd_pe[wv] = bump("pe", 4)
        last["nd_pe_prev"] = nd_pe[1]
        num_done = nd_pe[1]

        dmp = last.get("dump_sto", 0) if "praw_dump" in dbg_t else 0
        pta2 = last["prawT_act"]
        def f(v, num_done=num_done, dmp=dmp, pta2=pta2):
            v.wait_ge(S["pe"], num_done)
            v.wait_ge(S["sto"], dmp)
            v.wait_ge(S["act"], pta2)
            v.tensor_scalar_max(den_scr[:, 0:8], bass.AP(pb[0].ap().tensor, 0, [[512, 128], [2, 8]]), 1e-6)
            v.tensor_scalar_max(den_scr[:, 8:16], bass.AP(pb[1].ap().tensor, 0, [[512, 128], [2, 8]]), 1e-6)
            v.drain()
            v.reciprocal(rden_sb[:], den_scr[:])
            v.drain()
            v.engine_nop()
            v.drain().then_inc(S["dve"], 1)
        emit("vector", f)
        last["recip_dve"] = bump("dve")

        ptT = last.get("prawT_pe", 0)
        def f(v, ptT=ptT):
            v.wait_ge(S["pe"], ptT)
            rep_e = man_ap(rden_sb, 0, [[16, 128], [1, 8], [0, 64]])
            rep_o = man_ap(rden_sb, 8, [[16, 128], [1, 8], [0, 64]])
            dst_e = man_ap(praw_tok, 0, [[E, 128], [128, 8], [1, 64]])
            dst_o = man_ap(praw_tok, 64, [[E, 128], [128, 8], [1, 64]])
            v.tensor_tensor(dst_e, pb[4].ap().rearrange("p (a b) -> p a b", a=8), rep_e, ALU.mult)
            v.tensor_tensor(dst_o, pb[5].ap().rearrange("p (a b) -> p a b", a=8), rep_o,
                            ALU.mult).then_inc(S["dve"], 1)
        emit("vector", f)
        praw_done = bump("dve")
        last["praw_dve"] = praw_done

        if "praw_dump" in dbg_t:
            def f(g, i=i, t0g=t0g, praw_done=praw_done):
                g.wait_ge(S["dve"], praw_done)
                g.dma_start(out=dbg_t["praw_dump"][t0g + 128 * i:t0g + 128 * (i + 1), :],
                            in_=praw_tok[:]).then_inc(S["sto"], 16)
                g.dma_start(out=dbg_t["den_dump"][t0g + 128 * i:t0g + 128 * (i + 1), 0:16],
                            in_=den_scr[:]).then_inc(S["sto"], 16)
                g.dma_start(out=dbg_t["den_dump"][t0g + 128 * i:t0g + 128 * (i + 1), 16:32],
                            in_=rden_sb[:]).then_inc(S["sto"], 16)
                g.dma_start(out=dbg_t["S_dump"][t0g + 128 * i:t0g + 128 * (i + 1), :],
                            in_=S_sb[:]).then_inc(S["sto"], 16)
            emit("gpsimd", f)
            last["dump_sto"] = bump("sto", 64)

        if pend is not None:
            emit_pend_op(pend)

        return dict(i=i, t0g=t0g, kv=p2[i]["kv_dve"], silu_act=p2[i]["silu_act"],
                    praw_dve=praw_done)

    def emit_chunk_pre(i, p2):
        first = last["first_state"]
        kva = p2[i]["kv_dve"]
        def f(t, i=i, first=first, kva=kva):
            t.wait_ge(S["dve"], kva)
            for p in range(8):
                for par in range(2):
                    h = 2 * p + par
                    bank = pb[6] if p < 4 else pb[7]
                    r0 = 64 * par
                    c0 = 66 * (p % 4)
                    stf = first and (p in (0, 4))
                    t.matmul(bank.ap()[r0:r0 + 64, c0:c0 + 65],
                             ktok_sb[i][:, 64 * h:64 * h + 64],
                             vaug_sb[i][:, 65 * h:65 * h + 65],
                             start=stf, stop=False).then_inc(S["pe"], 1)
        emit("tensor", f)
        bump("pe", 16)
        last["first_state"] = False

    segs = (["pre"] if (prefix and dbg in ("full", "attn")) else []) + ["own"]
    for seg in segs:
        aug_done = set_aug(seg)
        xsrc = x_pre if seg == "pre" else x_own
        ngroups = NG if dbg in ("full", "attn") or seg == "pre" else (NG if dbg == "xn" else 1)
        if NG_OVR:
            ngroups = min(ngroups, NG_OVR)
        for g_i in range(ngroups):
            emit_group(seg, g_i, xsrc, aug_done)

    def f(sync, t0=cnt["sto0"], t1=cnt["sto1"], td=cnt["sto"]):
        sync.wait_ge(S["sto0"], t0)
        sync.wait_ge(S["sto1"], t1)
        if td:
            sync.wait_ge(S["sto"], td)
    emit("sync", f)

    if False:
        build_nc._mock = dict(prog=prog, sems=sems, ctx=ctx, cnt=dict(cnt))
        return nc, dbg_t
    with ctx, nc.Block() as block:
        @block.sync
        def _(e):
            for fn in prog["sync"]:
                fn(e)

        @block.gpsimd
        def _(e):
            for fn in prog["gpsimd"]:
                fn(e)

        @block.vector
        def _(e):
            for fn in prog["vector"]:
                fn(e)

        @block.scalar
        def _(e):
            for fn in prog["scalar"]:
                fn(e)

        @block.tensor
        def _(e):
            for fn in prog["tensor"]:
                fn(e)

    return nc, dbg_t



def kernel(L0_attn_x, L0_attn_W_qkv, L0_attn_W_out, L0_ffn_kernel, L0_ffn_bias,
           L0_ffn_a, L0_ffn_b):
    x = np.ascontiguousarray(np.asarray(L0_attn_x, dtype=np.float32))
    w_qkv = np.ascontiguousarray(np.asarray(L0_attn_W_qkv, dtype=np.float32))
    w_out = np.ascontiguousarray(np.asarray(L0_attn_W_out, dtype=np.float32))
    w_ffn = np.ascontiguousarray(np.asarray(L0_ffn_kernel, dtype=np.float32))
    b_ffn = np.ascontiguousarray(np.asarray(L0_ffn_bias, dtype=np.float32).reshape(1, E))
    a_lora = np.ascontiguousarray(np.asarray(L0_ffn_a, dtype=np.float32))
    b_lora = np.ascontiguousarray(np.asarray(L0_ffn_b, dtype=np.float32))

    mask = (np.triu(np.ones((128, 128), np.float32)) * 0.125)
    mask = np.ascontiguousarray(np.tile(mask, (1, 4)).astype(np.float32))
    ident = np.ascontiguousarray(np.eye(128).astype(ml_dtypes.bfloat16))
    zeros_pre = np.zeros((T, E), np.float32)

    nc, _ = build_nc(dbg="full", prefix=True)
    in_maps = []
    for c in range(8):
        b, h = c // 2, c % 2
        x_own = np.ascontiguousarray(x[b, 4096 * h:4096 * (h + 1), :])
        x_pre = np.ascontiguousarray(x[b, 0:4096, :]) if h == 1 else zeros_pre
        augc = np.full((128, 1), 1.0 if h == 1 else 0.0, np.float32)
        in_maps.append({
            "x_own": x_own, "x_pre": x_pre,
            "w_qkv": w_qkv, "w_out": w_out, "w_ffn": w_ffn,
            "b_ffn": b_ffn, "a_lora": a_lora, "b_lora": b_lora,
            "mask_in": mask, "ident_in": ident, "augc_in": augc,
        })
    res = run_bass_kernel_spmd(nc, in_maps, core_ids=list(range(8)))
    out = np.empty((B, L, E), np.float32)
    for c in range(8):
        b, h = c // 2, c % 2
        out[b, 4096 * h:4096 * (h + 1), :] = res.results[c]["out"]
    return out


# version 04
# speedup vs baseline: 1.0304x; 1.0304x over previous
import numpy as np
import ml_dtypes
import concourse.bass as bass
import concourse.mybir as mybir
from concourse.bass_utils import run_bass_kernel_spmd

F32 = mybir.dt.float32
BF16 = mybir.dt.bfloat16
I32 = mybir.dt.int32
AF = mybir.ActivationFunctionType
ALU = mybir.AluOpType
AX = mybir.AxisListType.X

B, L, E, H, D = 4, 8192, 1024, 16, 64
T = 4096
NS = 8
NG = T // 512
LN_EPS = 1e-5


def build_nc(dbg="full", prefix=True):
    SKIP = set()
    CHUNK_LVL = 9
    NG_OVR = 0
    nc = bass.Bass("TRN2", target_bir_lowering=False, detect_race_conditions=False)

    x_own = nc.dram_tensor("x_own", [T, E], F32, kind="ExternalInput").ap()
    x_pre = nc.dram_tensor("x_pre", [T, E], F32, kind="ExternalInput").ap()
    w_qkv = nc.dram_tensor("w_qkv", [3 * E, E], F32, kind="ExternalInput").ap()
    w_out = nc.dram_tensor("w_out", [E, E], F32, kind="ExternalInput").ap()
    w_ffn = nc.dram_tensor("w_ffn", [E, E], F32, kind="ExternalInput").ap()
    b_ffn = nc.dram_tensor("b_ffn", [1, E], F32, kind="ExternalInput").ap()
    a_lora = nc.dram_tensor("a_lora", [E, 16], F32, kind="ExternalInput").ap()
    b_lora = nc.dram_tensor("b_lora", [16, E], F32, kind="ExternalInput").ap()
    mask_in = nc.dram_tensor("mask_in", [128, 512], F32, kind="ExternalInput").ap()
    ident_in = nc.dram_tensor("ident_in", [128, 128], BF16, kind="ExternalInput").ap()
    augc_in = nc.dram_tensor("augc_in", [128, 1], F32, kind="ExternalInput").ap()
    out = nc.dram_tensor("out", [T, E], F32, kind="ExternalOutput").ap()

    dbg_t = {}
    if dbg == "xn":
        dbg_t["xn_dump"] = nc.dram_tensor("xn_dump", [T, E], F32, kind="ExternalOutput").ap()
    if dbg == "qkv":
        dbg_t["q_dump"] = nc.dram_tensor("q_dump", [E, 512], F32, kind="ExternalOutput").ap()
        dbg_t["k_dump"] = nc.dram_tensor("k_dump", [E, 512], F32, kind="ExternalOutput").ap()
        dbg_t["v_dump"] = nc.dram_tensor("v_dump", [512, 65 * H], F32, kind="ExternalOutput").ap()
        dbg_t["kt_dump"] = nc.dram_tensor("kt_dump", [512, E], F32, kind="ExternalOutput").ap()
        dbg_t["g_dump"] = nc.dram_tensor("g_dump", [512, E], F32, kind="ExternalOutput").ap()
    if dbg == "attn":
        dbg_t["praw_dump"] = nc.dram_tensor("praw_dump", [T, E], F32, kind="ExternalOutput").ap()
        dbg_t["den_dump"] = nc.dram_tensor("den_dump", [T, 32], F32, kind="ExternalOutput").ap()
        dbg_t["S_dump"] = nc.dram_tensor("S_dump", [T, 520], F32, kind="ExternalOutput").ap()

    from contextlib import ExitStack
    ctx = ExitStack()
    sbt = lambda n, s, d: ctx.enter_context(nc.sbuf_tensor(n, s, d))

    wqkvT_all = sbt("wqkvT_all", [128, 3 * E * NS], BF16)
    wqkvT = [wqkvT_all[:, 3 * E * s:3 * E * (s + 1)] for s in range(NS)]
    woutT_all = sbt("woutT_all", [128, E * NS], BF16)
    woutT = [woutT_all[:, E * s:E * (s + 1)] for s in range(NS)]
    wq_ffn = [sbt(f"wqffn{s}", [128, E], BF16) for s in range(NS)]
    a_sb = sbt("a_sb", [128, 16 * NS], BF16)
    b_sb = sbt("b_sb", [16, E], BF16)
    bias_sb = sbt("bias_sb", [1, E], BF16)
    mask_sb = sbt("mask_sb", [128, 512], F32)
    ident_sb = sbt("ident_sb", [128, 128], BF16)
    augc_sb = sbt("augc_sb", [128, 1], F32)
    ones_col = sbt("ones_col", [128, 1], BF16)
    ones_row = sbt("ones_row", [1, 128], BF16)

    wstage = [sbt(f"wstage{i}", [128, E], BF16) for i in range(2)]
    qscr = sbt("qscr", [128, E], F32)
    qscr_i = sbt("qscr_i", [128, E], I32)
    amax_c = sbt("amax_c", [128, NS + 2], F32)
    amax_r = sbt("amax_r", [1, 128], F32)
    scale_c = sbt("scale_c", [128, 2], F32)

    x_sb = [sbt(f"x_sb{i}", [128, E], F32) for i in range(4)]
    xn_sb = [sbt(f"xn_sb{i}", [128, E], BF16) for i in range(4)]
    xnT_all = sbt("xnT_all", [128, 512 * NS], BF16)
    xnT_sb = [xnT_all[:, 512 * s:512 * (s + 1)] for s in range(NS)]
    qT_sb = [sbt(f"qT{s}", [128, 512], BF16) for s in range(NS)]
    kT_sb = [sbt(f"kT{s}", [128, 512], BF16) for s in range(NS)]
    ktok_sb = [sbt(f"ktok{i}", [128, E], BF16) for i in range(4)]
    vaug_sb = [sbt(f"vaug{i}", [128, 65 * H], BF16) for i in range(4)]
    e8_sb = sbt("e8_sb", [128, 1024], F32)
    em_sb = sbt("em_sb", [128, 512], F32)
    stat_sum = sbt("stat_sum", [128, 4], F32)
    stat_sq = sbt("stat_sq", [128, 4], F32)
    stat_m = sbt("stat_m", [128, 4], F32)
    stat_v = sbt("stat_v", [128, 4], F32)
    stat_y = sbt("stat_y", [128, 4], F32)
    stat_t = sbt("stat_t", [128, 4], F32)
    ln_bias = sbt("ln_bias", [128, 4], F32)
    ln_scale = sbt("ln_scale", [128, 4], F32)
    lnscr = qscr

    attnT_sb = [sbt(f"attnT{w}", [128, 512], BF16) for w in range(4)]
    S_sb = sbt("S_sb", [128, 66 * 8], BF16)
    den_scr = sbt("den_scr", [128, 16], F32)
    rden_sb = sbt("rden_sb", [128, 16], F32)
    praw_tok = sbt("praw_tok", [128, E], BF16)
    praw_fm_all = sbt("praw_fm_all", [128, 1024], BF16)
    u_sb = sbt("u_sb", [16, 512], BF16)
    g_sb = [sbt(f"g_sb{i}", [128, E], BF16) for i in range(4)]
    sg_sb = [sbt(f"sg_sb{i}", [128, E], F32) for i in range(2)]
    out_sb = [sbt(f"out_sb{i}", [128, E], F32) for i in range(2)]
    kf32 = [x_sb[0], x_sb[1], x_sb[2], x_sb[3], sg_sb[0], sg_sb[1], out_sb[0], out_sb[1]]

    pb = [ctx.enter_context(nc.psum_tensor(f"pb{i}", [128, 512], F32)) for i in range(8)]
    pbv = [p.ap().bitcast(BF16) for p in pb]

    sems = {}
    cnt = {}
    for name in ["ldc", "ldk", "lab", "ldw0", "ldw1",
                 "ldx0", "ldx1", "ldx2", "ldx3",
                 "pe", "dve", "act", "gp", "sto", "sto0", "sto1"]:
        sems[name] = ctx.enter_context(nc.semaphore(name))
        cnt[name] = 0
    S = sems

    prog = {k: [] for k in ["sync", "gpsimd", "vector", "scalar", "tensor"]}
    SERIAL = ""

    _eng_sem = {"vector": "dve", "scalar": "act", "tensor": "pe"}

    def emit(eng, fn):
        if False and SERIAL and eng in ("vector", "scalar", "tensor"):
            waits = [(nm, cnt[nm]) for nm in ("pe", "dve", "act")
                     if nm != _eng_sem[eng] and cnt[nm] > 0]
            def wrapped(e, fn=fn, waits=waits):
                for nm, vv in waits:
                    e.wait_ge(S[nm], vv)
                fn(e)
            prog[eng].append(wrapped)
        else:
            prog[eng].append(fn)

    def bump(sem, n=1):
        cnt[sem] += n
        return cnt[sem]

    def ap_of(t):
        return t[:]

    def man_ap(tile, off, dims):
        return bass.AP(tile[:].tensor, off, dims)

    def f(sync):
        sync.dma_start(out=mask_sb[:], in_=mask_in[:]).then_inc(S["ldc"], 16)
        sync.dma_start(out=ident_sb[:], in_=ident_in[:]).then_inc(S["ldc"], 16)
        sync.dma_start(out=augc_sb[:], in_=augc_in[:]).then_inc(S["ldc"], 16)
        for s in range(NS):
            sync.dma_start(out=kf32[s][:], in_=w_ffn[128 * s:128 * (s + 1), :]).then_inc(S["ldk"], 16)
    emit("sync", f)
    const_done = bump("ldc", 48)
    kf32_done = bump("ldk", 16 * NS)

    def f(g):
        for s in range(NS):
            g.dma_start(out=a_sb[:, 16 * s:16 * (s + 1)],
                        in_=a_lora[128 * s:128 * (s + 1), :]).then_inc(S["lab"], 16)
        g.dma_start(out=b_sb[:], in_=b_lora[:]).then_inc(S["lab"], 16)
        g.dma_start(out=bias_sb[:], in_=b_ffn[:]).then_inc(S["lab"], 16)
    emit("gpsimd", f)
    ab_done = bump("lab", 16 * (NS + 2))

    raw_tiles = [("qkv", i) for i in range(24)] + [("out", i) for i in range(8)]
    wT_pe_base = cnt["pe"]
    wT_act_base = cnt["act"]

    if "wt" not in SKIP:
        def f(g, wT_pe_base=wT_pe_base):
            for idx, (kind, i) in enumerate(raw_tiles):
                stg = wstage[idx % 2]
                src = w_qkv[128 * i:128 * (i + 1), :] if kind == "qkv" else w_out[128 * i:128 * (i + 1), :]
                if idx >= 2:
                    g.wait_ge(S["pe"], wT_pe_base + (idx - 1) * 8)
                g.dma_start(out=stg[:], in_=src).then_inc(S["ldw" + str(idx % 2)], 16)
        emit("gpsimd", f)
        bump("ldw0", 16 * ((len(raw_tiles) + 1) // 2))
        bump("ldw1", 16 * (len(raw_tiles) // 2))

    if "wt" not in SKIP:
        def f(t, wT_act_base=wT_act_base, const_done=const_done):
            t.wait_ge(S["ldc"], const_done)
            for idx, (kind, i) in enumerate(raw_tiles):
                stg = wstage[idx % 2]
                t.wait_ge(S["ldw" + str(idx % 2)], 16 * (idx // 2 + 1))
                bv = pbv[idx % 2]
                if idx >= 2:
                    t.wait_ge(S["act"], wT_act_base + (idx - 1) * 8)
                for s in range(NS):
                    t.transpose(bv[:, 128 * s:128 * (s + 1)], stg[:, 128 * s:128 * (s + 1)],
                                ident_sb[:]).then_inc(S["pe"], 1)
        emit("tensor", f)
        bump("pe", 8 * len(raw_tiles))

        def f(a, wT_pe_base=wT_pe_base):
            for idx, (kind, i) in enumerate(raw_tiles):
                bv = pbv[idx % 2]
                for s in range(NS):
                    a.wait_ge(S["pe"], wT_pe_base + (idx + 1) * 8)
                    dst = (wqkvT[s] if kind == "qkv" else woutT[s])[:, 128 * i:128 * (i + 1)]
                    a.copy(dst, bv[:, 128 * s:128 * (s + 1)]).then_inc(S["act"], 1)
        emit("scalar", f)
    wT_act_done = bump("act", 8 * len(raw_tiles)) if "wt" not in SKIP else cnt["act"]

    if "quant" in SKIP:
        def f(v):
            v.memset(wq_ffn[0][:], 0.0)
            v.memset(ones_col[:], 1.0)
            v.memset(ones_row[:], 1.0).then_inc(S["dve"], 1)
        emit("vector", f)
        quant_done = bump("dve")
    if "quant" not in SKIP:
      def f(v, kf32_done=kf32_done):
        v.wait_ge(S["ldk"], kf32_done)
        for s in range(NS):
            v.reduce_max(amax_c[:, s:s + 1], kf32[s][:], axis=AX, apply_absolute_value=True)
        v.drain()
        v.reduce_max(amax_c[:, NS:NS + 1], amax_c[:, 0:NS], axis=AX).then_inc(S["dve"], 1)
      emit("vector", f)
      q1 = bump("dve")

    if "quant" not in SKIP:
      def f(g, q1=q1):
        g.wait_ge(S["dve"], q1)
        g.dma_start(out=amax_r[0:1, :], in_=amax_c[:, NS:NS + 1]).then_inc(S["gp"], 16)
      emit("gpsimd", f)
      q2 = bump("gp", 16)

      def f(v, q2=q2):
        v.wait_ge(S["gp"], q2)
        v.reduce_max(amax_r[0:1, 0:1], amax_r[0:1, :], axis=AX).then_inc(S["dve"], 1)
      emit("vector", f)
      q3 = bump("dve")

      def f(g, q3=q3):
        g.wait_ge(S["dve"], q3)
        g.dma_start(out=amax_c[:, NS + 1:NS + 2],
                    in_=man_ap(amax_r, 0, [[128, 1], [0, 128]])).then_inc(S["gp"], 16)
      emit("gpsimd", f)
      q4 = bump("gp", 16)

      def f(v, q4=q4):
        v.wait_ge(S["gp"], q4)
        am = amax_c[:, NS + 1:NS + 2]
        v.tensor_scalar(scale_c[:, 1:2], am, 1.0 / 7.0, None, ALU.mult)
        v.drain()
        v.reciprocal(scale_c[:, 0:1], scale_c[:, 1:2])
        v.drain()
        for s in range(NS):
            v.tensor_scalar(qscr[:], kf32[s][:], scale_c[:, 0:1], None, ALU.mult)
            v.tensor_copy(qscr_i[:], qscr[:])
            v.tensor_copy(qscr[:], qscr_i[:])
            v.tensor_scalar(wq_ffn[s][:], qscr[:], scale_c[:, 1:2], None, ALU.mult)
        v.memset(ones_col[:], 1.0)
        v.memset(ones_row[:], 1.0).then_inc(S["dve"], 1)
      emit("vector", f)
      quant_done = bump("dve")

    last = dict(
        x_free=[quant_done] * 4,
        x_free_act=[0, 0, 0, 0],
        gate_dve=[0, 0],
        out_dma=[0, 0],
        Ssb_dve=0,
        first_state=True,
        recip_dve=0,
        praw_dve=0,
        prawT_act=0,
        pb23_free_dve=0,
        pb01_k_dve=0,
        silu_act_last=0,
    )

    def set_aug(seg):
        if seg == "own":
            pe_now = cnt["pe"]
            def f(v, pe_now=pe_now):
                v.wait_ge(S["pe"], pe_now)
                for i in range(4):
                    v.memset(man_ap(vaug_sb[i], 64, [[65 * H, 128], [65, H]]), 1.0)
                v.engine_nop().then_inc(S["dve"], 1)
            emit("vector", f)
        else:
            def f(v, const_done=const_done):
                v.wait_ge(S["ldc"], const_done)
                for i in range(4):
                    v.tensor_copy(man_ap(vaug_sb[i], 64, [[65 * H, 128], [65, H]]),
                                  augc_sb[:, 0:1].broadcast_to([128, H]))
                v.engine_nop().then_inc(S["dve"], 1)
            emit("vector", f)
        return bump("dve")

    def f(v):
        v.memset(S_sb[:], 0.0).then_inc(S["dve"], 1)
    emit("vector", f)
    last["Ssb_dve"] = bump("dve")

    def emit_group(seg, g_i, x_src, aug_done):
        own = seg == "own"
        t0g = 512 * g_i

        xf = list(last["x_free"])
        xfa = list(last["x_free_act"])
        xfg = list(last.get("x_free_gp", [0, 0, 0, 0]))
        def f(sync, xf=xf, xfa=xfa, xfg=xfg, x_src=x_src, t0g=t0g):
            for i in range(4):
                sync.wait_ge(S["dve"], xf[i])
                sync.wait_ge(S["act"], xfa[i])
                sync.wait_ge(S["gp"], xfg[i])
                sync.dma_start(out=x_sb[i][:], in_=x_src[t0g + 128 * i:t0g + 128 * (i + 1), :]
                               ).then_inc(S["ldx" + str(i)], 16)
        emit("sync", f)
        xv = [bump("ldx" + str(i), 16) for i in range(4)]

        sq_base = cnt["act"]
        def f(a, xv=xv):
            for i in range(4):
                a.wait_ge(S["ldx" + str(i)], xv[i])
                a.activation(lnscr[:], x_sb[i][:], AF.Square,
                             accum_out=stat_sq[:, i:i + 1]).then_inc(S["act"], 1)
        emit("scalar", f)
        bump("act", 4)

        def f(v, xv=xv, sq_base=sq_base):
            for i in range(4):
                v.wait_ge(S["ldx" + str(i)], xv[i])
                v.reduce_sum(stat_sum[:, i:i + 1], x_sb[i][:], axis=AX)
            v.wait_ge(S["act"], sq_base + 4)
            v.drain()
            v.tensor_scalar(stat_m[:], stat_sum[:], 1.0 / E, None, ALU.mult)
            v.tensor_scalar(stat_v[:], stat_sq[:], 1.0 / E, None, ALU.mult)
            v.drain()
            v.tensor_tensor(stat_t[:], stat_m[:], stat_m[:], ALU.mult)
            v.drain()
            v.tensor_tensor(stat_v[:], stat_v[:], stat_t[:], ALU.subtract)
            v.drain()
            v.tensor_scalar(stat_v[:], stat_v[:], LN_EPS, None, ALU.add)
            v.drain()
            v.engine_nop().then_inc(S["dve"], 1)
        emit("vector", f)
        var_done = bump("dve")

        def f(a, var_done=var_done):
            a.wait_ge(S["dve"], var_done)
            a.activation(stat_t[:], stat_v[:], AF.Sqrt).then_inc(S["act"], 1)
        emit("scalar", f)
        sqrt_done = bump("act")

        def f(v, sqrt_done=sqrt_done):
            v.wait_ge(S["act"], sqrt_done)
            v.reciprocal(stat_y[:], stat_t[:])
            v.drain()
            v.tensor_tensor(stat_t[:], stat_y[:], stat_y[:], ALU.mult)
            v.drain()
            v.tensor_tensor(stat_t[:], stat_t[:], stat_v[:], ALU.mult)
            v.drain()
            v.tensor_scalar(stat_t[:], stat_t[:], -0.5, 1.5, ALU.mult, ALU.add)
            v.drain()
            v.tensor_tensor(ln_scale[:], stat_y[:], stat_t[:], ALU.mult)
            v.drain()
            v.tensor_tensor(ln_bias[:], stat_m[:], ln_scale[:], ALU.mult)
            v.drain()
            v.tensor_scalar(ln_bias[:], ln_bias[:], -1.0, None, ALU.mult)
            v.drain()
            v.engine_nop().then_inc(S["dve"], 1)
        emit("vector", f)
        inv_done = bump("dve")

        xn_base = cnt["act"]
        def f(a, inv_done=inv_done):
            a.wait_ge(S["dve"], inv_done)
            for i in range(4):
                a.activation(xn_sb[i][:], x_sb[i][:], AF.Identity,
                             bias=ln_bias[:, i:i + 1], scale=ln_scale[:, i:i + 1]).then_inc(S["act"], 1)
        emit("scalar", f)
        bump("act", 4)
        for i in range(4):
            last["x_free_act"][i] = xn_base + i + 1

        if own:
            def f(v, xv3=xv[3]):
                v.wait_ge(S["ldx3"], xv3)
                v.tensor_copy(qscr_i[:].bitcast(F32), x_sb[3][:]).then_inc(S["dve"], 1)
            emit("vector", f)
            last["x_free"][3] = bump("dve")

        if dbg == "xn":
            def f(g, xn_base=xn_base, t0g=t0g):
                for i in range(4):
                    g.wait_ge(S["act"], xn_base + i + 1)
                    g.dma_start(out=dbg_t["xn_dump"][t0g + 128 * i:t0g + 128 * (i + 1), :],
                                in_=xn_sb[i][:]).then_inc(S["sto"], 16)
            emit("gpsimd", f)
            dump_done = bump("sto", 64)
            def f(v, xn_base=xn_base, dump_done=dump_done):
                v.wait_ge(S["act"], xn_base + 4)
                v.wait_ge(S["sto"], dump_done)
                v.engine_nop().then_inc(S["dve"], 1)
            emit("vector", f)
            mk = bump("dve")
            for i in range(4):
                last["x_free"][i] = mk
            return

        xnT_pe_base = cnt["pe"]
        gmax = max(last["gate_dve"])
        k01 = last["pb01_k_dve"]
        def f(t, xn_base=xn_base, xnT_pe_base=xnT_pe_base, gmax=gmax, k01=k01,
              wT_act_done=wT_act_done):
            t.wait_ge(S["act"], wT_act_done)
            t.wait_ge(S["dve"], max(gmax, k01))
            for i in range(4):
                t.wait_ge(S["act"], xn_base + i + 1)
                bv = pbv[i % 2]
                if i >= 2:
                    t.wait_ge(S["act"], xnT_pe_base_act[0] + 8 * (i - 1))
                for s in range(NS):
                    t.transpose(bv[:, 128 * s:128 * (s + 1)], xn_sb[i][:, 128 * s:128 * (s + 1)],
                                ident_sb[:]).then_inc(S["pe"], 1)
        emit("tensor", f)
        bump("pe", 32)

        xnT_pe_base_act = [cnt["act"]]
        def f(a, xnT_pe_base=xnT_pe_base):
            for i in range(4):
                bv = pbv[i % 2]
                a.wait_ge(S["pe"], xnT_pe_base + 8 * (i + 1))
                dst = man_ap(xnT_all, 128 * i, [[512 * NS, 128], [512, 8], [1, 128]])
                a.copy(dst, bv[:, 0:1024]).then_inc(S["act"], 8)
        emit("scalar", f)
        xnT_done = bump("act", 32)

        qkT_done = 0
        u_done = 0
        if own:
            p1_pe_base = cnt["pe"]
            p1_stt_base_box = [0]
            def f(t, xnT_done=xnT_done, p1_stt_base_box=p1_stt_base_box):
                t.wait_ge(S["act"], xnT_done)
                for ft in range(16):
                    bank = pb[ft % 2]
                    if ft >= 2:
                        t.wait_ge(S["dve"], p1_stt_base_box[0] + (ft - 1))
                    for s in range(NS):
                        t.matmul(bank.ap(), wqkvT[s][:, 128 * ft:128 * (ft + 1)],
                                 xnT_sb[s][:], start=(s == 0), stop=(s == NS - 1)).then_inc(S["pe"], 1)
            emit("tensor", f)
            bump("pe", 16 * NS)

            p1_exp_base = cnt["act"]
            def f(a, p1_pe_base=p1_pe_base, p1_stt_base_box=p1_stt_base_box):
                for ft in range(16):
                    bank = pb[ft % 2]
                    a.wait_ge(S["pe"], p1_pe_base + NS * (ft + 1))
                    if ft >= 2:
                        a.wait_ge(S["dve"], p1_stt_base_box[0] + (ft - 1))
                    a.activation(e8_sb[:, 512 * (ft % 2):512 * (ft % 2 + 1)], bank.ap(),
                                 AF.Exp).then_inc(S["act"], 1)
            emit("scalar", f)
            bump("act", 16)

            p1_stt_base = cnt["dve"]
            p1_stt_base_box[0] = p1_stt_base
            def f(v, p1_exp_base=p1_exp_base):
                for ft in range(16):
                    bank = pb[ft % 2]
                    dstT = (qT_sb if ft < 8 else kT_sb)[ft % 8]
                    v.wait_ge(S["act"], p1_exp_base + ft + 1)
                    eh = e8_sb[:, 512 * (ft % 2):512 * (ft % 2 + 1)]
                    v.tensor_scalar_min(em_sb[:], eh, 1.0)
                    v.scalar_tensor_tensor(dstT[:], bank.ap(), 0.0, em_sb[:],
                                           ALU.max, ALU.add).then_inc(S["dve"], 1)
            emit("vector", f)
            qkT_done = bump("dve", 16)

            u_pe_base = cnt["pe"]
            pf23 = max(last["pb23_free_dve"], last.get("mask_dve_last", 0))
            pf23a = last["prawT_act"]
            def f(t, ab_done=ab_done, pf23=pf23, pf23a=pf23a, qkT_done=qkT_done):
                t.wait_ge(S["lab"], ab_done)
                t.wait_ge(S["dve"], pf23)
                t.wait_ge(S["act"], pf23a)
                for s in range(NS):
                    t.matmul(pb[2].ap()[0:16, :], a_sb[:, 16 * s:16 * (s + 1)], xnT_sb[s][:],
                             start=(s == 0), stop=(s == NS - 1)).then_inc(S["pe"], 1)
            emit("tensor", f)
            bump("pe", NS)

            def f(v, u_pe_base=u_pe_base):
                v.wait_ge(S["pe"], u_pe_base + NS)
                v.tensor_copy(u_sb[:], pb[2].ap()[0:16, :]).then_inc(S["dve"], 1)
            emit("vector", f)
            u_done = bump("dve")

        p2 = []
        for i in range(4):
            prev_kv = p2[i - 1]["kv_dve"] if i else 0
            prev_silu = p2[i - 1]["silu_act"] if (i and own) else 0
            gmax2 = max(last["gate_dve"])
            k01b = last["pb01_k_dve"]
            pf23 = last["pb23_free_dve"]
            pf23a = last["prawT_act"]
            md_last = last.get("mask_dve_last", 0)
            sil_last = last["silu_act_last"]
            prw_last = last["praw_dve"]
            def f(t, i=i, own=own, qkT_done=qkT_done, u_done=u_done, prev_kv=prev_kv,
                  prev_silu=prev_silu, gmax2=gmax2, k01b=k01b, pf23=pf23, pf23a=pf23a,
                  xnT_done=xnT_done, quant_done=quant_done, sil_last=sil_last,
                  prw_last=prw_last, md_last=md_last):
                t.wait_ge(S["act"], xnT_done)
                t.wait_ge(S["dve"], quant_done)
                if own:
                    t.wait_ge(S["dve"], qkT_done)
                    t.wait_ge(S["dve"], u_done)
                else:
                    t.wait_ge(S["dve"], max(gmax2, k01b))
                t.wait_ge(S["dve"], pf23)
                t.wait_ge(S["act"], pf23a)
                t.wait_ge(S["dve"], md_last)
                if i > 0:
                    t.wait_ge(S["dve"], prev_kv)
                    if own:
                        t.wait_ge(S["act"], prev_silu)
                if own and i == 0:
                    t.wait_ge(S["act"], sil_last)
                    t.wait_ge(S["dve"], prw_last)
                for s in range(NS):
                    st = (s == 0)
                    sp = (s == NS - 1)
                    xt = xnT_sb[s][:, 128 * i:128 * (i + 1)]
                    t.matmul(pb[0].ap(), xt, wqkvT[s][:, E:E + 512], start=st, stop=sp)
                    t.matmul(pb[1].ap(), xt, wqkvT[s][:, E + 512:2 * E], start=st, stop=sp)
                    t.matmul(pb[2].ap(), xt, wqkvT[s][:, 2 * E:2 * E + 512], start=st, stop=sp)
                    mmi = t.matmul(pb[3].ap(), xt, wqkvT[s][:, 2 * E + 512:3 * E], start=st, stop=sp)
                    if own:
                        t.matmul(pb[4].ap(), xt, wq_ffn[s][:, 0:512], start=st, stop=False)
                        t.matmul(pb[5].ap(), xt, wq_ffn[s][:, 512:1024], start=st, stop=False)
                if own:
                    t.matmul(pb[4].ap(), u_sb[:, 128 * i:128 * (i + 1)], b_sb[:, 0:512],
                             start=False, stop=False)
                    t.matmul(pb[5].ap(), u_sb[:, 128 * i:128 * (i + 1)], b_sb[:, 512:1024],
                             start=False, stop=False)
                    t.matmul(pb[4].ap(), ones_row[:], bias_sb[:, 0:512], start=False, stop=True)
                    mmi = t.matmul(pb[5].ap(), ones_row[:], bias_sb[:, 512:1024],
                                   start=False, stop=True)
                mmi.then_inc(S["pe"], 1)
            emit("tensor", f)
            p2_pe_done = bump("pe")

            kexp_base = cnt["act"]
            def f(a, p2_pe_done=p2_pe_done, qkT_done=qkT_done):
                a.wait_ge(S["pe"], p2_pe_done)
                a.wait_ge(S["dve"], qkT_done)
                a.activation(e8_sb[:, 0:512], pb[0].ap(), AF.Exp).then_inc(S["act"], 1)
                a.activation(e8_sb[:, 512:1024], pb[1].ap(), AF.Exp).then_inc(S["act"], 1)
            emit("scalar", f)
            bump("act", 2)

            def f(v, i=i, kexp_base=kexp_base, p2_pe_done=p2_pe_done, aug_done=aug_done):
                v.wait_ge(S["act"], kexp_base + 1)
                v.tensor_scalar_min(em_sb[:], e8_sb[:, 0:512], 1.0)
                v.scalar_tensor_tensor(ktok_sb[i][:, 0:512], pb[0].ap(), 0.0, em_sb[:],
                                       ALU.max, ALU.add)
                v.wait_ge(S["act"], kexp_base + 2)
                v.tensor_scalar_min(em_sb[:], e8_sb[:, 512:1024], 1.0)
                v.scalar_tensor_tensor(ktok_sb[i][:, 512:1024], pb[1].ap(), 0.0, em_sb[:],
                                       ALU.max, ALU.add)
                v.wait_ge(S["dve"], aug_done)
                v.tensor_copy(man_ap(vaug_sb[i], 0, [[65 * H, 128], [65, 8], [1, 64]]),
                              pb[2].ap().rearrange("p (h e) -> p h e", h=8))
                v.tensor_copy(man_ap(vaug_sb[i], 65 * 8, [[65 * H, 128], [65, 8], [1, 64]]),
                              pb[3].ap().rearrange("p (h e) -> p h e", h=8)).then_inc(S["dve"], 1)
            emit("vector", f)
            kv_dve = bump("dve")

            silu_act = 0
            if own:
                def f(a, i=i, p2_pe_done=p2_pe_done):
                    a.wait_ge(S["pe"], p2_pe_done)
                    a.activation(g_sb[i][:, 0:512], pb[4].ap(), AF.Silu).then_inc(S["act"], 1)
                    a.activation(g_sb[i][:, 512:1024], pb[5].ap(), AF.Silu).then_inc(S["act"], 1)
                emit("scalar", f)
                silu_act = bump("act", 2)
            p2.append(dict(kv_dve=kv_dve, silu_act=silu_act))

        last["pb01_k_dve"] = p2[3]["kv_dve"]
        last["pb23_free_dve"] = p2[3]["kv_dve"]
        if own:
            last["silu_act_last"] = p2[3]["silu_act"]

        if dbg == "qkv":
            def f(g, p2=p2, qkT_done=qkT_done):
                for i in range(4):
                    g.wait_ge(S["dve"], p2[i]["kv_dve"])
                    g.dma_start(out=dbg_t["kt_dump"][128 * i:128 * (i + 1), :],
                                in_=ktok_sb[i][:]).then_inc(S["sto"], 16)
                    g.dma_start(out=dbg_t["v_dump"][128 * i:128 * (i + 1), :],
                                in_=vaug_sb[i][:]).then_inc(S["sto"], 16)
                    g.wait_ge(S["act"], p2[i]["silu_act"])
                    g.dma_start(out=dbg_t["g_dump"][128 * i:128 * (i + 1), :],
                                in_=g_sb[i][:]).then_inc(S["sto"], 16)
                g.wait_ge(S["dve"], qkT_done)
                for s in range(NS):
                    g.dma_start(out=dbg_t["q_dump"][128 * s:128 * (s + 1), :],
                                in_=qT_sb[s][:]).then_inc(S["sto"], 16)
                    g.dma_start(out=dbg_t["k_dump"][128 * s:128 * (s + 1), :],
                                in_=kT_sb[s][:]).then_inc(S["sto"], 16)
            emit("gpsimd", f)
            bump("sto", 16 * 28)
            for i in range(4):
                last["x_free"][i] = cnt["dve"]
            return

        if own:
            ROT = True
            pend = None
            for i in range(4):
                if ROT:
                    pend = emit_chunk_own(g_i, i, t0g, p2, qkT_done, pend)
                else:
                    p_ = emit_chunk_own(g_i, i, t0g, p2, qkT_done, None)
                    emit_pend_state(p_)
                    emit_pend_prawT(p_)
                    emit_pend_op(p_)
            if ROT:
                emit_pend_state(pend)
                emit_pend_prawT(pend)
                emit_pend_op(pend)
        else:
            for i in range(4):
                emit_chunk_pre(i, p2)

        if not own:
            for i in range(4):
                last["x_free"][i] = var_done
            if g_i == NG - 1:
                sp = cnt["pe"]
                def f(v, sp=sp):
                    v.wait_ge(S["pe"], sp)
                    v.tensor_scalar(S_sb[:, 0:264], pb[6].ap()[:, 0:264], 0.125, None, ALU.mult)
                    v.tensor_scalar(S_sb[:, 264:528], pb[7].ap()[:, 0:264], 0.125, None,
                                    ALU.mult).then_inc(S["dve"], 1)
                emit("vector", f)
                last["Ssb_dve"] = bump("dve")

    def emit_pend_state(pend):
        pi = pend["i"]
        first = last["first_state"]
        kva = pend["kv"]
        ssb_prev = last["Ssb_dve"]
        def f(t, pi=pi, first=first, kva=kva, ssb_prev=ssb_prev):
            t.wait_ge(S["dve"], kva)
            t.wait_ge(S["dve"], ssb_prev)
            for p in range(8):
                for par in range(2):
                    h = 2 * p + par
                    bank = pb[6] if p < 4 else pb[7]
                    r0 = 64 * par
                    c0 = 66 * (p % 4)
                    stf = first and (p in (0, 4))
                    t.matmul(bank.ap()[r0:r0 + 64, c0:c0 + 65],
                             ktok_sb[pi][:, 64 * h:64 * h + 64],
                             vaug_sb[pi][:, 65 * h:65 * h + 65],
                             start=stf, stop=False).then_inc(S["pe"], 1)
        emit("tensor", f)
        state_done = bump("pe", 16)
        last["first_state"] = False

        dmp = last.get("dump_sto", 0) if "praw_dump" in dbg_t else 0
        def f(v, state_done=state_done, dmp=dmp):
            v.wait_ge(S["pe"], state_done)
            v.wait_ge(S["sto"], dmp)
            v.tensor_scalar(S_sb[:, 0:264], pb[6].ap()[:, 0:264], 0.125, None, ALU.mult)
            v.tensor_scalar(S_sb[:, 264:528], pb[7].ap()[:, 0:264], 0.125, None,
                            ALU.mult).then_inc(S["dve"], 1)
        emit("vector", f)
        last["Ssb_dve"] = bump("dve")

    def emit_pend_prawT(pend):
        prawT_base = cnt["pe"]
        pd = pend["praw_dve"]
        gm = max(last["gate_dve"])
        def f(t, prawT_base=prawT_base, pd=pd, gm=gm):
            t.wait_ge(S["dve"], pd)
            t.wait_ge(S["dve"], gm)
            for sl in range(NS):
                bv = pbv[sl % 2]
                t.transpose(bv[:, 512 + 128 * (sl // 2):512 + 128 * (sl // 2) + 128],
                            praw_tok[:, 128 * sl:128 * (sl + 1)],
                            ident_sb[:]).then_inc(S["pe"], 1)
        emit("tensor", f)
        prawT_pe = bump("pe", 8)
        last["prawT_pe"] = prawT_pe

        def f(a, prawT_base=prawT_base):
            a.wait_ge(S["pe"], prawT_base + 8)
            a.copy(man_ap(praw_fm_all, 0, [[1024, 128], [256, 4], [1, 128]]),
                   pbv[0][:, 512:1024]).then_inc(S["act"], 4)
            a.copy(man_ap(praw_fm_all, 128, [[1024, 128], [256, 4], [1, 128]]),
                   pbv[1][:, 512:1024]).then_inc(S["act"], 4)
        emit("scalar", f)
        last["prawT_act"] = bump("act", 8)

    def emit_pend_op(pend):
        pi = pend["i"]
        t0g = pend["t0g"]
        prawT_act = last["prawT_act"]
        rec = last["recip_dve"]
        gm = max(last["gate_dve"])
        def f(t, prawT_act=prawT_act, rec=rec, gm=gm):
            t.wait_ge(S["act"], prawT_act)
            t.wait_ge(S["dve"], rec)
            t.wait_ge(S["dve"], gm)
            for sl in range(NS):
                pf = praw_fm_all[:, 128 * sl:128 * (sl + 1)]
                t.matmul(pb[0].ap(), pf, woutT[sl][:, 0:512], start=(sl == 0), stop=(sl == NS - 1))
                mm2 = t.matmul(pb[1].ap(), pf, woutT[sl][:, 512:1024], start=(sl == 0),
                               stop=(sl == NS - 1))
            mm2.then_inc(S["pe"], 1)
        emit("tensor", f)
        op_done = bump("pe")

        ob = pi % 2
        ga = pend["silu_act"]
        od = last["out_dma"][ob]
        pta3 = last["prawT_act"]
        def f(v, pi=pi, ob=ob, op_done=op_done, ga=ga, od=od, pta3=pta3):
            v.wait_ge(S["pe"], op_done)
            v.wait_ge(S["act"], ga)
            v.wait_ge(S["sto" + str(ob)], od)
            v.wait_ge(S["act"], pta3)
            v.scalar_tensor_tensor(sg_sb[ob][:, 0:512], g_sb[pi][:, 0:512], 1.0,
                                   pb[0].ap(), ALU.mult, ALU.mult)
            v.scalar_tensor_tensor(sg_sb[ob][:, 512:1024], g_sb[pi][:, 512:1024], 1.0,
                                   pb[1].ap(), ALU.mult, ALU.mult).then_inc(S["dve"], 1)
        emit("vector", f)
        sg_done = bump("dve")
        last["gate_dve"][ob] = sg_done
        if pi != 3:
            last["x_free"][pi] = sg_done

        xres = x_sb[pi][:] if pi != 3 else qscr_i[:].bitcast(F32)
        def f(v2, ob=ob, xres=xres, sg_done=sg_done):
            v2.tensor_tensor(out_sb[ob][:], sg_sb[ob][:], xres, ALU.add).then_inc(S["dve"], 1)
        emit("vector", f)
        gate_done = bump("dve")
        if pi != 3:
            last["x_free"][pi] = gate_done
        def f(g, pi=pi, ob=ob, t0g=t0g, gate_done=gate_done):
            g.wait_ge(S["dve"], gate_done)
            g.dma_start(out=out[t0g + 128 * pi:t0g + 128 * (pi + 1), :],
                        in_=out_sb[ob][:]).then_inc(S["sto" + str(ob)], 16)
        emit("gpsimd", f)
        last["out_dma"][ob] = bump("sto" + str(ob), 16)

    CSER = False

    def emit_chunk_own(g_i, i, t0g, p2, qkT_done, pend):
        mask_dve = [0, 0]
        if CSER:
            dv_now, ac_now, pe_now2 = cnt["dve"], cnt["act"], cnt["pe"]
            def fser(t, a=dv_now, b=ac_now):
                t.wait_ge(S["dve"], a); t.wait_ge(S["act"], b)
            emit("tensor", fser)
            def fser2(v, a=pe_now2):
                v.wait_ge(S["pe"], a)
            emit("vector", fser2)
            def fser3(a_, a=pe_now2):
                a_.wait_ge(S["pe"], a)
            emit("scalar", fser3)
        sc_pe = [0, 0]
        nd_pe = [0, 0]

        pf23 = last["pb23_free_dve"]
        md_prev = last.get("mask_dve_last", 0)
        base = cnt["pe"]
        def f(t, i=i, qkT_done=qkT_done, pf23=pf23, md_prev=md_prev):
            t.wait_ge(S["dve"], qkT_done)
            t.wait_ge(S["dve"], pf23)
            t.wait_ge(S["dve"], md_prev)
            for p in range(4):
                kTe = kT_sb[p][0:64, 128 * i:128 * (i + 1)]
                kTo = kT_sb[p][64:128, 128 * i:128 * (i + 1)]
                qTe = qT_sb[p][0:64, 128 * i:128 * (i + 1)]
                qTo = qT_sb[p][64:128, 128 * i:128 * (i + 1)]
                c0 = 128 * p
                t.matmul(pb[2].ap()[:, c0:c0 + 128], kTe, qTe, start=True, stop=True)
                t.matmul(pb[3].ap()[:, c0:c0 + 128], kTo, qTo, start=True,
                         stop=True).then_inc(S["pe"], 1)
        emit("tensor", f)
        sc_pe[0] = bump("pe", 4)

        prev_nd = last.get("nd_pe_prev", 0)
        def f(v, sc=sc_pe[0], prev_nd=prev_nd):
            v.wait_ge(S["pe"], sc)
            v.wait_ge(S["pe"], prev_nd)
            v.tensor_tensor(attnT_sb[0][:], pb[2].ap(), mask_sb[:], ALU.mult)
            v.tensor_tensor(attnT_sb[1][:], pb[3].ap(), mask_sb[:], ALU.mult).then_inc(S["dve"], 1)
        emit("vector", f)
        mask_dve[0] = bump("dve")

        if pend is not None:
            emit_pend_state(pend)
            emit_pend_prawT(pend)

        def f(t, i=i, md0=mask_dve[0]):
            t.wait_ge(S["dve"], md0)
            for p in range(4, 8):
                kTe = kT_sb[p][0:64, 128 * i:128 * (i + 1)]
                kTo = kT_sb[p][64:128, 128 * i:128 * (i + 1)]
                qTe = qT_sb[p][0:64, 128 * i:128 * (i + 1)]
                qTo = qT_sb[p][64:128, 128 * i:128 * (i + 1)]
                c0 = 128 * (p % 4)
                t.matmul(pb[2].ap()[:, c0:c0 + 128], kTe, qTe, start=True, stop=True)
                t.matmul(pb[3].ap()[:, c0:c0 + 128], kTo, qTo, start=True,
                         stop=True).then_inc(S["pe"], 1)
        emit("tensor", f)
        sc_pe[1] = bump("pe", 4)

        def f(v, sc=sc_pe[1], prev_nd=prev_nd):
            v.wait_ge(S["pe"], sc)
            v.wait_ge(S["pe"], prev_nd)
            v.tensor_tensor(attnT_sb[2][:], pb[2].ap(), mask_sb[:], ALU.mult)
            v.tensor_tensor(attnT_sb[3][:], pb[3].ap(), mask_sb[:], ALU.mult).then_inc(S["dve"], 1)
        emit("vector", f)
        mask_dve[1] = bump("dve")
        last["mask_dve_last"] = mask_dve[1]

        for wv in range(2):
            Ssb = last["Ssb_dve"]
            kva = p2[3]["kv_dve"]
            sil = p2[3]["silu_act"]
            rec = last["recip_dve"]
            gm = max(last["gate_dve"])
            prw = last["praw_dve"]
            pta = last["prawT_act"]
            def f(t, i=i, wv=wv, md=mask_dve[wv], Ssb=Ssb, kva=kva, sil=sil, rec=rec, gm=gm,
                  prw=prw, pta=pta):
                t.wait_ge(S["dve"], md)
                t.wait_ge(S["dve"], Ssb)
                t.wait_ge(S["dve"], kva)
                if wv == 0:
                    t.wait_ge(S["act"], sil)
                    t.wait_ge(S["act"], pta)
                    t.wait_ge(S["dve"], prw)
                    t.wait_ge(S["dve"], rec)
                    t.wait_ge(S["dve"], gm)
                for p in range(4 * wv, 4 * wv + 4):
                    he, ho = 2 * p, 2 * p + 1
                    pc = p % 4
                    fw = (pc == 0 and wv == 0)
                    aTe = attnT_sb[2 * wv][:, 128 * pc:128 * pc + 128]
                    aTo = attnT_sb[2 * wv + 1][:, 128 * pc:128 * pc + 128]
                    qTe = qT_sb[p][0:64, 128 * i:128 * (i + 1)]
                    qTo = qT_sb[p][64:128, 128 * i:128 * (i + 1)]
                    t.matmul(pb[4].ap()[:, 64 * p:64 * p + 64], aTe,
                             vaug_sb[i][:, 65 * he:65 * he + 64], start=fw, stop=False)
                    t.matmul(pb[0].ap()[:, 2 * p:2 * p + 1], aTe, ones_col[:], start=fw, stop=False)
                    t.matmul(pb[5].ap()[:, 64 * p:64 * p + 64], aTo,
                             vaug_sb[i][:, 65 * ho:65 * ho + 64], start=fw, stop=False)
                    t.matmul(pb[1].ap()[:, 2 * p:2 * p + 1], aTo, ones_col[:], start=fw, stop=False)
                    t.matmul(pb[4].ap()[:, 64 * p:64 * p + 64], qTe,
                             S_sb[0:64, 66 * p:66 * p + 64], start=False, stop=True)
                    t.matmul(pb[0].ap()[:, 2 * p:2 * p + 1], qTe,
                             S_sb[0:64, 66 * p + 64:66 * p + 65], start=False, stop=True)
                    t.matmul(pb[5].ap()[:, 64 * p:64 * p + 64], qTo,
                             S_sb[64:128, 66 * p:66 * p + 64], start=False, stop=True)
                    t.matmul(pb[1].ap()[:, 2 * p:2 * p + 1], qTo,
                             S_sb[64:128, 66 * p + 64:66 * p + 65], start=False,
                             stop=True).then_inc(S["pe"], 1)
            emit("tensor", f)
            nd_pe[wv] = bump("pe", 4)
        last["nd_pe_prev"] = nd_pe[1]
        num_done = nd_pe[1]

        dmp = last.get("dump_sto", 0) if "praw_dump" in dbg_t else 0
        pta2 = last["prawT_act"]
        def f(v, num_done=num_done, dmp=dmp, pta2=pta2):
            v.wait_ge(S["pe"], num_done)
            v.wait_ge(S["sto"], dmp)
            v.wait_ge(S["act"], pta2)
            v.tensor_scalar_max(den_scr[:, 0:8], bass.AP(pb[0].ap().tensor, 0, [[512, 128], [2, 8]]), 1e-6)
            v.tensor_scalar_max(den_scr[:, 8:16], bass.AP(pb[1].ap().tensor, 0, [[512, 128], [2, 8]]), 1e-6)
            v.drain()
            v.reciprocal(rden_sb[:], den_scr[:])
            v.drain()
            v.engine_nop()
            v.drain().then_inc(S["dve"], 1)
        emit("vector", f)
        last["recip_dve"] = bump("dve")

        ptT = last.get("prawT_pe", 0)
        def f(v, ptT=ptT):
            v.wait_ge(S["pe"], ptT)
            rep_e = man_ap(rden_sb, 0, [[16, 128], [1, 8], [0, 64]])
            rep_o = man_ap(rden_sb, 8, [[16, 128], [1, 8], [0, 64]])
            dst_e = man_ap(praw_tok, 0, [[E, 128], [128, 8], [1, 64]])
            dst_o = man_ap(praw_tok, 64, [[E, 128], [128, 8], [1, 64]])
            v.tensor_tensor(dst_e, pb[4].ap().rearrange("p (a b) -> p a b", a=8), rep_e, ALU.mult)
            v.tensor_tensor(dst_o, pb[5].ap().rearrange("p (a b) -> p a b", a=8), rep_o,
                            ALU.mult).then_inc(S["dve"], 1)
        emit("vector", f)
        praw_done = bump("dve")
        last["praw_dve"] = praw_done

        if "praw_dump" in dbg_t:
            def f(g, i=i, t0g=t0g, praw_done=praw_done):
                g.wait_ge(S["dve"], praw_done)
                g.dma_start(out=dbg_t["praw_dump"][t0g + 128 * i:t0g + 128 * (i + 1), :],
                            in_=praw_tok[:]).then_inc(S["sto"], 16)
                g.dma_start(out=dbg_t["den_dump"][t0g + 128 * i:t0g + 128 * (i + 1), 0:16],
                            in_=den_scr[:]).then_inc(S["sto"], 16)
                g.dma_start(out=dbg_t["den_dump"][t0g + 128 * i:t0g + 128 * (i + 1), 16:32],
                            in_=rden_sb[:]).then_inc(S["sto"], 16)
                g.dma_start(out=dbg_t["S_dump"][t0g + 128 * i:t0g + 128 * (i + 1), :],
                            in_=S_sb[:]).then_inc(S["sto"], 16)
            emit("gpsimd", f)
            last["dump_sto"] = bump("sto", 64)

        if pend is not None:
            emit_pend_op(pend)

        return dict(i=i, t0g=t0g, kv=p2[i]["kv_dve"], silu_act=p2[i]["silu_act"],
                    praw_dve=praw_done)

    def emit_chunk_pre(i, p2):
        first = last["first_state"]
        kva = p2[i]["kv_dve"]
        def f(t, i=i, first=first, kva=kva):
            t.wait_ge(S["dve"], kva)
            for p in range(8):
                for par in range(2):
                    h = 2 * p + par
                    bank = pb[6] if p < 4 else pb[7]
                    r0 = 64 * par
                    c0 = 66 * (p % 4)
                    stf = first and (p in (0, 4))
                    t.matmul(bank.ap()[r0:r0 + 64, c0:c0 + 65],
                             ktok_sb[i][:, 64 * h:64 * h + 64],
                             vaug_sb[i][:, 65 * h:65 * h + 65],
                             start=stf, stop=False).then_inc(S["pe"], 1)
        emit("tensor", f)
        bump("pe", 16)
        last["first_state"] = False

    segs = (["pre"] if (prefix and dbg in ("full", "attn")) else []) + ["own"]
    for seg in segs:
        aug_done = set_aug(seg)
        xsrc = x_pre if seg == "pre" else x_own
        ngroups = NG if dbg in ("full", "attn") or seg == "pre" else (NG if dbg == "xn" else 1)
        if NG_OVR:
            ngroups = min(ngroups, NG_OVR)
        for g_i in range(ngroups):
            emit_group(seg, g_i, xsrc, aug_done)

    def f(sync, t0=cnt["sto0"], t1=cnt["sto1"], td=cnt["sto"]):
        sync.wait_ge(S["sto0"], t0)
        sync.wait_ge(S["sto1"], t1)
        if td:
            sync.wait_ge(S["sto"], td)
    emit("sync", f)

    if False:
        build_nc._mock = dict(prog=prog, sems=sems, ctx=ctx, cnt=dict(cnt))
        return nc, dbg_t
    with ctx, nc.Block() as block:
        @block.sync
        def _(e):
            for fn in prog["sync"]:
                fn(e)

        @block.gpsimd
        def _(e):
            for fn in prog["gpsimd"]:
                fn(e)

        @block.vector
        def _(e):
            for fn in prog["vector"]:
                fn(e)

        @block.scalar
        def _(e):
            for fn in prog["scalar"]:
                fn(e)

        @block.tensor
        def _(e):
            for fn in prog["tensor"]:
                fn(e)

    return nc, dbg_t



def kernel(L0_attn_x, L0_attn_W_qkv, L0_attn_W_out, L0_ffn_kernel, L0_ffn_bias,
           L0_ffn_a, L0_ffn_b):
    x = np.ascontiguousarray(np.asarray(L0_attn_x, dtype=np.float32))
    w_qkv = np.ascontiguousarray(np.asarray(L0_attn_W_qkv, dtype=np.float32))
    w_out = np.ascontiguousarray(np.asarray(L0_attn_W_out, dtype=np.float32))
    w_ffn = np.ascontiguousarray(np.asarray(L0_ffn_kernel, dtype=np.float32))
    b_ffn = np.ascontiguousarray(np.asarray(L0_ffn_bias, dtype=np.float32).reshape(1, E))
    a_lora = np.ascontiguousarray(np.asarray(L0_ffn_a, dtype=np.float32))
    b_lora = np.ascontiguousarray(np.asarray(L0_ffn_b, dtype=np.float32))

    mask = (np.triu(np.ones((128, 128), np.float32)) * 0.125)
    mask = np.ascontiguousarray(np.tile(mask, (1, 4)).astype(np.float32))
    ident = np.ascontiguousarray(np.eye(128).astype(ml_dtypes.bfloat16))
    zeros_pre = np.zeros((T, E), np.float32)

    nc, _ = build_nc(dbg="full", prefix=True)
    in_maps = []
    for c in range(8):
        b, h = c // 2, c % 2
        x_own = np.ascontiguousarray(x[b, 4096 * h:4096 * (h + 1), :])
        x_pre = np.ascontiguousarray(x[b, 0:4096, :]) if h == 1 else zeros_pre
        augc = np.full((128, 1), 1.0 if h == 1 else 0.0, np.float32)
        in_maps.append({
            "x_own": x_own, "x_pre": x_pre,
            "w_qkv": w_qkv, "w_out": w_out, "w_ffn": w_ffn,
            "b_ffn": b_ffn, "a_lora": a_lora, "b_lora": b_lora,
            "mask_in": mask, "ident_in": ident, "augc_in": augc,
        })
    res = run_bass_kernel_spmd(nc, in_maps, core_ids=list(range(8)))
    out = np.empty((B, L, E), np.float32)
    for c in range(8):
        b, h = c // 2, c % 2
        out[b, 4096 * h:4096 * (h + 1), :] = res.results[c]["out"]
    return out


# version 05 (final)
# speedup vs baseline: 1.0361x; 1.0055x over previous
import numpy as np
import ml_dtypes
import concourse.bass as bass
import concourse.mybir as mybir
from concourse.bass_utils import run_bass_kernel_spmd

F32 = mybir.dt.float32
BF16 = mybir.dt.bfloat16
I32 = mybir.dt.int32
AF = mybir.ActivationFunctionType
ALU = mybir.AluOpType
AX = mybir.AxisListType.X

B, L, E, H, D = 4, 8192, 1024, 16, 64
T = 4096
NS = 8
NG = T // 512
LN_EPS = 1e-5


def build_nc(dbg="full", prefix=True):
    SKIP = set()
    CHUNK_LVL = 9
    NG_OVR = 0
    nc = bass.Bass("TRN2", target_bir_lowering=False, detect_race_conditions=False)

    x_own = nc.dram_tensor("x_own", [T, E], F32, kind="ExternalInput").ap()
    x_pre = nc.dram_tensor("x_pre", [T, E], F32, kind="ExternalInput").ap()
    w_qkv = nc.dram_tensor("w_qkv", [3 * E, E], F32, kind="ExternalInput").ap()
    w_out = nc.dram_tensor("w_out", [E, E], F32, kind="ExternalInput").ap()
    w_ffn = nc.dram_tensor("w_ffn", [E, E], F32, kind="ExternalInput").ap()
    b_ffn = nc.dram_tensor("b_ffn", [1, E], F32, kind="ExternalInput").ap()
    a_lora = nc.dram_tensor("a_lora", [E, 16], F32, kind="ExternalInput").ap()
    b_lora = nc.dram_tensor("b_lora", [16, E], F32, kind="ExternalInput").ap()
    mask_in = nc.dram_tensor("mask_in", [128, 512], F32, kind="ExternalInput").ap()
    ident_in = nc.dram_tensor("ident_in", [128, 128], BF16, kind="ExternalInput").ap()
    augc_in = nc.dram_tensor("augc_in", [128, 1], F32, kind="ExternalInput").ap()
    out = nc.dram_tensor("out", [T, E], F32, kind="ExternalOutput").ap()

    dbg_t = {}
    if dbg == "xn":
        dbg_t["xn_dump"] = nc.dram_tensor("xn_dump", [T, E], F32, kind="ExternalOutput").ap()
    if dbg == "qkv":
        dbg_t["q_dump"] = nc.dram_tensor("q_dump", [E, 512], F32, kind="ExternalOutput").ap()
        dbg_t["k_dump"] = nc.dram_tensor("k_dump", [E, 512], F32, kind="ExternalOutput").ap()
        dbg_t["v_dump"] = nc.dram_tensor("v_dump", [512, 65 * H], F32, kind="ExternalOutput").ap()
        dbg_t["kt_dump"] = nc.dram_tensor("kt_dump", [512, E], F32, kind="ExternalOutput").ap()
        dbg_t["g_dump"] = nc.dram_tensor("g_dump", [512, E], F32, kind="ExternalOutput").ap()
    if dbg == "attn":
        dbg_t["praw_dump"] = nc.dram_tensor("praw_dump", [T, E], F32, kind="ExternalOutput").ap()
        dbg_t["den_dump"] = nc.dram_tensor("den_dump", [T, 32], F32, kind="ExternalOutput").ap()
        dbg_t["S_dump"] = nc.dram_tensor("S_dump", [T, 520], F32, kind="ExternalOutput").ap()

    from contextlib import ExitStack
    ctx = ExitStack()
    sbt = lambda n, s, d: ctx.enter_context(nc.sbuf_tensor(n, s, d))

    wqkvT_all = sbt("wqkvT_all", [128, 3 * E * NS], BF16)
    wqkvT = [wqkvT_all[:, 3 * E * s:3 * E * (s + 1)] for s in range(NS)]
    woutT_all = sbt("woutT_all", [128, E * NS], BF16)
    woutT = [woutT_all[:, E * s:E * (s + 1)] for s in range(NS)]
    wq_ffn = [sbt(f"wqffn{s}", [128, E], BF16) for s in range(NS)]
    a_sb = sbt("a_sb", [128, 16 * NS], BF16)
    b_sb = sbt("b_sb", [16, E], BF16)
    bias_sb = sbt("bias_sb", [1, E], BF16)
    mask_sb = sbt("mask_sb", [128, 512], F32)
    ident_sb = sbt("ident_sb", [128, 128], BF16)
    augc_sb = sbt("augc_sb", [128, 1], F32)
    ones_col = sbt("ones_col", [128, 1], BF16)
    ones_row = sbt("ones_row", [1, 128], BF16)

    wstage = [sbt(f"wstage{i}", [128, E], BF16) for i in range(2)]
    qscr = sbt("qscr", [128, E], F32)
    qscr_i = sbt("qscr_i", [128, E], I32)
    amax_c = sbt("amax_c", [128, NS + 2], F32)
    amax_r = sbt("amax_r", [1, 128], F32)
    scale_c = sbt("scale_c", [128, 2], F32)

    x_sb = [sbt(f"x_sb{i}", [128, E], F32) for i in range(4)]
    xn_sb = [sbt(f"xn_sb{i}", [128, E], BF16) for i in range(4)]
    xnT_all = sbt("xnT_all", [128, 512 * NS], BF16)
    xnT_sb = [xnT_all[:, 512 * s:512 * (s + 1)] for s in range(NS)]
    qT_sb = [sbt(f"qT{s}", [128, 512], BF16) for s in range(NS)]
    kT_sb = [sbt(f"kT{s}", [128, 512], BF16) for s in range(NS)]
    ktok_sb = [sbt(f"ktok{i}", [128, E], BF16) for i in range(4)]
    vaug_sb = [sbt(f"vaug{i}", [128, 65 * H], BF16) for i in range(4)]
    e8_sb = sbt("e8_sb", [128, 1024], F32)
    em_sb = sbt("em_sb", [128, 512], F32)
    stat_sum = sbt("stat_sum", [128, 4], F32)
    stat_sq = sbt("stat_sq", [128, 4], F32)
    stat_m = sbt("stat_m", [128, 4], F32)
    stat_v = sbt("stat_v", [128, 4], F32)
    stat_y = sbt("stat_y", [128, 4], F32)
    stat_t = sbt("stat_t", [128, 4], F32)
    ln_bias = sbt("ln_bias", [128, 4], F32)
    ln_scale = sbt("ln_scale", [128, 4], F32)
    lnscr = qscr

    attnT_sb = [sbt(f"attnT{w}", [128, 512], BF16) for w in range(4)]
    S_sb = sbt("S_sb", [128, 66 * 8], BF16)
    den_scr = sbt("den_scr", [128, 16], F32)
    rden_sb = sbt("rden_sb", [128, 16], F32)
    praw_tok = sbt("praw_tok", [128, E], BF16)
    praw_fm_all = sbt("praw_fm_all", [128, 1024], BF16)
    u_sb = sbt("u_sb", [16, 512], BF16)
    g_sb = [sbt(f"g_sb{i}", [128, E], BF16) for i in range(4)]
    sg_sb = [sbt(f"sg_sb{i}", [128, E], F32) for i in range(2)]
    out_sb = [sbt(f"out_sb{i}", [128, E], F32) for i in range(2)]
    kf32 = [x_sb[0], x_sb[1], x_sb[2], x_sb[3], sg_sb[0], sg_sb[1], out_sb[0], out_sb[1]]

    pb = [ctx.enter_context(nc.psum_tensor(f"pb{i}", [128, 512], F32)) for i in range(8)]
    pbv = [p.ap().bitcast(BF16) for p in pb]

    sems = {}
    cnt = {}
    for name in ["ldc", "ldk", "lab", "ldw0", "ldw1",
                 "ldx0", "ldx1", "ldx2", "ldx3",
                 "pe", "dve", "act", "gp", "sto", "sto0", "sto1"]:
        sems[name] = ctx.enter_context(nc.semaphore(name))
        cnt[name] = 0
    S = sems

    prog = {k: [] for k in ["sync", "gpsimd", "vector", "scalar", "tensor"]}
    SERIAL = ""

    _eng_sem = {"vector": "dve", "scalar": "act", "tensor": "pe"}

    def emit(eng, fn):
        if False and SERIAL and eng in ("vector", "scalar", "tensor"):
            waits = [(nm, cnt[nm]) for nm in ("pe", "dve", "act")
                     if nm != _eng_sem[eng] and cnt[nm] > 0]
            def wrapped(e, fn=fn, waits=waits):
                for nm, vv in waits:
                    e.wait_ge(S[nm], vv)
                fn(e)
            prog[eng].append(wrapped)
        else:
            prog[eng].append(fn)

    def bump(sem, n=1):
        cnt[sem] += n
        return cnt[sem]

    def ap_of(t):
        return t[:]

    def man_ap(tile, off, dims):
        return bass.AP(tile[:].tensor, off, dims)

    def f(sync):
        sync.dma_start(out=mask_sb[:], in_=mask_in[:]).then_inc(S["ldc"], 16)
        sync.dma_start(out=ident_sb[:], in_=ident_in[:]).then_inc(S["ldc"], 16)
        sync.dma_start(out=augc_sb[:], in_=augc_in[:]).then_inc(S["ldc"], 16)
        for s in range(NS):
            sync.dma_start(out=kf32[s][:], in_=w_ffn[128 * s:128 * (s + 1), :]).then_inc(S["ldk"], 16)
    emit("sync", f)
    const_done = bump("ldc", 48)
    kf32_done = bump("ldk", 16 * NS)

    def f(g):
        for s in range(NS):
            g.dma_start(out=a_sb[:, 16 * s:16 * (s + 1)],
                        in_=a_lora[128 * s:128 * (s + 1), :]).then_inc(S["lab"], 16)
        g.dma_start(out=b_sb[:], in_=b_lora[:]).then_inc(S["lab"], 16)
        g.dma_start(out=bias_sb[:], in_=b_ffn[:]).then_inc(S["lab"], 16)
    emit("gpsimd", f)
    ab_done = bump("lab", 16 * (NS + 2))

    raw_tiles = ([("qkv", i) for i in range(8, 24)]
                 + [("qkv", i) for i in range(8)]
                 + [("out", i) for i in range(8)])
    wtrans_pe_mark = []
    wevac_act_mark = []
    wnext = [0]

    def emit_wpart(n, pb01_dve=0, pb01_act=0):
        j0 = wnext[0]
        j1 = min(j0 + n, len(raw_tiles))
        wnext[0] = j1
        if j1 == j0:
            return
        marks = list(wtrans_pe_mark)
        peA_base = cnt["pe"]
        def fd(g, j0=j0, j1=j1, marks=marks, peA_base=peA_base):
            for j in range(j0, j1):
                if j >= 2:
                    if j - 2 < j0:
                        g.wait_ge(S["pe"], marks[j - 2])
                    else:
                        g.wait_ge(S["pe"], peA_base + 8 * (j - 1 - j0))
                kind, i = raw_tiles[j]
                src = w_qkv[128 * i:128 * (i + 1), :] if kind == "qkv" \
                    else w_out[128 * i:128 * (i + 1), :]
                g.dma_start(out=wstage[j % 2][:], in_=src).then_inc(S["ldw" + str(j % 2)], 16)
        emit("gpsimd", fd)
        for j in range(j0, j1):
            bump("ldw" + str(j % 2), 16)
        emarks = list(wevac_act_mark)
        actA_base = cnt["act"]
        def ft(t, j0=j0, j1=j1, emarks=emarks, pb01_dve=pb01_dve, pb01_act=pb01_act,
               const_done=const_done, actA_base=actA_base):
            if j0 == 0:
                t.wait_ge(S["ldc"], const_done)
            if pb01_dve:
                t.wait_ge(S["dve"], pb01_dve)
            if pb01_act:
                t.wait_ge(S["act"], pb01_act)
            for j in range(j0, j1):
                t.wait_ge(S["ldw" + str(j % 2)], 16 * (j // 2 + 1))
                if j >= 2:
                    if j - 2 < len(emarks):
                        t.wait_ge(S["act"], emarks[j - 2])
                    else:
                        t.wait_ge(S["act"], actA_base + 8 * (j - 1 - j0))
                bv = pbv[j % 2]
                for s in range(NS):
                    t.transpose(bv[:, 128 * s:128 * (s + 1)], wstage[j % 2][:, 128 * s:128 * (s + 1)],
                                ident_sb[:]).then_inc(S["pe"], 1)
        emit("tensor", ft)
        for j in range(j0, j1):
            wtrans_pe_mark.append(bump("pe", 8))
        def fe(a, j0=j0, j1=j1, pmarks=list(wtrans_pe_mark)):
            for j in range(j0, j1):
                kind, i = raw_tiles[j]
                bv = pbv[j % 2]
                a.wait_ge(S["pe"], pmarks[j])
                for s in range(NS):
                    dst = (wqkvT[s] if kind == "qkv" else woutT[s])[:, 128 * i:128 * (i + 1)]
                    a.copy(dst, bv[:, 128 * s:128 * (s + 1)]).then_inc(S["act"], 1)
        emit("scalar", fe)
        for j in range(j0, j1):
            wevac_act_mark.append(bump("act", 8))

    NA = 16
    has_pre = prefix and dbg in ("full", "attn")
    if "wt" not in SKIP:
        emit_wpart(NA)
        if not has_pre:
            emit_wpart(len(raw_tiles) - NA)
    wT_act_done = cnt["act"]

    if "quant" in SKIP:
        def f(v):
            v.memset(wq_ffn[0][:], 0.0)
            v.memset(ones_col[:], 1.0)
            v.memset(ones_row[:], 1.0).then_inc(S["dve"], 1)
        emit("vector", f)
        quant_done = bump("dve")
    if "quant" not in SKIP:
      def f(v, kf32_done=kf32_done):
        v.wait_ge(S["ldk"], kf32_done)
        for s in range(NS):
            v.reduce_max(amax_c[:, s:s + 1], kf32[s][:], axis=AX, apply_absolute_value=True)
        v.drain()
        v.reduce_max(amax_c[:, NS:NS + 1], amax_c[:, 0:NS], axis=AX).then_inc(S["dve"], 1)
      emit("vector", f)
      q1 = bump("dve")

    if "quant" not in SKIP:
      def f(g, q1=q1):
        g.wait_ge(S["dve"], q1)
        g.dma_start(out=amax_r[0:1, :], in_=amax_c[:, NS:NS + 1]).then_inc(S["gp"], 16)
      emit("gpsimd", f)
      q2 = bump("gp", 16)

      def f(v, q2=q2):
        v.wait_ge(S["gp"], q2)
        v.reduce_max(amax_r[0:1, 0:1], amax_r[0:1, :], axis=AX).then_inc(S["dve"], 1)
      emit("vector", f)
      q3 = bump("dve")

      def f(g, q3=q3):
        g.wait_ge(S["dve"], q3)
        g.dma_start(out=amax_c[:, NS + 1:NS + 2],
                    in_=man_ap(amax_r, 0, [[128, 1], [0, 128]])).then_inc(S["gp"], 16)
      emit("gpsimd", f)
      q4 = bump("gp", 16)

      def f(v, q4=q4):
        v.wait_ge(S["gp"], q4)
        am = amax_c[:, NS + 1:NS + 2]
        v.tensor_scalar(scale_c[:, 1:2], am, 1.0 / 7.0, None, ALU.mult)
        v.drain()
        v.reciprocal(scale_c[:, 0:1], scale_c[:, 1:2])
        v.drain()
        for s in range(NS):
            v.tensor_scalar(qscr[:], kf32[s][:], scale_c[:, 0:1], None, ALU.mult)
            v.tensor_copy(qscr_i[:], qscr[:])
            v.tensor_copy(qscr[:], qscr_i[:])
            v.tensor_scalar(wq_ffn[s][:], qscr[:], scale_c[:, 1:2], None, ALU.mult)
        v.memset(ones_col[:], 1.0)
        v.memset(ones_row[:], 1.0).then_inc(S["dve"], 1)
      emit("vector", f)
      quant_done = bump("dve")

    last = dict(
        x_free=[quant_done] * 4,
        x_free_act=[0, 0, 0, 0],
        gate_dve=[0, 0],
        out_dma=[0, 0],
        Ssb_dve=0,
        first_state=True,
        recip_dve=0,
        praw_dve=0,
        prawT_act=0,
        pb23_free_dve=0,
        pb01_k_dve=0,
        silu_act_last=0,
    )

    def set_aug(seg):
        if seg == "own":
            pe_now = cnt["pe"]
            def f(v, pe_now=pe_now):
                v.wait_ge(S["pe"], pe_now)
                for i in range(4):
                    v.memset(man_ap(vaug_sb[i], 64, [[65 * H, 128], [65, H]]), 1.0)
                v.engine_nop().then_inc(S["dve"], 1)
            emit("vector", f)
        else:
            def f(v, const_done=const_done):
                v.wait_ge(S["ldc"], const_done)
                for i in range(4):
                    v.tensor_copy(man_ap(vaug_sb[i], 64, [[65 * H, 128], [65, H]]),
                                  augc_sb[:, 0:1].broadcast_to([128, H]))
                v.engine_nop().then_inc(S["dve"], 1)
            emit("vector", f)
        return bump("dve")

    def f(v):
        v.memset(S_sb[:], 0.0).then_inc(S["dve"], 1)
    emit("vector", f)
    last["Ssb_dve"] = bump("dve")

    def emit_group(seg, g_i, x_src, aug_done):
        own = seg == "own"
        t0g = 512 * g_i

        xf = list(last["x_free"])
        xfa = list(last["x_free_act"])
        xfg = list(last.get("x_free_gp", [0, 0, 0, 0]))
        def f(sync, xf=xf, xfa=xfa, xfg=xfg, x_src=x_src, t0g=t0g):
            for i in range(4):
                sync.wait_ge(S["dve"], xf[i])
                sync.wait_ge(S["act"], xfa[i])
                sync.wait_ge(S["gp"], xfg[i])
                sync.dma_start(out=x_sb[i][:], in_=x_src[t0g + 128 * i:t0g + 128 * (i + 1), :]
                               ).then_inc(S["ldx" + str(i)], 16)
        emit("sync", f)
        xv = [bump("ldx" + str(i), 16) for i in range(4)]

        sq_base = cnt["act"]
        def f(a, xv=xv):
            for i in range(4):
                a.wait_ge(S["ldx" + str(i)], xv[i])
                a.activation(lnscr[:], x_sb[i][:], AF.Square,
                             accum_out=stat_sq[:, i:i + 1]).then_inc(S["act"], 1)
        emit("scalar", f)
        bump("act", 4)

        def f(v, xv=xv, sq_base=sq_base):
            for i in range(4):
                v.wait_ge(S["ldx" + str(i)], xv[i])
                v.reduce_sum(stat_sum[:, i:i + 1], x_sb[i][:], axis=AX)
            v.wait_ge(S["act"], sq_base + 4)
            v.drain()
            v.tensor_scalar(stat_m[:], stat_sum[:], 1.0 / E, None, ALU.mult)
            v.tensor_scalar(stat_v[:], stat_sq[:], 1.0 / E, None, ALU.mult)
            v.drain()
            v.tensor_tensor(stat_t[:], stat_m[:], stat_m[:], ALU.mult)
            v.drain()
            v.tensor_tensor(stat_v[:], stat_v[:], stat_t[:], ALU.subtract)
            v.drain()
            v.tensor_scalar(stat_v[:], stat_v[:], LN_EPS, None, ALU.add)
            v.drain()
            v.engine_nop().then_inc(S["dve"], 1)
        emit("vector", f)
        var_done = bump("dve")

        def f(a, var_done=var_done):
            a.wait_ge(S["dve"], var_done)
            a.activation(stat_t[:], stat_v[:], AF.Sqrt).then_inc(S["act"], 1)
        emit("scalar", f)
        sqrt_done = bump("act")

        def f(v, sqrt_done=sqrt_done):
            v.wait_ge(S["act"], sqrt_done)
            v.reciprocal(stat_y[:], stat_t[:])
            v.drain()
            v.tensor_tensor(stat_t[:], stat_y[:], stat_y[:], ALU.mult)
            v.drain()
            v.tensor_tensor(stat_t[:], stat_t[:], stat_v[:], ALU.mult)
            v.drain()
            v.tensor_scalar(stat_t[:], stat_t[:], -0.5, 1.5, ALU.mult, ALU.add)
            v.drain()
            v.tensor_tensor(ln_scale[:], stat_y[:], stat_t[:], ALU.mult)
            v.drain()
            v.tensor_tensor(ln_bias[:], stat_m[:], ln_scale[:], ALU.mult)
            v.drain()
            v.tensor_scalar(ln_bias[:], ln_bias[:], -1.0, None, ALU.mult)
            v.drain()
            v.engine_nop().then_inc(S["dve"], 1)
        emit("vector", f)
        inv_done = bump("dve")

        xn_base = cnt["act"]
        def f(a, inv_done=inv_done):
            a.wait_ge(S["dve"], inv_done)
            for i in range(4):
                a.activation(xn_sb[i][:], x_sb[i][:], AF.Identity,
                             bias=ln_bias[:, i:i + 1], scale=ln_scale[:, i:i + 1]).then_inc(S["act"], 1)
        emit("scalar", f)
        bump("act", 4)
        for i in range(4):
            last["x_free_act"][i] = xn_base + i + 1

        if own:
            def f(v, xv3=xv[3]):
                v.wait_ge(S["ldx3"], xv3)
                v.tensor_copy(qscr_i[:].bitcast(F32), x_sb[3][:]).then_inc(S["dve"], 1)
            emit("vector", f)
            last["x_free"][3] = bump("dve")

        if dbg == "xn":
            def f(g, xn_base=xn_base, t0g=t0g):
                for i in range(4):
                    g.wait_ge(S["act"], xn_base + i + 1)
                    g.dma_start(out=dbg_t["xn_dump"][t0g + 128 * i:t0g + 128 * (i + 1), :],
                                in_=xn_sb[i][:]).then_inc(S["sto"], 16)
            emit("gpsimd", f)
            dump_done = bump("sto", 64)
            def f(v, xn_base=xn_base, dump_done=dump_done):
                v.wait_ge(S["act"], xn_base + 4)
                v.wait_ge(S["sto"], dump_done)
                v.engine_nop().then_inc(S["dve"], 1)
            emit("vector", f)
            mk = bump("dve")
            for i in range(4):
                last["x_free"][i] = mk
            return

        xnT_pe_base = cnt["pe"]
        gmax = max(last["gate_dve"])
        k01 = last["pb01_k_dve"]
        def f(t, xn_base=xn_base, xnT_pe_base=xnT_pe_base, gmax=gmax, k01=k01,
              wT_act_done=wT_act_done):
            t.wait_ge(S["act"], wT_act_done)
            t.wait_ge(S["dve"], max(gmax, k01))
            for i in range(4):
                t.wait_ge(S["act"], xn_base + i + 1)
                bv = pbv[i % 2]
                if i >= 2:
                    t.wait_ge(S["act"], xnT_pe_base_act[0] + 8 * (i - 1))
                for s in range(NS):
                    t.transpose(bv[:, 128 * s:128 * (s + 1)], xn_sb[i][:, 128 * s:128 * (s + 1)],
                                ident_sb[:]).then_inc(S["pe"], 1)
        emit("tensor", f)
        bump("pe", 32)

        xnT_pe_base_act = [cnt["act"]]
        def f(a, xnT_pe_base=xnT_pe_base):
            for i in range(4):
                bv = pbv[i % 2]
                a.wait_ge(S["pe"], xnT_pe_base + 8 * (i + 1))
                dst = man_ap(xnT_all, 128 * i, [[512 * NS, 128], [512, 8], [1, 128]])
                a.copy(dst, bv[:, 0:1024]).then_inc(S["act"], 8)
        emit("scalar", f)
        xnT_done = bump("act", 32)
        last["xnT_act"] = xnT_done

        qkT_done = 0
        u_done = 0
        if own:
            p1_pe_base = cnt["pe"]
            p1_stt_base_box = [0]
            def f(t, xnT_done=xnT_done, p1_stt_base_box=p1_stt_base_box):
                t.wait_ge(S["act"], xnT_done)
                for ft in range(16):
                    bank = pb[ft % 2]
                    if ft >= 2:
                        t.wait_ge(S["dve"], p1_stt_base_box[0] + (ft - 1))
                    for s in range(NS):
                        t.matmul(bank.ap(), wqkvT[s][:, 128 * ft:128 * (ft + 1)],
                                 xnT_sb[s][:], start=(s == 0), stop=(s == NS - 1)).then_inc(S["pe"], 1)
            emit("tensor", f)
            bump("pe", 16 * NS)

            p1_exp_base = cnt["act"]
            def f(a, p1_pe_base=p1_pe_base, p1_stt_base_box=p1_stt_base_box):
                for ft in range(16):
                    bank = pb[ft % 2]
                    a.wait_ge(S["pe"], p1_pe_base + NS * (ft + 1))
                    if ft >= 2:
                        a.wait_ge(S["dve"], p1_stt_base_box[0] + (ft - 1))
                    a.activation(e8_sb[:, 512 * (ft % 2):512 * (ft % 2 + 1)], bank.ap(),
                                 AF.Exp).then_inc(S["act"], 1)
            emit("scalar", f)
            bump("act", 16)

            p1_stt_base = cnt["dve"]
            p1_stt_base_box[0] = p1_stt_base
            def f(v, p1_exp_base=p1_exp_base):
                for ft in range(16):
                    bank = pb[ft % 2]
                    dstT = (qT_sb if ft < 8 else kT_sb)[ft % 8]
                    v.wait_ge(S["act"], p1_exp_base + ft + 1)
                    eh = e8_sb[:, 512 * (ft % 2):512 * (ft % 2 + 1)]
                    v.tensor_scalar_min(em_sb[:], eh, 1.0)
                    v.scalar_tensor_tensor(dstT[:], bank.ap(), 0.0, em_sb[:],
                                           ALU.max, ALU.add).then_inc(S["dve"], 1)
            emit("vector", f)
            qkT_done = bump("dve", 16)

            u_pe_base = cnt["pe"]
            pf23 = max(last["pb23_free_dve"], last.get("mask_dve_last", 0))
            pf23a = last["prawT_act"]
            def f(t, ab_done=ab_done, pf23=pf23, pf23a=pf23a, qkT_done=qkT_done):
                t.wait_ge(S["lab"], ab_done)
                t.wait_ge(S["dve"], pf23)
                t.wait_ge(S["act"], pf23a)
                for s in range(NS):
                    t.matmul(pb[2].ap()[0:16, :], a_sb[:, 16 * s:16 * (s + 1)], xnT_sb[s][:],
                             start=(s == 0), stop=(s == NS - 1)).then_inc(S["pe"], 1)
            emit("tensor", f)
            bump("pe", NS)

            def f(v, u_pe_base=u_pe_base):
                v.wait_ge(S["pe"], u_pe_base + NS)
                v.tensor_copy(u_sb[:], pb[2].ap()[0:16, :]).then_inc(S["dve"], 1)
            emit("vector", f)
            u_done = bump("dve")

        p2 = []
        for i in range(4):
            prev_kv = p2[i - 1]["kv_dve"] if i else 0
            prev_silu = p2[i - 1]["silu_act"] if (i and own) else 0
            gmax2 = max(last["gate_dve"])
            k01b = last["pb01_k_dve"]
            pf23 = last["pb23_free_dve"]
            pf23a = last["prawT_act"]
            md_last = last.get("mask_dve_last", 0)
            sil_last = last["silu_act_last"]
            prw_last = last["praw_dve"]
            def f(t, i=i, own=own, qkT_done=qkT_done, u_done=u_done, prev_kv=prev_kv,
                  prev_silu=prev_silu, gmax2=gmax2, k01b=k01b, pf23=pf23, pf23a=pf23a,
                  xnT_done=xnT_done, quant_done=quant_done, sil_last=sil_last,
                  prw_last=prw_last, md_last=md_last):
                t.wait_ge(S["act"], xnT_done)
                t.wait_ge(S["dve"], quant_done)
                if own:
                    t.wait_ge(S["dve"], qkT_done)
                    t.wait_ge(S["dve"], u_done)
                else:
                    t.wait_ge(S["dve"], max(gmax2, k01b))
                t.wait_ge(S["dve"], pf23)
                t.wait_ge(S["act"], pf23a)
                t.wait_ge(S["dve"], md_last)
                if i > 0:
                    t.wait_ge(S["dve"], prev_kv)
                    if own:
                        t.wait_ge(S["act"], prev_silu)
                if own and i == 0:
                    t.wait_ge(S["act"], sil_last)
                    t.wait_ge(S["dve"], prw_last)
                for s in range(NS):
                    st = (s == 0)
                    sp = (s == NS - 1)
                    xt = xnT_sb[s][:, 128 * i:128 * (i + 1)]
                    t.matmul(pb[0].ap(), xt, wqkvT[s][:, E:E + 512], start=st, stop=sp)
                    t.matmul(pb[1].ap(), xt, wqkvT[s][:, E + 512:2 * E], start=st, stop=sp)
                    t.matmul(pb[2].ap(), xt, wqkvT[s][:, 2 * E:2 * E + 512], start=st, stop=sp)
                    mmi = t.matmul(pb[3].ap(), xt, wqkvT[s][:, 2 * E + 512:3 * E], start=st, stop=sp)
                    if own:
                        t.matmul(pb[4].ap(), xt, wq_ffn[s][:, 0:512], start=st, stop=False)
                        t.matmul(pb[5].ap(), xt, wq_ffn[s][:, 512:1024], start=st, stop=False)
                if own:
                    t.matmul(pb[4].ap(), u_sb[:, 128 * i:128 * (i + 1)], b_sb[:, 0:512],
                             start=False, stop=False)
                    t.matmul(pb[5].ap(), u_sb[:, 128 * i:128 * (i + 1)], b_sb[:, 512:1024],
                             start=False, stop=False)
                    t.matmul(pb[4].ap(), ones_row[:], bias_sb[:, 0:512], start=False, stop=True)
                    mmi = t.matmul(pb[5].ap(), ones_row[:], bias_sb[:, 512:1024],
                                   start=False, stop=True)
                mmi.then_inc(S["pe"], 1)
            emit("tensor", f)
            p2_pe_done = bump("pe")

            kexp_base = cnt["act"]
            def f(a, p2_pe_done=p2_pe_done, qkT_done=qkT_done):
                a.wait_ge(S["pe"], p2_pe_done)
                a.wait_ge(S["dve"], qkT_done)
                a.activation(e8_sb[:, 0:512], pb[0].ap(), AF.Exp).then_inc(S["act"], 1)
                a.activation(e8_sb[:, 512:1024], pb[1].ap(), AF.Exp).then_inc(S["act"], 1)
            emit("scalar", f)
            bump("act", 2)

            def f(v, i=i, kexp_base=kexp_base, p2_pe_done=p2_pe_done, aug_done=aug_done):
                v.wait_ge(S["act"], kexp_base + 1)
                v.tensor_scalar_min(em_sb[:], e8_sb[:, 0:512], 1.0)
                v.scalar_tensor_tensor(ktok_sb[i][:, 0:512], pb[0].ap(), 0.0, em_sb[:],
                                       ALU.max, ALU.add)
                v.wait_ge(S["act"], kexp_base + 2)
                v.tensor_scalar_min(em_sb[:], e8_sb[:, 512:1024], 1.0)
                v.scalar_tensor_tensor(ktok_sb[i][:, 512:1024], pb[1].ap(), 0.0, em_sb[:],
                                       ALU.max, ALU.add)
                v.wait_ge(S["dve"], aug_done)
                v.tensor_copy(man_ap(vaug_sb[i], 0, [[65 * H, 128], [65, 8], [1, 64]]),
                              pb[2].ap().rearrange("p (h e) -> p h e", h=8))
                v.tensor_copy(man_ap(vaug_sb[i], 65 * 8, [[65 * H, 128], [65, 8], [1, 64]]),
                              pb[3].ap().rearrange("p (h e) -> p h e", h=8)).then_inc(S["dve"], 1)
            emit("vector", f)
            kv_dve = bump("dve")

            silu_act = 0
            if own:
                def f(a, i=i, p2_pe_done=p2_pe_done):
                    a.wait_ge(S["pe"], p2_pe_done)
                    a.activation(g_sb[i][:, 0:512], pb[4].ap(), AF.Silu).then_inc(S["act"], 1)
                    a.activation(g_sb[i][:, 512:1024], pb[5].ap(), AF.Silu).then_inc(S["act"], 1)
                emit("scalar", f)
                silu_act = bump("act", 2)
            p2.append(dict(kv_dve=kv_dve, silu_act=silu_act))

        last["pb01_k_dve"] = p2[3]["kv_dve"]
        last["pb23_free_dve"] = p2[3]["kv_dve"]
        if own:
            last["silu_act_last"] = p2[3]["silu_act"]

        if dbg == "qkv":
            def f(g, p2=p2, qkT_done=qkT_done):
                for i in range(4):
                    g.wait_ge(S["dve"], p2[i]["kv_dve"])
                    g.dma_start(out=dbg_t["kt_dump"][128 * i:128 * (i + 1), :],
                                in_=ktok_sb[i][:]).then_inc(S["sto"], 16)
                    g.dma_start(out=dbg_t["v_dump"][128 * i:128 * (i + 1), :],
                                in_=vaug_sb[i][:]).then_inc(S["sto"], 16)
                    g.wait_ge(S["act"], p2[i]["silu_act"])
                    g.dma_start(out=dbg_t["g_dump"][128 * i:128 * (i + 1), :],
                                in_=g_sb[i][:]).then_inc(S["sto"], 16)
                g.wait_ge(S["dve"], qkT_done)
                for s in range(NS):
                    g.dma_start(out=dbg_t["q_dump"][128 * s:128 * (s + 1), :],
                                in_=qT_sb[s][:]).then_inc(S["sto"], 16)
                    g.dma_start(out=dbg_t["k_dump"][128 * s:128 * (s + 1), :],
                                in_=kT_sb[s][:]).then_inc(S["sto"], 16)
            emit("gpsimd", f)
            bump("sto", 16 * 28)
            for i in range(4):
                last["x_free"][i] = cnt["dve"]
            return

        if own:
            ROT = True
            pend = None
            for i in range(4):
                if ROT:
                    pend = emit_chunk_own(g_i, i, t0g, p2, qkT_done, pend)
                else:
                    p_ = emit_chunk_own(g_i, i, t0g, p2, qkT_done, None)
                    emit_pend_state(p_)
                    emit_pend_prawT(p_)
                    emit_pend_op(p_)
            if ROT:
                emit_pend_state(pend)
                emit_pend_prawT(pend)
                emit_pend_op(pend)
        else:
            for i in range(4):
                emit_chunk_pre(i, p2)

        if not own:
            for i in range(4):
                last["x_free"][i] = var_done
            if g_i == NG - 1:
                sp = cnt["pe"]
                def f(v, sp=sp):
                    v.wait_ge(S["pe"], sp)
                    v.tensor_scalar(S_sb[:, 0:264], pb[6].ap()[:, 0:264], 0.125, None, ALU.mult)
                    v.tensor_scalar(S_sb[:, 264:528], pb[7].ap()[:, 0:264], 0.125, None,
                                    ALU.mult).then_inc(S["dve"], 1)
                emit("vector", f)
                last["Ssb_dve"] = bump("dve")

    def emit_pend_state(pend):
        pi = pend["i"]
        first = last["first_state"]
        kva = pend["kv"]
        ssb_prev = last["Ssb_dve"]
        def f(t, pi=pi, first=first, kva=kva, ssb_prev=ssb_prev):
            t.wait_ge(S["dve"], kva)
            t.wait_ge(S["dve"], ssb_prev)
            for p in range(8):
                for par in range(2):
                    h = 2 * p + par
                    bank = pb[6] if p < 4 else pb[7]
                    r0 = 64 * par
                    c0 = 66 * (p % 4)
                    stf = first and (p in (0, 4))
                    t.matmul(bank.ap()[r0:r0 + 64, c0:c0 + 65],
                             ktok_sb[pi][:, 64 * h:64 * h + 64],
                             vaug_sb[pi][:, 65 * h:65 * h + 65],
                             start=stf, stop=False).then_inc(S["pe"], 1)
        emit("tensor", f)
        state_done = bump("pe", 16)
        last["first_state"] = False

        dmp = last.get("dump_sto", 0) if "praw_dump" in dbg_t else 0
        def f(v, state_done=state_done, dmp=dmp):
            v.wait_ge(S["pe"], state_done)
            v.wait_ge(S["sto"], dmp)
            v.tensor_scalar(S_sb[:, 0:264], pb[6].ap()[:, 0:264], 0.125, None, ALU.mult)
            v.tensor_scalar(S_sb[:, 264:528], pb[7].ap()[:, 0:264], 0.125, None,
                            ALU.mult).then_inc(S["dve"], 1)
        emit("vector", f)
        last["Ssb_dve"] = bump("dve")

    def emit_pend_prawT(pend):
        prawT_base = cnt["pe"]
        pd = pend["praw_dve"]
        gm = max(last["gate_dve"])
        def f(t, prawT_base=prawT_base, pd=pd, gm=gm):
            t.wait_ge(S["dve"], pd)
            t.wait_ge(S["dve"], gm)
            for sl in range(NS):
                bv = pbv[sl % 2]
                t.transpose(bv[:, 512 + 128 * (sl // 2):512 + 128 * (sl // 2) + 128],
                            praw_tok[:, 128 * sl:128 * (sl + 1)],
                            ident_sb[:]).then_inc(S["pe"], 1)
        emit("tensor", f)
        prawT_pe = bump("pe", 8)
        last["prawT_pe"] = prawT_pe

        def f(a, prawT_base=prawT_base):
            a.wait_ge(S["pe"], prawT_base + 8)
            a.copy(man_ap(praw_fm_all, 0, [[1024, 128], [256, 4], [1, 128]]),
                   pbv[0][:, 512:1024]).then_inc(S["act"], 4)
            a.copy(man_ap(praw_fm_all, 128, [[1024, 128], [256, 4], [1, 128]]),
                   pbv[1][:, 512:1024]).then_inc(S["act"], 4)
        emit("scalar", f)
        last["prawT_act"] = bump("act", 8)

    def emit_pend_op(pend):
        pi = pend["i"]
        t0g = pend["t0g"]
        prawT_act = last["prawT_act"]
        rec = last["recip_dve"]
        gm = max(last["gate_dve"])
        def f(t, prawT_act=prawT_act, rec=rec, gm=gm):
            t.wait_ge(S["act"], prawT_act)
            t.wait_ge(S["dve"], rec)
            t.wait_ge(S["dve"], gm)
            for sl in range(NS):
                pf = praw_fm_all[:, 128 * sl:128 * (sl + 1)]
                t.matmul(pb[0].ap(), pf, woutT[sl][:, 0:512], start=(sl == 0), stop=(sl == NS - 1))
                mm2 = t.matmul(pb[1].ap(), pf, woutT[sl][:, 512:1024], start=(sl == 0),
                               stop=(sl == NS - 1))
            mm2.then_inc(S["pe"], 1)
        emit("tensor", f)
        op_done = bump("pe")

        ob = pi % 2
        ga = pend["silu_act"]
        od = last["out_dma"][ob]
        pta3 = last["prawT_act"]
        def f(v, pi=pi, ob=ob, op_done=op_done, ga=ga, od=od, pta3=pta3):
            v.wait_ge(S["pe"], op_done)
            v.wait_ge(S["act"], ga)
            v.wait_ge(S["sto" + str(ob)], od)
            v.wait_ge(S["act"], pta3)
            v.scalar_tensor_tensor(sg_sb[ob][:, 0:512], g_sb[pi][:, 0:512], 1.0,
                                   pb[0].ap(), ALU.mult, ALU.mult)
            v.scalar_tensor_tensor(sg_sb[ob][:, 512:1024], g_sb[pi][:, 512:1024], 1.0,
                                   pb[1].ap(), ALU.mult, ALU.mult).then_inc(S["dve"], 1)
        emit("vector", f)
        sg_done = bump("dve")
        last["gate_dve"][ob] = sg_done
        if pi != 3:
            last["x_free"][pi] = sg_done

        xres = x_sb[pi][:] if pi != 3 else qscr_i[:].bitcast(F32)
        def f(v2, ob=ob, xres=xres, sg_done=sg_done):
            v2.tensor_tensor(out_sb[ob][:], sg_sb[ob][:], xres, ALU.add).then_inc(S["dve"], 1)
        emit("vector", f)
        gate_done = bump("dve")
        if pi != 3:
            last["x_free"][pi] = gate_done
        def f(g, pi=pi, ob=ob, t0g=t0g, gate_done=gate_done):
            g.wait_ge(S["dve"], gate_done)
            g.dma_start(out=out[t0g + 128 * pi:t0g + 128 * (pi + 1), :],
                        in_=out_sb[ob][:]).then_inc(S["sto" + str(ob)], 16)
        emit("gpsimd", f)
        last["out_dma"][ob] = bump("sto" + str(ob), 16)

    CSER = False

    def emit_chunk_own(g_i, i, t0g, p2, qkT_done, pend):
        mask_dve = [0, 0]
        if CSER:
            dv_now, ac_now, pe_now2 = cnt["dve"], cnt["act"], cnt["pe"]
            def fser(t, a=dv_now, b=ac_now):
                t.wait_ge(S["dve"], a); t.wait_ge(S["act"], b)
            emit("tensor", fser)
            def fser2(v, a=pe_now2):
                v.wait_ge(S["pe"], a)
            emit("vector", fser2)
            def fser3(a_, a=pe_now2):
                a_.wait_ge(S["pe"], a)
            emit("scalar", fser3)
        sc_pe = [0, 0]
        nd_pe = [0, 0]

        pf23 = last["pb23_free_dve"]
        md_prev = last.get("mask_dve_last", 0)
        base = cnt["pe"]
        def f(t, i=i, qkT_done=qkT_done, pf23=pf23, md_prev=md_prev):
            t.wait_ge(S["dve"], qkT_done)
            t.wait_ge(S["dve"], pf23)
            t.wait_ge(S["dve"], md_prev)
            for p in range(4):
                kTe = kT_sb[p][0:64, 128 * i:128 * (i + 1)]
                kTo = kT_sb[p][64:128, 128 * i:128 * (i + 1)]
                qTe = qT_sb[p][0:64, 128 * i:128 * (i + 1)]
                qTo = qT_sb[p][64:128, 128 * i:128 * (i + 1)]
                c0 = 128 * p
                t.matmul(pb[2].ap()[:, c0:c0 + 128], kTe, qTe, start=True, stop=True)
                t.matmul(pb[3].ap()[:, c0:c0 + 128], kTo, qTo, start=True,
                         stop=True).then_inc(S["pe"], 1)
        emit("tensor", f)
        sc_pe[0] = bump("pe", 4)

        prev_nd = last.get("nd_pe_prev", 0)
        def f(v, sc=sc_pe[0], prev_nd=prev_nd):
            v.wait_ge(S["pe"], sc)
            v.wait_ge(S["pe"], prev_nd)
            v.tensor_tensor(attnT_sb[0][:], pb[2].ap(), mask_sb[:], ALU.mult)
            v.tensor_tensor(attnT_sb[1][:], pb[3].ap(), mask_sb[:], ALU.mult).then_inc(S["dve"], 1)
        emit("vector", f)
        mask_dve[0] = bump("dve")

        if pend is not None:
            emit_pend_state(pend)
            emit_pend_prawT(pend)

        def f(t, i=i, md0=mask_dve[0]):
            t.wait_ge(S["dve"], md0)
            for p in range(4, 8):
                kTe = kT_sb[p][0:64, 128 * i:128 * (i + 1)]
                kTo = kT_sb[p][64:128, 128 * i:128 * (i + 1)]
                qTe = qT_sb[p][0:64, 128 * i:128 * (i + 1)]
                qTo = qT_sb[p][64:128, 128 * i:128 * (i + 1)]
                c0 = 128 * (p % 4)
                t.matmul(pb[2].ap()[:, c0:c0 + 128], kTe, qTe, start=True, stop=True)
                t.matmul(pb[3].ap()[:, c0:c0 + 128], kTo, qTo, start=True,
                         stop=True).then_inc(S["pe"], 1)
        emit("tensor", f)
        sc_pe[1] = bump("pe", 4)

        def f(v, sc=sc_pe[1], prev_nd=prev_nd):
            v.wait_ge(S["pe"], sc)
            v.wait_ge(S["pe"], prev_nd)
            v.tensor_tensor(attnT_sb[2][:], pb[2].ap(), mask_sb[:], ALU.mult)
            v.tensor_tensor(attnT_sb[3][:], pb[3].ap(), mask_sb[:], ALU.mult).then_inc(S["dve"], 1)
        emit("vector", f)
        mask_dve[1] = bump("dve")
        last["mask_dve_last"] = mask_dve[1]

        for wv in range(2):
            Ssb = last["Ssb_dve"]
            kva = p2[3]["kv_dve"]
            sil = p2[3]["silu_act"]
            rec = last["recip_dve"]
            gm = max(last["gate_dve"])
            prw = last["praw_dve"]
            pta = last["prawT_act"]
            def f(t, i=i, wv=wv, md=mask_dve[wv], Ssb=Ssb, kva=kva, sil=sil, rec=rec, gm=gm,
                  prw=prw, pta=pta):
                t.wait_ge(S["dve"], md)
                t.wait_ge(S["dve"], Ssb)
                t.wait_ge(S["dve"], kva)
                if wv == 0:
                    t.wait_ge(S["act"], sil)
                    t.wait_ge(S["act"], pta)
                    t.wait_ge(S["dve"], prw)
                    t.wait_ge(S["dve"], rec)
                    t.wait_ge(S["dve"], gm)
                for p in range(4 * wv, 4 * wv + 4):
                    he, ho = 2 * p, 2 * p + 1
                    pc = p % 4
                    fw = (pc == 0 and wv == 0)
                    aTe = attnT_sb[2 * wv][:, 128 * pc:128 * pc + 128]
                    aTo = attnT_sb[2 * wv + 1][:, 128 * pc:128 * pc + 128]
                    qTe = qT_sb[p][0:64, 128 * i:128 * (i + 1)]
                    qTo = qT_sb[p][64:128, 128 * i:128 * (i + 1)]
                    t.matmul(pb[4].ap()[:, 64 * p:64 * p + 64], aTe,
                             vaug_sb[i][:, 65 * he:65 * he + 64], start=fw, stop=False)
                    t.matmul(pb[0].ap()[:, 2 * p:2 * p + 1], aTe, ones_col[:], start=fw, stop=False)
                    t.matmul(pb[5].ap()[:, 64 * p:64 * p + 64], aTo,
                             vaug_sb[i][:, 65 * ho:65 * ho + 64], start=fw, stop=False)
                    t.matmul(pb[1].ap()[:, 2 * p:2 * p + 1], aTo, ones_col[:], start=fw, stop=False)
                    t.matmul(pb[4].ap()[:, 64 * p:64 * p + 64], qTe,
                             S_sb[0:64, 66 * p:66 * p + 64], start=False, stop=True)
                    t.matmul(pb[0].ap()[:, 2 * p:2 * p + 1], qTe,
                             S_sb[0:64, 66 * p + 64:66 * p + 65], start=False, stop=True)
                    t.matmul(pb[5].ap()[:, 64 * p:64 * p + 64], qTo,
                             S_sb[64:128, 66 * p:66 * p + 64], start=False, stop=True)
                    t.matmul(pb[1].ap()[:, 2 * p:2 * p + 1], qTo,
                             S_sb[64:128, 66 * p + 64:66 * p + 65], start=False,
                             stop=True).then_inc(S["pe"], 1)
            emit("tensor", f)
            nd_pe[wv] = bump("pe", 4)
        last["nd_pe_prev"] = nd_pe[1]
        num_done = nd_pe[1]

        dmp = last.get("dump_sto", 0) if "praw_dump" in dbg_t else 0
        pta2 = last["prawT_act"]
        def f(v, num_done=num_done, dmp=dmp, pta2=pta2):
            v.wait_ge(S["pe"], num_done)
            v.wait_ge(S["sto"], dmp)
            v.wait_ge(S["act"], pta2)
            v.tensor_scalar_max(den_scr[:, 0:8], bass.AP(pb[0].ap().tensor, 0, [[512, 128], [2, 8]]), 1e-6)
            v.tensor_scalar_max(den_scr[:, 8:16], bass.AP(pb[1].ap().tensor, 0, [[512, 128], [2, 8]]), 1e-6)
            v.drain()
            v.reciprocal(rden_sb[:], den_scr[:])
            v.drain()
            v.engine_nop()
            v.drain().then_inc(S["dve"], 1)
        emit("vector", f)
        last["recip_dve"] = bump("dve")

        ptT = last.get("prawT_pe", 0)
        def f(v, ptT=ptT):
            v.wait_ge(S["pe"], ptT)
            rep_e = man_ap(rden_sb, 0, [[16, 128], [1, 8], [0, 64]])
            rep_o = man_ap(rden_sb, 8, [[16, 128], [1, 8], [0, 64]])
            dst_e = man_ap(praw_tok, 0, [[E, 128], [128, 8], [1, 64]])
            dst_o = man_ap(praw_tok, 64, [[E, 128], [128, 8], [1, 64]])
            v.tensor_tensor(dst_e, pb[4].ap().rearrange("p (a b) -> p a b", a=8), rep_e, ALU.mult)
            v.tensor_tensor(dst_o, pb[5].ap().rearrange("p (a b) -> p a b", a=8), rep_o,
                            ALU.mult).then_inc(S["dve"], 1)
        emit("vector", f)
        praw_done = bump("dve")
        last["praw_dve"] = praw_done

        if "praw_dump" in dbg_t:
            def f(g, i=i, t0g=t0g, praw_done=praw_done):
                g.wait_ge(S["dve"], praw_done)
                g.dma_start(out=dbg_t["praw_dump"][t0g + 128 * i:t0g + 128 * (i + 1), :],
                            in_=praw_tok[:]).then_inc(S["sto"], 16)
                g.dma_start(out=dbg_t["den_dump"][t0g + 128 * i:t0g + 128 * (i + 1), 0:16],
                            in_=den_scr[:]).then_inc(S["sto"], 16)
                g.dma_start(out=dbg_t["den_dump"][t0g + 128 * i:t0g + 128 * (i + 1), 16:32],
                            in_=rden_sb[:]).then_inc(S["sto"], 16)
                g.dma_start(out=dbg_t["S_dump"][t0g + 128 * i:t0g + 128 * (i + 1), :],
                            in_=S_sb[:]).then_inc(S["sto"], 16)
            emit("gpsimd", f)
            last["dump_sto"] = bump("sto", 64)

        if pend is not None:
            emit_pend_op(pend)

        return dict(i=i, t0g=t0g, kv=p2[i]["kv_dve"], silu_act=p2[i]["silu_act"],
                    praw_dve=praw_done)

    def emit_chunk_pre(i, p2):
        first = last["first_state"]
        kva = p2[i]["kv_dve"]
        def f(t, i=i, first=first, kva=kva):
            t.wait_ge(S["dve"], kva)
            for p in range(8):
                for par in range(2):
                    h = 2 * p + par
                    bank = pb[6] if p < 4 else pb[7]
                    r0 = 64 * par
                    c0 = 66 * (p % 4)
                    stf = first and (p in (0, 4))
                    t.matmul(bank.ap()[r0:r0 + 64, c0:c0 + 65],
                             ktok_sb[i][:, 64 * h:64 * h + 64],
                             vaug_sb[i][:, 65 * h:65 * h + 65],
                             start=stf, stop=False).then_inc(S["pe"], 1)
        emit("tensor", f)
        bump("pe", 16)
        last["first_state"] = False

    segs = (["pre"] if (prefix and dbg in ("full", "attn")) else []) + ["own"]
    for seg in segs:
        aug_done = set_aug(seg)
        xsrc = x_pre if seg == "pre" else x_own
        ngroups = NG if dbg in ("full", "attn") or seg == "pre" else (NG if dbg == "xn" else 1)
        if NG_OVR:
            ngroups = min(ngroups, NG_OVR)
        for g_i in range(ngroups):
            emit_group(seg, g_i, xsrc, aug_done)
            if seg == "pre" and "wt" not in SKIP:
                emit_wpart(2, pb01_dve=last["pb01_k_dve"],
                           pb01_act=last.get("xnT_act", 0))
                wT_act_done = cnt["act"]

    def f(sync, t0=cnt["sto0"], t1=cnt["sto1"], td=cnt["sto"]):
        sync.wait_ge(S["sto0"], t0)
        sync.wait_ge(S["sto1"], t1)
        if td:
            sync.wait_ge(S["sto"], td)
    emit("sync", f)

    if False:
        build_nc._mock = dict(prog=prog, sems=sems, ctx=ctx, cnt=dict(cnt))
        return nc, dbg_t
    with ctx, nc.Block() as block:
        @block.sync
        def _(e):
            for fn in prog["sync"]:
                fn(e)

        @block.gpsimd
        def _(e):
            for fn in prog["gpsimd"]:
                fn(e)

        @block.vector
        def _(e):
            for fn in prog["vector"]:
                fn(e)

        @block.scalar
        def _(e):
            for fn in prog["scalar"]:
                fn(e)

        @block.tensor
        def _(e):
            for fn in prog["tensor"]:
                fn(e)

    return nc, dbg_t



def kernel(L0_attn_x, L0_attn_W_qkv, L0_attn_W_out, L0_ffn_kernel, L0_ffn_bias,
           L0_ffn_a, L0_ffn_b):
    x = np.ascontiguousarray(np.asarray(L0_attn_x, dtype=np.float32))
    w_qkv = np.ascontiguousarray(np.asarray(L0_attn_W_qkv, dtype=np.float32))
    w_out = np.ascontiguousarray(np.asarray(L0_attn_W_out, dtype=np.float32))
    w_ffn = np.ascontiguousarray(np.asarray(L0_ffn_kernel, dtype=np.float32))
    b_ffn = np.ascontiguousarray(np.asarray(L0_ffn_bias, dtype=np.float32).reshape(1, E))
    a_lora = np.ascontiguousarray(np.asarray(L0_ffn_a, dtype=np.float32))
    b_lora = np.ascontiguousarray(np.asarray(L0_ffn_b, dtype=np.float32))

    mask = (np.triu(np.ones((128, 128), np.float32)) * 0.125)
    mask = np.ascontiguousarray(np.tile(mask, (1, 4)).astype(np.float32))
    ident = np.ascontiguousarray(np.eye(128).astype(ml_dtypes.bfloat16))
    zeros_pre = np.zeros((T, E), np.float32)

    nc, _ = build_nc(dbg="full", prefix=True)
    in_maps = []
    for c in range(8):
        b, h = c // 2, c % 2
        x_own = np.ascontiguousarray(x[b, 4096 * h:4096 * (h + 1), :])
        x_pre = np.ascontiguousarray(x[b, 0:4096, :]) if h == 1 else zeros_pre
        augc = np.full((128, 1), 1.0 if h == 1 else 0.0, np.float32)
        in_maps.append({
            "x_own": x_own, "x_pre": x_pre,
            "w_qkv": w_qkv, "w_out": w_out, "w_ffn": w_ffn,
            "b_ffn": b_ffn, "a_lora": a_lora, "b_lora": b_lora,
            "mask_in": mask, "ident_in": ident, "augc_in": augc,
        })
    res = run_bass_kernel_spmd(nc, in_maps, core_ids=list(range(8)))
    out = np.empty((B, L, E), np.float32)
    for c in range(8):
        b, h = c // 2, c % 2
        out[b, 4096 * h:4096 * (h + 1), :] = res.results[c]["out"]
    return out


# version 06
# speedup vs baseline: 1.0378x; 1.0017x over previous
import numpy as np
import ml_dtypes
import concourse.bass as bass
import concourse.mybir as mybir
from concourse.bass_utils import run_bass_kernel_spmd

F32 = mybir.dt.float32
BF16 = mybir.dt.bfloat16
I32 = mybir.dt.int32
AF = mybir.ActivationFunctionType
ALU = mybir.AluOpType
AX = mybir.AxisListType.X

B, L, E, H, D = 4, 8192, 1024, 16, 64
T = 4096
NS = 8
NG = T // 512
LN_EPS = 1e-5


def build_nc(dbg="full", prefix=True):
    SKIP = set()
    CHUNK_LVL = 9
    NG_OVR = 0
    nc = bass.Bass("TRN2", target_bir_lowering=False, detect_race_conditions=False)

    x_own = nc.dram_tensor("x_own", [T, E], F32, kind="ExternalInput").ap()
    x_pre = nc.dram_tensor("x_pre", [T, E], F32, kind="ExternalInput").ap()
    w_qkv = nc.dram_tensor("w_qkv", [3 * E, E], F32, kind="ExternalInput").ap()
    w_out = nc.dram_tensor("w_out", [E, E], F32, kind="ExternalInput").ap()
    w_ffn = nc.dram_tensor("w_ffn", [E, E], F32, kind="ExternalInput").ap()
    b_ffn = nc.dram_tensor("b_ffn", [1, E], F32, kind="ExternalInput").ap()
    a_lora = nc.dram_tensor("a_lora", [E, 16], F32, kind="ExternalInput").ap()
    b_lora = nc.dram_tensor("b_lora", [16, E], F32, kind="ExternalInput").ap()
    mask_in = nc.dram_tensor("mask_in", [128, 512], F32, kind="ExternalInput").ap()
    ident_in = nc.dram_tensor("ident_in", [128, 128], BF16, kind="ExternalInput").ap()
    augc_in = nc.dram_tensor("augc_in", [128, 1], F32, kind="ExternalInput").ap()
    out = nc.dram_tensor("out", [T, E], F32, kind="ExternalOutput").ap()

    dbg_t = {}
    if dbg == "xn":
        dbg_t["xn_dump"] = nc.dram_tensor("xn_dump", [T, E], F32, kind="ExternalOutput").ap()
    if dbg == "qkv":
        dbg_t["q_dump"] = nc.dram_tensor("q_dump", [E, 512], F32, kind="ExternalOutput").ap()
        dbg_t["k_dump"] = nc.dram_tensor("k_dump", [E, 512], F32, kind="ExternalOutput").ap()
        dbg_t["v_dump"] = nc.dram_tensor("v_dump", [512, 65 * H], F32, kind="ExternalOutput").ap()
        dbg_t["kt_dump"] = nc.dram_tensor("kt_dump", [512, E], F32, kind="ExternalOutput").ap()
        dbg_t["g_dump"] = nc.dram_tensor("g_dump", [512, E], F32, kind="ExternalOutput").ap()
    if dbg == "attn":
        dbg_t["praw_dump"] = nc.dram_tensor("praw_dump", [T, E], F32, kind="ExternalOutput").ap()
        dbg_t["den_dump"] = nc.dram_tensor("den_dump", [T, 32], F32, kind="ExternalOutput").ap()
        dbg_t["S_dump"] = nc.dram_tensor("S_dump", [T, 520], F32, kind="ExternalOutput").ap()

    from contextlib import ExitStack
    ctx = ExitStack()
    sbt = lambda n, s, d: ctx.enter_context(nc.sbuf_tensor(n, s, d))

    wqkvT_all = sbt("wqkvT_all", [128, 3 * E * NS], BF16)
    wqkvT = [wqkvT_all[:, 3 * E * s:3 * E * (s + 1)] for s in range(NS)]
    woutT_all = sbt("woutT_all", [128, E * NS], BF16)
    woutT = [woutT_all[:, E * s:E * (s + 1)] for s in range(NS)]
    wq_ffn = [sbt(f"wqffn{s}", [128, E], BF16) for s in range(NS)]
    a_sb = sbt("a_sb", [128, 16 * NS], BF16)
    b_sb = sbt("b_sb", [16, E], BF16)
    bias_sb = sbt("bias_sb", [1, E], BF16)
    mask_sb = sbt("mask_sb", [128, 512], F32)
    ident_sb = sbt("ident_sb", [128, 128], BF16)
    augc_sb = sbt("augc_sb", [128, 1], F32)
    ones_col = sbt("ones_col", [128, 1], BF16)
    ones_row = sbt("ones_row", [1, 128], BF16)

    wstage = [sbt(f"wstage{i}", [128, E], BF16) for i in range(2)]
    qscr = sbt("qscr", [128, E], F32)
    qscr_i = sbt("qscr_i", [128, E], I32)
    amax_c = sbt("amax_c", [128, NS + 2], F32)
    amax_r = sbt("amax_r", [1, 128], F32)
    scale_c = sbt("scale_c", [128, 2], F32)

    x_sb = [sbt(f"x_sb{i}", [128, E], F32) for i in range(4)]
    xn_sb = [sbt(f"xn_sb{i}", [128, E], BF16) for i in range(4)]
    xnT_all = sbt("xnT_all", [128, 512 * NS], BF16)
    xnT_sb = [xnT_all[:, 512 * s:512 * (s + 1)] for s in range(NS)]
    qT_sb = [sbt(f"qT{s}", [128, 512], BF16) for s in range(NS)]
    kT_sb = [sbt(f"kT{s}", [128, 512], BF16) for s in range(NS)]
    ktok_sb = [sbt(f"ktok{i}", [128, E], BF16) for i in range(4)]
    vaug_sb = [sbt(f"vaug{i}", [128, 65 * H], BF16) for i in range(4)]
    e8_sb = sbt("e8_sb", [128, 1024], F32)
    em_sb = sbt("em_sb", [128, 512], F32)
    stat_sum = sbt("stat_sum", [128, 4], F32)
    stat_sq = sbt("stat_sq", [128, 4], F32)
    stat_m = sbt("stat_m", [128, 4], F32)
    stat_v = sbt("stat_v", [128, 4], F32)
    stat_y = sbt("stat_y", [128, 4], F32)
    stat_t = sbt("stat_t", [128, 4], F32)
    ln_bias = sbt("ln_bias", [128, 4], F32)
    ln_scale = sbt("ln_scale", [128, 4], F32)
    lnscr = qscr

    attnT_sb = [sbt(f"attnT{w}", [128, 512], BF16) for w in range(4)]
    S_sb = sbt("S_sb", [128, 66 * 8], BF16)
    den_scr = sbt("den_scr", [128, 16], F32)
    rden_sb = sbt("rden_sb", [128, 16], F32)
    praw_tok = sbt("praw_tok", [128, E], BF16)
    praw_fm_all = sbt("praw_fm_all", [128, 1024], BF16)
    u_sb = sbt("u_sb", [16, 512], BF16)
    g_sb = [sbt(f"g_sb{i}", [128, E], BF16) for i in range(4)]
    sg_sb = [sbt(f"sg_sb{i}", [128, E], F32) for i in range(2)]
    out_sb = [sbt(f"out_sb{i}", [128, E], F32) for i in range(2)]
    kf32 = [x_sb[0], x_sb[1], x_sb[2], x_sb[3], sg_sb[0], sg_sb[1], out_sb[0], out_sb[1]]

    pb = [ctx.enter_context(nc.psum_tensor(f"pb{i}", [128, 512], F32)) for i in range(8)]
    pbv = [p.ap().bitcast(BF16) for p in pb]

    sems = {}
    cnt = {}
    for name in ["ldc", "ldk", "lab", "ldw0", "ldw1",
                 "ldx0", "ldx1", "ldx2", "ldx3",
                 "pe", "dve", "act", "gp", "sto", "sto0", "sto1"]:
        sems[name] = ctx.enter_context(nc.semaphore(name))
        cnt[name] = 0
    S = sems

    prog = {k: [] for k in ["sync", "gpsimd", "vector", "scalar", "tensor"]}
    SERIAL = ""

    _eng_sem = {"vector": "dve", "scalar": "act", "tensor": "pe"}

    def emit(eng, fn):
        if False and SERIAL and eng in ("vector", "scalar", "tensor"):
            waits = [(nm, cnt[nm]) for nm in ("pe", "dve", "act")
                     if nm != _eng_sem[eng] and cnt[nm] > 0]
            def wrapped(e, fn=fn, waits=waits):
                for nm, vv in waits:
                    e.wait_ge(S[nm], vv)
                fn(e)
            prog[eng].append(wrapped)
        else:
            prog[eng].append(fn)

    def bump(sem, n=1):
        cnt[sem] += n
        return cnt[sem]

    def ap_of(t):
        return t[:]

    def man_ap(tile, off, dims):
        return bass.AP(tile[:].tensor, off, dims)

    def f(sync):
        sync.dma_start(out=mask_sb[:], in_=mask_in[:]).then_inc(S["ldc"], 16)
        sync.dma_start(out=ident_sb[:], in_=ident_in[:]).then_inc(S["ldc"], 16)
        sync.dma_start(out=augc_sb[:], in_=augc_in[:]).then_inc(S["ldc"], 16)
        for s in range(NS):
            sync.dma_start(out=kf32[s][:], in_=w_ffn[128 * s:128 * (s + 1), :]).then_inc(S["ldk"], 16)
    emit("sync", f)
    const_done = bump("ldc", 48)
    kf32_done = bump("ldk", 16 * NS)

    def f(g):
        for s in range(NS):
            g.dma_start(out=a_sb[:, 16 * s:16 * (s + 1)],
                        in_=a_lora[128 * s:128 * (s + 1), :]).then_inc(S["lab"], 16)
        g.dma_start(out=b_sb[:], in_=b_lora[:]).then_inc(S["lab"], 16)
        g.dma_start(out=bias_sb[:], in_=b_ffn[:]).then_inc(S["lab"], 16)
    emit("gpsimd", f)
    ab_done = bump("lab", 16 * (NS + 2))

    raw_tiles = ([("qkv", i) for i in range(8, 24)]
                 + [("qkv", i) for i in range(8)]
                 + [("out", i) for i in range(8)])
    wtrans_pe_mark = []
    wevac_act_mark = []
    wnext = [0]

    def emit_wpart(n, pb01_dve=0, pb01_act=0):
        j0 = wnext[0]
        j1 = min(j0 + n, len(raw_tiles))
        wnext[0] = j1
        if j1 == j0:
            return
        marks = list(wtrans_pe_mark)
        peA_base = cnt["pe"]
        def fd(g, j0=j0, j1=j1, marks=marks, peA_base=peA_base):
            for j in range(j0, j1):
                if j >= 2:
                    if j - 2 < j0:
                        g.wait_ge(S["pe"], marks[j - 2])
                    else:
                        g.wait_ge(S["pe"], peA_base + 8 * (j - 1 - j0))
                kind, i = raw_tiles[j]
                src = w_qkv[128 * i:128 * (i + 1), :] if kind == "qkv" \
                    else w_out[128 * i:128 * (i + 1), :]
                g.dma_start(out=wstage[j % 2][:], in_=src).then_inc(S["ldw" + str(j % 2)], 16)
        emit("gpsimd", fd)
        for j in range(j0, j1):
            bump("ldw" + str(j % 2), 16)
        emarks = list(wevac_act_mark)
        actA_base = cnt["act"]
        def ft(t, j0=j0, j1=j1, emarks=emarks, pb01_dve=pb01_dve, pb01_act=pb01_act,
               const_done=const_done, actA_base=actA_base):
            if j0 == 0:
                t.wait_ge(S["ldc"], const_done)
            if pb01_dve:
                t.wait_ge(S["dve"], pb01_dve)
            if pb01_act:
                t.wait_ge(S["act"], pb01_act)
            for j in range(j0, j1):
                t.wait_ge(S["ldw" + str(j % 2)], 16 * (j // 2 + 1))
                if j >= 2:
                    if j - 2 < len(emarks):
                        t.wait_ge(S["act"], emarks[j - 2])
                    else:
                        t.wait_ge(S["act"], actA_base + 8 * (j - 1 - j0))
                bv = pbv[2 + j % 2]
                for s in range(NS):
                    t.transpose(bv[:, 128 * s:128 * (s + 1)], wstage[j % 2][:, 128 * s:128 * (s + 1)],
                                ident_sb[:]).then_inc(S["pe"], 1)
        emit("tensor", ft)
        for j in range(j0, j1):
            wtrans_pe_mark.append(bump("pe", 8))
        def fe(a, j0=j0, j1=j1, pmarks=list(wtrans_pe_mark)):
            for j in range(j0, j1):
                kind, i = raw_tiles[j]
                bv = pbv[2 + j % 2]
                a.wait_ge(S["pe"], pmarks[j])
                for s in range(NS):
                    dst = (wqkvT[s] if kind == "qkv" else woutT[s])[:, 128 * i:128 * (i + 1)]
                    a.copy(dst, bv[:, 128 * s:128 * (s + 1)]).then_inc(S["act"], 1)
        emit("scalar", fe)
        for j in range(j0, j1):
            wevac_act_mark.append(bump("act", 8))

    NA = 16
    has_pre = prefix and dbg in ("full", "attn")
    if "wt" not in SKIP:
        emit_wpart(NA)
        actA_done = cnt["act"]
        if not has_pre:
            emit_wpart(len(raw_tiles) - NA)
    else:
        actA_done = cnt["act"]
    wT_act_done = cnt["act"]

    if "quant" in SKIP:
        def f(v):
            v.memset(wq_ffn[0][:], 0.0)
            v.memset(ones_col[:], 1.0)
            v.memset(ones_row[:], 1.0).then_inc(S["dve"], 1)
        emit("vector", f)
        quant_done = bump("dve")
    if "quant" not in SKIP:
      def f(v, kf32_done=kf32_done):
        v.wait_ge(S["ldk"], kf32_done)
        for s in range(NS):
            v.reduce_max(amax_c[:, s:s + 1], kf32[s][:], axis=AX, apply_absolute_value=True)
        v.drain()
        v.reduce_max(amax_c[:, NS:NS + 1], amax_c[:, 0:NS], axis=AX).then_inc(S["dve"], 1)
      emit("vector", f)
      q1 = bump("dve")

    if "quant" not in SKIP:
      def f(g, q1=q1):
        g.wait_ge(S["dve"], q1)
        g.dma_start(out=amax_r[0:1, :], in_=amax_c[:, NS:NS + 1]).then_inc(S["gp"], 16)
      emit("gpsimd", f)
      q2 = bump("gp", 16)

      def f(v, q2=q2):
        v.wait_ge(S["gp"], q2)
        v.reduce_max(amax_r[0:1, 0:1], amax_r[0:1, :], axis=AX).then_inc(S["dve"], 1)
      emit("vector", f)
      q3 = bump("dve")

      def f(g, q3=q3):
        g.wait_ge(S["dve"], q3)
        g.dma_start(out=amax_c[:, NS + 1:NS + 2],
                    in_=man_ap(amax_r, 0, [[128, 1], [0, 128]])).then_inc(S["gp"], 16)
      emit("gpsimd", f)
      q4 = bump("gp", 16)

      def f(v, q4=q4):
        v.wait_ge(S["gp"], q4)
        am = amax_c[:, NS + 1:NS + 2]
        v.tensor_scalar(scale_c[:, 1:2], am, 1.0 / 7.0, None, ALU.mult)
        v.drain()
        v.reciprocal(scale_c[:, 0:1], scale_c[:, 1:2])
        v.drain()
        for s in range(NS):
            v.tensor_scalar(qscr[:], kf32[s][:], scale_c[:, 0:1], None, ALU.mult)
            v.tensor_copy(qscr_i[:], qscr[:])
            v.tensor_copy(qscr[:], qscr_i[:])
            v.tensor_scalar(wq_ffn[s][:], qscr[:], scale_c[:, 1:2], None, ALU.mult)
        v.memset(ones_col[:], 1.0)
        v.memset(ones_row[:], 1.0).then_inc(S["dve"], 1)
      emit("vector", f)
      quant_done = bump("dve")

    last = dict(
        x_free=[quant_done] * 4,
        x_free_act=[0, 0, 0, 0],
        gate_dve=[0, 0],
        out_dma=[0, 0],
        Ssb_dve=0,
        first_state=True,
        recip_dve=0,
        praw_dve=0,
        prawT_act=0,
        pb23_free_dve=0,
        pb01_k_dve=0,
        silu_act_last=0,
    )

    def set_aug(seg):
        if seg == "own":
            pe_now = cnt["pe"]
            def f(v, pe_now=pe_now):
                v.wait_ge(S["pe"], pe_now)
                for i in range(4):
                    v.memset(man_ap(vaug_sb[i], 64, [[65 * H, 128], [65, H]]), 1.0)
                v.engine_nop().then_inc(S["dve"], 1)
            emit("vector", f)
        else:
            def f(v, const_done=const_done):
                v.wait_ge(S["ldc"], const_done)
                for i in range(4):
                    v.tensor_copy(man_ap(vaug_sb[i], 64, [[65 * H, 128], [65, H]]),
                                  augc_sb[:, 0:1].broadcast_to([128, H]))
                v.engine_nop().then_inc(S["dve"], 1)
            emit("vector", f)
        return bump("dve")

    def f(v):
        v.memset(S_sb[:], 0.0).then_inc(S["dve"], 1)
    emit("vector", f)
    last["Ssb_dve"] = bump("dve")

    def emit_group(seg, g_i, x_src, aug_done):
        own = seg == "own"
        t0g = 512 * g_i

        xf = list(last["x_free"])
        xfa = list(last["x_free_act"])
        xfg = list(last.get("x_free_gp", [0, 0, 0, 0]))
        def f(sync, xf=xf, xfa=xfa, xfg=xfg, x_src=x_src, t0g=t0g):
            for i in range(4):
                sync.wait_ge(S["dve"], xf[i])
                sync.wait_ge(S["act"], xfa[i])
                sync.wait_ge(S["gp"], xfg[i])
                sync.dma_start(out=x_sb[i][:], in_=x_src[t0g + 128 * i:t0g + 128 * (i + 1), :]
                               ).then_inc(S["ldx" + str(i)], 16)
        emit("sync", f)
        xv = [bump("ldx" + str(i), 16) for i in range(4)]

        sq_base = cnt["act"]
        def f(a, xv=xv):
            for i in range(4):
                a.wait_ge(S["ldx" + str(i)], xv[i])
                a.activation(lnscr[:], x_sb[i][:], AF.Square,
                             accum_out=stat_sq[:, i:i + 1]).then_inc(S["act"], 1)
        emit("scalar", f)
        bump("act", 4)

        def f(v, xv=xv, sq_base=sq_base):
            for i in range(4):
                v.wait_ge(S["ldx" + str(i)], xv[i])
                v.reduce_sum(stat_sum[:, i:i + 1], x_sb[i][:], axis=AX)
            v.wait_ge(S["act"], sq_base + 4)
            v.drain()
            v.tensor_scalar(stat_m[:], stat_sum[:], 1.0 / E, None, ALU.mult)
            v.tensor_scalar(stat_v[:], stat_sq[:], 1.0 / E, None, ALU.mult)
            v.drain()
            v.tensor_tensor(stat_t[:], stat_m[:], stat_m[:], ALU.mult)
            v.drain()
            v.tensor_tensor(stat_v[:], stat_v[:], stat_t[:], ALU.subtract)
            v.drain()
            v.tensor_scalar(stat_v[:], stat_v[:], LN_EPS, None, ALU.add)
            v.drain()
            v.engine_nop().then_inc(S["dve"], 1)
        emit("vector", f)
        var_done = bump("dve")

        def f(a, var_done=var_done):
            a.wait_ge(S["dve"], var_done)
            a.activation(stat_t[:], stat_v[:], AF.Sqrt).then_inc(S["act"], 1)
        emit("scalar", f)
        sqrt_done = bump("act")

        def f(v, sqrt_done=sqrt_done):
            v.wait_ge(S["act"], sqrt_done)
            v.reciprocal(stat_y[:], stat_t[:])
            v.drain()
            v.tensor_tensor(stat_t[:], stat_y[:], stat_y[:], ALU.mult)
            v.drain()
            v.tensor_tensor(stat_t[:], stat_t[:], stat_v[:], ALU.mult)
            v.drain()
            v.tensor_scalar(stat_t[:], stat_t[:], -0.5, 1.5, ALU.mult, ALU.add)
            v.drain()
            v.tensor_tensor(ln_scale[:], stat_y[:], stat_t[:], ALU.mult)
            v.drain()
            v.tensor_tensor(ln_bias[:], stat_m[:], ln_scale[:], ALU.mult)
            v.drain()
            v.tensor_scalar(ln_bias[:], ln_bias[:], -1.0, None, ALU.mult)
            v.drain()
            v.engine_nop().then_inc(S["dve"], 1)
        emit("vector", f)
        inv_done = bump("dve")

        xn_base = cnt["act"]
        def f(a, inv_done=inv_done):
            a.wait_ge(S["dve"], inv_done)
            for i in range(4):
                a.activation(xn_sb[i][:], x_sb[i][:], AF.Identity,
                             bias=ln_bias[:, i:i + 1], scale=ln_scale[:, i:i + 1]).then_inc(S["act"], 1)
        emit("scalar", f)
        bump("act", 4)
        for i in range(4):
            last["x_free_act"][i] = xn_base + i + 1

        if own:
            def f(v, xv3=xv[3]):
                v.wait_ge(S["ldx3"], xv3)
                v.tensor_copy(qscr_i[:].bitcast(F32), x_sb[3][:]).then_inc(S["dve"], 1)
            emit("vector", f)
            last["x_free"][3] = bump("dve")

        if dbg == "xn":
            def f(g, xn_base=xn_base, t0g=t0g):
                for i in range(4):
                    g.wait_ge(S["act"], xn_base + i + 1)
                    g.dma_start(out=dbg_t["xn_dump"][t0g + 128 * i:t0g + 128 * (i + 1), :],
                                in_=xn_sb[i][:]).then_inc(S["sto"], 16)
            emit("gpsimd", f)
            dump_done = bump("sto", 64)
            def f(v, xn_base=xn_base, dump_done=dump_done):
                v.wait_ge(S["act"], xn_base + 4)
                v.wait_ge(S["sto"], dump_done)
                v.engine_nop().then_inc(S["dve"], 1)
            emit("vector", f)
            mk = bump("dve")
            for i in range(4):
                last["x_free"][i] = mk
            return

        xnT_pe_base = cnt["pe"]
        gmax = max(last["gate_dve"])
        k01 = last["pb01_k_dve"]
        wneed = wT_act_done if own else actA_done
        def f(t, xn_base=xn_base, xnT_pe_base=xnT_pe_base, gmax=gmax, k01=k01,
              wneed=wneed):
            t.wait_ge(S["act"], wneed)
            t.wait_ge(S["dve"], max(gmax, k01))
            for i in range(4):
                t.wait_ge(S["act"], xn_base + i + 1)
                bv = pbv[i % 2]
                if i >= 2:
                    t.wait_ge(S["act"], xnT_pe_base_act[0] + 8 * (i - 1))
                for s in range(NS):
                    t.transpose(bv[:, 128 * s:128 * (s + 1)], xn_sb[i][:, 128 * s:128 * (s + 1)],
                                ident_sb[:]).then_inc(S["pe"], 1)
        emit("tensor", f)
        bump("pe", 32)

        xnT_pe_base_act = [cnt["act"]]
        def f(a, xnT_pe_base=xnT_pe_base):
            for i in range(4):
                bv = pbv[i % 2]
                a.wait_ge(S["pe"], xnT_pe_base + 8 * (i + 1))
                dst = man_ap(xnT_all, 128 * i, [[512 * NS, 128], [512, 8], [1, 128]])
                a.copy(dst, bv[:, 0:1024]).then_inc(S["act"], 8)
        emit("scalar", f)
        xnT_done = bump("act", 32)
        last["xnT_act"] = xnT_done

        qkT_done = 0
        u_done = 0
        if own:
            p1_pe_base = cnt["pe"]
            p1_stt_base_box = [0]
            def f(t, xnT_done=xnT_done, p1_stt_base_box=p1_stt_base_box):
                t.wait_ge(S["act"], xnT_done)
                for ft in range(16):
                    bank = pb[ft % 2]
                    if ft >= 2:
                        t.wait_ge(S["dve"], p1_stt_base_box[0] + (ft - 1))
                    for s in range(NS):
                        t.matmul(bank.ap(), wqkvT[s][:, 128 * ft:128 * (ft + 1)],
                                 xnT_sb[s][:], start=(s == 0), stop=(s == NS - 1)).then_inc(S["pe"], 1)
            emit("tensor", f)
            bump("pe", 16 * NS)

            p1_exp_base = cnt["act"]
            def f(a, p1_pe_base=p1_pe_base, p1_stt_base_box=p1_stt_base_box):
                for ft in range(16):
                    bank = pb[ft % 2]
                    a.wait_ge(S["pe"], p1_pe_base + NS * (ft + 1))
                    if ft >= 2:
                        a.wait_ge(S["dve"], p1_stt_base_box[0] + (ft - 1))
                    a.activation(e8_sb[:, 512 * (ft % 2):512 * (ft % 2 + 1)], bank.ap(),
                                 AF.Exp).then_inc(S["act"], 1)
            emit("scalar", f)
            bump("act", 16)

            p1_stt_base = cnt["dve"]
            p1_stt_base_box[0] = p1_stt_base
            def f(v, p1_exp_base=p1_exp_base):
                for ft in range(16):
                    bank = pb[ft % 2]
                    dstT = (qT_sb if ft < 8 else kT_sb)[ft % 8]
                    v.wait_ge(S["act"], p1_exp_base + ft + 1)
                    eh = e8_sb[:, 512 * (ft % 2):512 * (ft % 2 + 1)]
                    v.tensor_scalar_min(em_sb[:], eh, 1.0)
                    v.scalar_tensor_tensor(dstT[:], bank.ap(), 0.0, em_sb[:],
                                           ALU.max, ALU.add).then_inc(S["dve"], 1)
            emit("vector", f)
            qkT_done = bump("dve", 16)

            u_pe_base = cnt["pe"]
            pf23 = max(last["pb23_free_dve"], last.get("mask_dve_last", 0))
            pf23a = last["prawT_act"]
            def f(t, ab_done=ab_done, pf23=pf23, pf23a=pf23a, qkT_done=qkT_done):
                t.wait_ge(S["lab"], ab_done)
                t.wait_ge(S["dve"], pf23)
                t.wait_ge(S["act"], pf23a)
                for s in range(NS):
                    t.matmul(pb[2].ap()[0:16, :], a_sb[:, 16 * s:16 * (s + 1)], xnT_sb[s][:],
                             start=(s == 0), stop=(s == NS - 1)).then_inc(S["pe"], 1)
            emit("tensor", f)
            bump("pe", NS)

            def f(v, u_pe_base=u_pe_base):
                v.wait_ge(S["pe"], u_pe_base + NS)
                v.tensor_copy(u_sb[:], pb[2].ap()[0:16, :]).then_inc(S["dve"], 1)
            emit("vector", f)
            u_done = bump("dve")

        p2 = []
        for i in range(4):
            prev_kv = p2[i - 1]["kv_dve"] if i else 0
            prev_silu = p2[i - 1]["silu_act"] if (i and own) else 0
            gmax2 = max(last["gate_dve"])
            k01b = last["pb01_k_dve"]
            pf23 = last["pb23_free_dve"]
            pf23a = last["prawT_act"]
            md_last = last.get("mask_dve_last", 0)
            sil_last = last["silu_act_last"]
            prw_last = last["praw_dve"]
            def f(t, i=i, own=own, qkT_done=qkT_done, u_done=u_done, prev_kv=prev_kv,
                  prev_silu=prev_silu, gmax2=gmax2, k01b=k01b, pf23=pf23, pf23a=pf23a,
                  xnT_done=xnT_done, quant_done=quant_done, sil_last=sil_last,
                  prw_last=prw_last, md_last=md_last):
                t.wait_ge(S["act"], xnT_done)
                t.wait_ge(S["dve"], quant_done)
                if own:
                    t.wait_ge(S["dve"], qkT_done)
                    t.wait_ge(S["dve"], u_done)
                else:
                    t.wait_ge(S["dve"], max(gmax2, k01b))
                t.wait_ge(S["dve"], pf23)
                t.wait_ge(S["act"], pf23a)
                t.wait_ge(S["dve"], md_last)
                if i > 0:
                    t.wait_ge(S["dve"], prev_kv)
                    if own:
                        t.wait_ge(S["act"], prev_silu)
                if own and i == 0:
                    t.wait_ge(S["act"], sil_last)
                    t.wait_ge(S["dve"], prw_last)
                for s in range(NS):
                    st = (s == 0)
                    sp = (s == NS - 1)
                    xt = xnT_sb[s][:, 128 * i:128 * (i + 1)]
                    t.matmul(pb[0].ap(), xt, wqkvT[s][:, E:E + 512], start=st, stop=sp)
                    t.matmul(pb[1].ap(), xt, wqkvT[s][:, E + 512:2 * E], start=st, stop=sp)
                    t.matmul(pb[2].ap(), xt, wqkvT[s][:, 2 * E:2 * E + 512], start=st, stop=sp)
                    mmi = t.matmul(pb[3].ap(), xt, wqkvT[s][:, 2 * E + 512:3 * E], start=st, stop=sp)
                    if own:
                        t.matmul(pb[4].ap(), xt, wq_ffn[s][:, 0:512], start=st, stop=False)
                        t.matmul(pb[5].ap(), xt, wq_ffn[s][:, 512:1024], start=st, stop=False)
                if own:
                    t.matmul(pb[4].ap(), u_sb[:, 128 * i:128 * (i + 1)], b_sb[:, 0:512],
                             start=False, stop=False)
                    t.matmul(pb[5].ap(), u_sb[:, 128 * i:128 * (i + 1)], b_sb[:, 512:1024],
                             start=False, stop=False)
                    t.matmul(pb[4].ap(), ones_row[:], bias_sb[:, 0:512], start=False, stop=True)
                    mmi = t.matmul(pb[5].ap(), ones_row[:], bias_sb[:, 512:1024],
                                   start=False, stop=True)
                mmi.then_inc(S["pe"], 1)
            emit("tensor", f)
            p2_pe_done = bump("pe")

            kexp_base = cnt["act"]
            def f(a, p2_pe_done=p2_pe_done, qkT_done=qkT_done):
                a.wait_ge(S["pe"], p2_pe_done)
                a.wait_ge(S["dve"], qkT_done)
                a.activation(e8_sb[:, 0:512], pb[0].ap(), AF.Exp).then_inc(S["act"], 1)
                a.activation(e8_sb[:, 512:1024], pb[1].ap(), AF.Exp).then_inc(S["act"], 1)
            emit("scalar", f)
            bump("act", 2)

            def f(v, i=i, kexp_base=kexp_base, p2_pe_done=p2_pe_done, aug_done=aug_done):
                v.wait_ge(S["act"], kexp_base + 1)
                v.tensor_scalar_min(em_sb[:], e8_sb[:, 0:512], 1.0)
                v.scalar_tensor_tensor(ktok_sb[i][:, 0:512], pb[0].ap(), 0.0, em_sb[:],
                                       ALU.max, ALU.add)
                v.wait_ge(S["act"], kexp_base + 2)
                v.tensor_scalar_min(em_sb[:], e8_sb[:, 512:1024], 1.0)
                v.scalar_tensor_tensor(ktok_sb[i][:, 512:1024], pb[1].ap(), 0.0, em_sb[:],
                                       ALU.max, ALU.add)
                v.wait_ge(S["dve"], aug_done)
                v.tensor_copy(man_ap(vaug_sb[i], 0, [[65 * H, 128], [65, 8], [1, 64]]),
                              pb[2].ap().rearrange("p (h e) -> p h e", h=8))
                v.tensor_copy(man_ap(vaug_sb[i], 65 * 8, [[65 * H, 128], [65, 8], [1, 64]]),
                              pb[3].ap().rearrange("p (h e) -> p h e", h=8)).then_inc(S["dve"], 1)
            emit("vector", f)
            kv_dve = bump("dve")

            silu_act = 0
            if own:
                def f(a, i=i, p2_pe_done=p2_pe_done):
                    a.wait_ge(S["pe"], p2_pe_done)
                    a.activation(g_sb[i][:, 0:512], pb[4].ap(), AF.Silu).then_inc(S["act"], 1)
                    a.activation(g_sb[i][:, 512:1024], pb[5].ap(), AF.Silu).then_inc(S["act"], 1)
                emit("scalar", f)
                silu_act = bump("act", 2)
            p2.append(dict(kv_dve=kv_dve, silu_act=silu_act))

        last["pb01_k_dve"] = p2[3]["kv_dve"]
        last["pb23_free_dve"] = p2[3]["kv_dve"]
        if own:
            last["silu_act_last"] = p2[3]["silu_act"]

        if dbg == "qkv":
            def f(g, p2=p2, qkT_done=qkT_done):
                for i in range(4):
                    g.wait_ge(S["dve"], p2[i]["kv_dve"])
                    g.dma_start(out=dbg_t["kt_dump"][128 * i:128 * (i + 1), :],
                                in_=ktok_sb[i][:]).then_inc(S["sto"], 16)
                    g.dma_start(out=dbg_t["v_dump"][128 * i:128 * (i + 1), :],
                                in_=vaug_sb[i][:]).then_inc(S["sto"], 16)
                    g.wait_ge(S["act"], p2[i]["silu_act"])
                    g.dma_start(out=dbg_t["g_dump"][128 * i:128 * (i + 1), :],
                                in_=g_sb[i][:]).then_inc(S["sto"], 16)
                g.wait_ge(S["dve"], qkT_done)
                for s in range(NS):
                    g.dma_start(out=dbg_t["q_dump"][128 * s:128 * (s + 1), :],
                                in_=qT_sb[s][:]).then_inc(S["sto"], 16)
                    g.dma_start(out=dbg_t["k_dump"][128 * s:128 * (s + 1), :],
                                in_=kT_sb[s][:]).then_inc(S["sto"], 16)
            emit("gpsimd", f)
            bump("sto", 16 * 28)
            for i in range(4):
                last["x_free"][i] = cnt["dve"]
            return

        if own:
            ROT = True
            pend = None
            for i in range(4):
                if ROT:
                    pend = emit_chunk_own(g_i, i, t0g, p2, qkT_done, pend)
                else:
                    p_ = emit_chunk_own(g_i, i, t0g, p2, qkT_done, None)
                    emit_pend_state(p_)
                    emit_pend_prawT(p_)
                    emit_pend_op(p_)
            if ROT:
                emit_pend_state(pend)
                emit_pend_prawT(pend)
                emit_pend_op(pend)
        else:
            for i in range(4):
                emit_chunk_pre(i, p2)

        if not own:
            for i in range(4):
                last["x_free"][i] = var_done
            if g_i == NG - 1:
                sp = cnt["pe"]
                def f(v, sp=sp):
                    v.wait_ge(S["pe"], sp)
                    v.tensor_scalar(S_sb[:, 0:264], pb[6].ap()[:, 0:264], 0.125, None, ALU.mult)
                    v.tensor_scalar(S_sb[:, 264:528], pb[7].ap()[:, 0:264], 0.125, None,
                                    ALU.mult).then_inc(S["dve"], 1)
                emit("vector", f)
                last["Ssb_dve"] = bump("dve")

    def emit_pend_state(pend):
        pi = pend["i"]
        first = last["first_state"]
        kva = pend["kv"]
        ssb_prev = last["Ssb_dve"]
        def f(t, pi=pi, first=first, kva=kva, ssb_prev=ssb_prev):
            t.wait_ge(S["dve"], kva)
            t.wait_ge(S["dve"], ssb_prev)
            for p in range(8):
                for par in range(2):
                    h = 2 * p + par
                    bank = pb[6] if p < 4 else pb[7]
                    r0 = 64 * par
                    c0 = 66 * (p % 4)
                    stf = first and (p in (0, 4))
                    t.matmul(bank.ap()[r0:r0 + 64, c0:c0 + 65],
                             ktok_sb[pi][:, 64 * h:64 * h + 64],
                             vaug_sb[pi][:, 65 * h:65 * h + 65],
                             start=stf, stop=False).then_inc(S["pe"], 1)
        emit("tensor", f)
        state_done = bump("pe", 16)
        last["first_state"] = False

        dmp = last.get("dump_sto", 0) if "praw_dump" in dbg_t else 0
        def f(v, state_done=state_done, dmp=dmp):
            v.wait_ge(S["pe"], state_done)
            v.wait_ge(S["sto"], dmp)
            v.tensor_scalar(S_sb[:, 0:264], pb[6].ap()[:, 0:264], 0.125, None, ALU.mult)
            v.tensor_scalar(S_sb[:, 264:528], pb[7].ap()[:, 0:264], 0.125, None,
                            ALU.mult).then_inc(S["dve"], 1)
        emit("vector", f)
        last["Ssb_dve"] = bump("dve")

    def emit_pend_prawT(pend):
        prawT_base = cnt["pe"]
        pd = pend["praw_dve"]
        gm = max(last["gate_dve"])
        def f(t, prawT_base=prawT_base, pd=pd, gm=gm):
            t.wait_ge(S["dve"], pd)
            t.wait_ge(S["dve"], gm)
            for sl in range(NS):
                bv = pbv[sl % 2]
                t.transpose(bv[:, 512 + 128 * (sl // 2):512 + 128 * (sl // 2) + 128],
                            praw_tok[:, 128 * sl:128 * (sl + 1)],
                            ident_sb[:]).then_inc(S["pe"], 1)
        emit("tensor", f)
        prawT_pe = bump("pe", 8)
        last["prawT_pe"] = prawT_pe

        def f(a, prawT_base=prawT_base):
            a.wait_ge(S["pe"], prawT_base + 8)
            a.copy(man_ap(praw_fm_all, 0, [[1024, 128], [256, 4], [1, 128]]),
                   pbv[0][:, 512:1024]).then_inc(S["act"], 4)
            a.copy(man_ap(praw_fm_all, 128, [[1024, 128], [256, 4], [1, 128]]),
                   pbv[1][:, 512:1024]).then_inc(S["act"], 4)
        emit("scalar", f)
        last["prawT_act"] = bump("act", 8)

    def emit_pend_op(pend):
        pi = pend["i"]
        t0g = pend["t0g"]
        prawT_act = last["prawT_act"]
        rec = last["recip_dve"]
        gm = max(last["gate_dve"])
        def f(t, prawT_act=prawT_act, rec=rec, gm=gm):
            t.wait_ge(S["act"], prawT_act)
            t.wait_ge(S["dve"], rec)
            t.wait_ge(S["dve"], gm)
            for sl in range(NS):
                pf = praw_fm_all[:, 128 * sl:128 * (sl + 1)]
                t.matmul(pb[0].ap(), pf, woutT[sl][:, 0:512], start=(sl == 0), stop=(sl == NS - 1))
                mm2 = t.matmul(pb[1].ap(), pf, woutT[sl][:, 512:1024], start=(sl == 0),
                               stop=(sl == NS - 1))
            mm2.then_inc(S["pe"], 1)
        emit("tensor", f)
        op_done = bump("pe")

        ob = pi % 2
        ga = pend["silu_act"]
        od = last["out_dma"][ob]
        pta3 = last["prawT_act"]
        def f(v, pi=pi, ob=ob, op_done=op_done, ga=ga, od=od, pta3=pta3):
            v.wait_ge(S["pe"], op_done)
            v.wait_ge(S["act"], ga)
            v.wait_ge(S["sto" + str(ob)], od)
            v.wait_ge(S["act"], pta3)
            v.scalar_tensor_tensor(sg_sb[ob][:, 0:512], g_sb[pi][:, 0:512], 1.0,
                                   pb[0].ap(), ALU.mult, ALU.mult)
            v.scalar_tensor_tensor(sg_sb[ob][:, 512:1024], g_sb[pi][:, 512:1024], 1.0,
                                   pb[1].ap(), ALU.mult, ALU.mult).then_inc(S["dve"], 1)
        emit("vector", f)
        sg_done = bump("dve")
        last["gate_dve"][ob] = sg_done
        if pi != 3:
            last["x_free"][pi] = sg_done

        xres = x_sb[pi][:] if pi != 3 else qscr_i[:].bitcast(F32)
        def f(v2, ob=ob, xres=xres, sg_done=sg_done):
            v2.tensor_tensor(out_sb[ob][:], sg_sb[ob][:], xres, ALU.add).then_inc(S["dve"], 1)
        emit("vector", f)
        gate_done = bump("dve")
        if pi != 3:
            last["x_free"][pi] = gate_done
        def f(g, pi=pi, ob=ob, t0g=t0g, gate_done=gate_done):
            g.wait_ge(S["dve"], gate_done)
            g.dma_start(out=out[t0g + 128 * pi:t0g + 128 * (pi + 1), :],
                        in_=out_sb[ob][:]).then_inc(S["sto" + str(ob)], 16)
        emit("gpsimd", f)
        last["out_dma"][ob] = bump("sto" + str(ob), 16)

    CSER = False

    def emit_chunk_own(g_i, i, t0g, p2, qkT_done, pend):
        mask_dve = [0, 0]
        if CSER:
            dv_now, ac_now, pe_now2 = cnt["dve"], cnt["act"], cnt["pe"]
            def fser(t, a=dv_now, b=ac_now):
                t.wait_ge(S["dve"], a); t.wait_ge(S["act"], b)
            emit("tensor", fser)
            def fser2(v, a=pe_now2):
                v.wait_ge(S["pe"], a)
            emit("vector", fser2)
            def fser3(a_, a=pe_now2):
                a_.wait_ge(S["pe"], a)
            emit("scalar", fser3)
        sc_pe = [0, 0]
        nd_pe = [0, 0]

        pf23 = last["pb23_free_dve"]
        md_prev = last.get("mask_dve_last", 0)
        base = cnt["pe"]
        def f(t, i=i, qkT_done=qkT_done, pf23=pf23, md_prev=md_prev):
            t.wait_ge(S["dve"], qkT_done)
            t.wait_ge(S["dve"], pf23)
            t.wait_ge(S["dve"], md_prev)
            for p in range(4):
                kTe = kT_sb[p][0:64, 128 * i:128 * (i + 1)]
                kTo = kT_sb[p][64:128, 128 * i:128 * (i + 1)]
                qTe = qT_sb[p][0:64, 128 * i:128 * (i + 1)]
                qTo = qT_sb[p][64:128, 128 * i:128 * (i + 1)]
                c0 = 128 * p
                t.matmul(pb[2].ap()[:, c0:c0 + 128], kTe, qTe, start=True, stop=True)
                t.matmul(pb[3].ap()[:, c0:c0 + 128], kTo, qTo, start=True,
                         stop=True).then_inc(S["pe"], 1)
        emit("tensor", f)
        sc_pe[0] = bump("pe", 4)

        prev_nd = last.get("nd_pe_prev", 0)
        def f(v, sc=sc_pe[0], prev_nd=prev_nd):
            v.wait_ge(S["pe"], sc)
            v.wait_ge(S["pe"], prev_nd)
            v.tensor_tensor(attnT_sb[0][:], pb[2].ap(), mask_sb[:], ALU.mult)
            v.tensor_tensor(attnT_sb[1][:], pb[3].ap(), mask_sb[:], ALU.mult).then_inc(S["dve"], 1)
        emit("vector", f)
        mask_dve[0] = bump("dve")

        if pend is not None:
            emit_pend_state(pend)
            emit_pend_prawT(pend)

        def f(t, i=i, md0=mask_dve[0]):
            t.wait_ge(S["dve"], md0)
            for p in range(4, 8):
                kTe = kT_sb[p][0:64, 128 * i:128 * (i + 1)]
                kTo = kT_sb[p][64:128, 128 * i:128 * (i + 1)]
                qTe = qT_sb[p][0:64, 128 * i:128 * (i + 1)]
                qTo = qT_sb[p][64:128, 128 * i:128 * (i + 1)]
                c0 = 128 * (p % 4)
                t.matmul(pb[2].ap()[:, c0:c0 + 128], kTe, qTe, start=True, stop=True)
                t.matmul(pb[3].ap()[:, c0:c0 + 128], kTo, qTo, start=True,
                         stop=True).then_inc(S["pe"], 1)
        emit("tensor", f)
        sc_pe[1] = bump("pe", 4)

        def f(v, sc=sc_pe[1], prev_nd=prev_nd):
            v.wait_ge(S["pe"], sc)
            v.wait_ge(S["pe"], prev_nd)
            v.tensor_tensor(attnT_sb[2][:], pb[2].ap(), mask_sb[:], ALU.mult)
            v.tensor_tensor(attnT_sb[3][:], pb[3].ap(), mask_sb[:], ALU.mult).then_inc(S["dve"], 1)
        emit("vector", f)
        mask_dve[1] = bump("dve")
        last["mask_dve_last"] = mask_dve[1]

        for wv in range(2):
            Ssb = last["Ssb_dve"]
            kva = p2[3]["kv_dve"]
            sil = p2[3]["silu_act"]
            rec = last["recip_dve"]
            gm = max(last["gate_dve"])
            prw = last["praw_dve"]
            pta = last["prawT_act"]
            def f(t, i=i, wv=wv, md=mask_dve[wv], Ssb=Ssb, kva=kva, sil=sil, rec=rec, gm=gm,
                  prw=prw, pta=pta):
                t.wait_ge(S["dve"], md)
                t.wait_ge(S["dve"], Ssb)
                t.wait_ge(S["dve"], kva)
                if wv == 0:
                    t.wait_ge(S["act"], sil)
                    t.wait_ge(S["act"], pta)
                    t.wait_ge(S["dve"], prw)
                    t.wait_ge(S["dve"], rec)
                    t.wait_ge(S["dve"], gm)
                for p in range(4 * wv, 4 * wv + 4):
                    he, ho = 2 * p, 2 * p + 1
                    pc = p % 4
                    fw = (pc == 0 and wv == 0)
                    aTe = attnT_sb[2 * wv][:, 128 * pc:128 * pc + 128]
                    aTo = attnT_sb[2 * wv + 1][:, 128 * pc:128 * pc + 128]
                    qTe = qT_sb[p][0:64, 128 * i:128 * (i + 1)]
                    qTo = qT_sb[p][64:128, 128 * i:128 * (i + 1)]
                    t.matmul(pb[4].ap()[:, 64 * p:64 * p + 64], aTe,
                             vaug_sb[i][:, 65 * he:65 * he + 64], start=fw, stop=False)
                    t.matmul(pb[0].ap()[:, 2 * p:2 * p + 1], aTe, ones_col[:], start=fw, stop=False)
                    t.matmul(pb[5].ap()[:, 64 * p:64 * p + 64], aTo,
                             vaug_sb[i][:, 65 * ho:65 * ho + 64], start=fw, stop=False)
                    t.matmul(pb[1].ap()[:, 2 * p:2 * p + 1], aTo, ones_col[:], start=fw, stop=False)
                    t.matmul(pb[4].ap()[:, 64 * p:64 * p + 64], qTe,
                             S_sb[0:64, 66 * p:66 * p + 64], start=False, stop=True)
                    t.matmul(pb[0].ap()[:, 2 * p:2 * p + 1], qTe,
                             S_sb[0:64, 66 * p + 64:66 * p + 65], start=False, stop=True)
                    t.matmul(pb[5].ap()[:, 64 * p:64 * p + 64], qTo,
                             S_sb[64:128, 66 * p:66 * p + 64], start=False, stop=True)
                    t.matmul(pb[1].ap()[:, 2 * p:2 * p + 1], qTo,
                             S_sb[64:128, 66 * p + 64:66 * p + 65], start=False,
                             stop=True).then_inc(S["pe"], 1)
            emit("tensor", f)
            nd_pe[wv] = bump("pe", 4)
        last["nd_pe_prev"] = nd_pe[1]
        num_done = nd_pe[1]

        dmp = last.get("dump_sto", 0) if "praw_dump" in dbg_t else 0
        pta2 = last["prawT_act"]
        def f(v, num_done=num_done, dmp=dmp, pta2=pta2):
            v.wait_ge(S["pe"], num_done)
            v.wait_ge(S["sto"], dmp)
            v.wait_ge(S["act"], pta2)
            v.tensor_scalar_max(den_scr[:, 0:8], bass.AP(pb[0].ap().tensor, 0, [[512, 128], [2, 8]]), 1e-6)
            v.tensor_scalar_max(den_scr[:, 8:16], bass.AP(pb[1].ap().tensor, 0, [[512, 128], [2, 8]]), 1e-6)
            v.drain()
            v.reciprocal(rden_sb[:], den_scr[:])
            v.drain()
            v.engine_nop()
            v.drain().then_inc(S["dve"], 1)
        emit("vector", f)
        last["recip_dve"] = bump("dve")

        ptT = last.get("prawT_pe", 0)
        def f(v, ptT=ptT):
            v.wait_ge(S["pe"], ptT)
            rep_e = man_ap(rden_sb, 0, [[16, 128], [1, 8], [0, 64]])
            rep_o = man_ap(rden_sb, 8, [[16, 128], [1, 8], [0, 64]])
            dst_e = man_ap(praw_tok, 0, [[E, 128], [128, 8], [1, 64]])
            dst_o = man_ap(praw_tok, 64, [[E, 128], [128, 8], [1, 64]])
            v.tensor_tensor(dst_e, pb[4].ap().rearrange("p (a b) -> p a b", a=8), rep_e, ALU.mult)
            v.tensor_tensor(dst_o, pb[5].ap().rearrange("p (a b) -> p a b", a=8), rep_o,
                            ALU.mult).then_inc(S["dve"], 1)
        emit("vector", f)
        praw_done = bump("dve")
        last["praw_dve"] = praw_done

        if "praw_dump" in dbg_t:
            def f(g, i=i, t0g=t0g, praw_done=praw_done):
                g.wait_ge(S["dve"], praw_done)
                g.dma_start(out=dbg_t["praw_dump"][t0g + 128 * i:t0g + 128 * (i + 1), :],
                            in_=praw_tok[:]).then_inc(S["sto"], 16)
                g.dma_start(out=dbg_t["den_dump"][t0g + 128 * i:t0g + 128 * (i + 1), 0:16],
                            in_=den_scr[:]).then_inc(S["sto"], 16)
                g.dma_start(out=dbg_t["den_dump"][t0g + 128 * i:t0g + 128 * (i + 1), 16:32],
                            in_=rden_sb[:]).then_inc(S["sto"], 16)
                g.dma_start(out=dbg_t["S_dump"][t0g + 128 * i:t0g + 128 * (i + 1), :],
                            in_=S_sb[:]).then_inc(S["sto"], 16)
            emit("gpsimd", f)
            last["dump_sto"] = bump("sto", 64)

        if pend is not None:
            emit_pend_op(pend)

        return dict(i=i, t0g=t0g, kv=p2[i]["kv_dve"], silu_act=p2[i]["silu_act"],
                    praw_dve=praw_done)

    def emit_chunk_pre(i, p2):
        first = last["first_state"]
        kva = p2[i]["kv_dve"]
        def f(t, i=i, first=first, kva=kva):
            t.wait_ge(S["dve"], kva)
            for p in range(8):
                for par in range(2):
                    h = 2 * p + par
                    bank = pb[6] if p < 4 else pb[7]
                    r0 = 64 * par
                    c0 = 66 * (p % 4)
                    stf = first and (p in (0, 4))
                    t.matmul(bank.ap()[r0:r0 + 64, c0:c0 + 65],
                             ktok_sb[i][:, 64 * h:64 * h + 64],
                             vaug_sb[i][:, 65 * h:65 * h + 65],
                             start=stf, stop=False).then_inc(S["pe"], 1)
        emit("tensor", f)
        bump("pe", 16)
        last["first_state"] = False

    segs = (["pre"] if (prefix and dbg in ("full", "attn")) else []) + ["own"]
    for seg in segs:
        aug_done = set_aug(seg)
        xsrc = x_pre if seg == "pre" else x_own
        ngroups = NG if dbg in ("full", "attn") or seg == "pre" else (NG if dbg == "xn" else 1)
        if NG_OVR:
            ngroups = min(ngroups, NG_OVR)
        for g_i in range(ngroups):
            emit_group(seg, g_i, xsrc, aug_done)
            if seg == "pre" and "wt" not in SKIP:
                emit_wpart(2, pb01_dve=last["pb23_free_dve"])
                wT_act_done = cnt["act"]

    def f(sync, t0=cnt["sto0"], t1=cnt["sto1"], td=cnt["sto"]):
        sync.wait_ge(S["sto0"], t0)
        sync.wait_ge(S["sto1"], t1)
        if td:
            sync.wait_ge(S["sto"], td)
    emit("sync", f)

    if False:
        build_nc._mock = dict(prog=prog, sems=sems, ctx=ctx, cnt=dict(cnt))
        return nc, dbg_t
    with ctx, nc.Block() as block:
        @block.sync
        def _(e):
            for fn in prog["sync"]:
                fn(e)

        @block.gpsimd
        def _(e):
            for fn in prog["gpsimd"]:
                fn(e)

        @block.vector
        def _(e):
            for fn in prog["vector"]:
                fn(e)

        @block.scalar
        def _(e):
            for fn in prog["scalar"]:
                fn(e)

        @block.tensor
        def _(e):
            for fn in prog["tensor"]:
                fn(e)

    return nc, dbg_t



def kernel(L0_attn_x, L0_attn_W_qkv, L0_attn_W_out, L0_ffn_kernel, L0_ffn_bias,
           L0_ffn_a, L0_ffn_b):
    x = np.ascontiguousarray(np.asarray(L0_attn_x, dtype=np.float32))
    w_qkv = np.ascontiguousarray(np.asarray(L0_attn_W_qkv, dtype=np.float32))
    w_out = np.ascontiguousarray(np.asarray(L0_attn_W_out, dtype=np.float32))
    w_ffn = np.ascontiguousarray(np.asarray(L0_ffn_kernel, dtype=np.float32))
    b_ffn = np.ascontiguousarray(np.asarray(L0_ffn_bias, dtype=np.float32).reshape(1, E))
    a_lora = np.ascontiguousarray(np.asarray(L0_ffn_a, dtype=np.float32))
    b_lora = np.ascontiguousarray(np.asarray(L0_ffn_b, dtype=np.float32))

    mask = (np.triu(np.ones((128, 128), np.float32)) * 0.125)
    mask = np.ascontiguousarray(np.tile(mask, (1, 4)).astype(np.float32))
    ident = np.ascontiguousarray(np.eye(128).astype(ml_dtypes.bfloat16))
    zeros_pre = np.zeros((T, E), np.float32)

    nc, _ = build_nc(dbg="full", prefix=True)
    in_maps = []
    for c in range(8):
        b, h = c // 2, c % 2
        x_own = np.ascontiguousarray(x[b, 4096 * h:4096 * (h + 1), :])
        x_pre = np.ascontiguousarray(x[b, 0:4096, :]) if h == 1 else zeros_pre
        augc = np.full((128, 1), 1.0 if h == 1 else 0.0, np.float32)
        in_maps.append({
            "x_own": x_own, "x_pre": x_pre,
            "w_qkv": w_qkv, "w_out": w_out, "w_ffn": w_ffn,
            "b_ffn": b_ffn, "a_lora": a_lora, "b_lora": b_lora,
            "mask_in": mask, "ident_in": ident, "augc_in": augc,
        })
    res = run_bass_kernel_spmd(nc, in_maps, core_ids=list(range(8)))
    out = np.empty((B, L, E), np.float32)
    for c in range(8):
        b, h = c // 2, c % 2
        out[b, 4096 * h:4096 * (h + 1), :] = res.results[c]["out"]
    return out
